# version 02
# baseline (speedup 1.0000x reference)
import numpy as np
import ml_dtypes

import concourse.bass as bass
import concourse.mybir as mybir
from concourse.bass_utils import run_bass_kernel_spmd

F32 = mybir.dt.float32
BF16 = mybir.dt.bfloat16
AF = mybir.ActivationFunctionType
ALU = mybir.AluOpType
AX = mybir.AxisListType
BF = ml_dtypes.bfloat16

DIM = 2048
SEQ = 8192
NH = 4
HD = 512
DH = 512
NAH = 16
AHD = 128
CHUNK = 1024
NCHUNK = SEQ // CHUNK
WIN = 1024
FFN_H = 8192
BASE_LR = 0.01
BASE_LR_INV = float(BASE_LR + np.log(-np.expm1(-BASE_LR)))
NS_COEFFS = [(4.0848, -6.8946, 2.927), (3.9505, -6.3029, 2.6377),
             (3.7418, -5.5913, 2.3037), (2.8769, -3.1427, 1.2046),
             (2.8366, -3.0525, 1.2012)]
NEG_BIG = -1e30


class P:

    def __init__(self, name):
        self.nc = bass.Bass("TRN2", target_bir_lowering=False)
        self.name = name
        self._sem_ct = {}
        self._ctx = []

    def dram(self, name, shape, dtype, kind="Internal"):
        return self.nc.dram_tensor(name, list(shape), dtype, kind=kind).ap()

    def sb(self, name, shape, dtype):
        t = self.nc.sbuf_tensor(name, list(shape), dtype)
        h = t.__enter__()
        self._ctx.append(t)
        return h

    def ps(self, name, shape, dtype=F32):
        t = self.nc.psum_tensor(name, list(shape), dtype)
        h = t.__enter__()
        self._ctx.append(t)
        return h

    def sem(self, name):
        t = self.nc.semaphore(name)
        h = t.__enter__()
        self._ctx.append(t)
        self._sem_ct[h] = 0
        return h

    def close(self):
        for t in reversed(self._ctx):
            t.__exit__(None, None, None)
        self._ctx = []

    def inc(self, instr, sem, n=1):
        instr.then_inc(sem, n)
        self._sem_ct[sem] += n
        return (sem, self._sem_ct[sem])

    def wait(self, eng, tok):
        if tok is None:
            return
        if isinstance(tok, list):
            for t in tok:
                self.wait(eng, t)
            return
        sem, val = tok
        if val > 0:
            eng.wait_ge(sem, val)

    def val(self, sem):
        return self._sem_ct[sem]

    def fence(self, eng, instr, sem):
        tok = self.inc(instr, sem)
        self.wait(eng, tok)
        return tok


def cdiv(a, b):
    return (a + b - 1) // b



AUGW = 6144 + 128
N_OC = AUGW // 128
TBLK = 512


def build_qkv(ttok):
    p = P("qkv")
    nc = p.nc
    xt_d = p.dram("xt", [DIM, ttok], F32, kind="ExternalInput")
    waug_d = p.dram("waug", [DIM, AUGW], F32, kind="ExternalInput")
    qkp_d = p.dram("qkp", [128, 64], F32, kind="ExternalInput")
    vbias_d = p.dram("vbias", [128, 1], F32, kind="ExternalInput")
    qkv_o = p.dram("qkv", [3 * DIM, ttok], BF16, kind="ExternalOutput")
    aqk_o = p.dram("aqk", [2 * DIM, ttok], BF16, kind="ExternalOutput")
    av_o = p.dram("av", [DIM, ttok], BF16, kind="ExternalOutput")
    aux_o = p.dram("aux", [128, ttok], F32, kind="ExternalOutput")

    nblk = ttok // TBLK
    SEGS = [(0, 16), (16, 32), (32, N_OC)]
    MAXSEG = max(e - s for s, e in SEGS)

    xb = [p.sb(f"xb{i}", [128, 16 * TBLK], BF16) for i in range(nblk)]
    xsq = p.sb("xsq", [128, 16 * TBLK], BF16)
    wb = [p.sb(f"wb{i}", [128, 16 * 128 * MAXSEG], BF16) for i in range(1)]
    onesb = p.sb("onesb", [128, 1], BF16)
    onesrow_f = p.sb("onesrow_f", [1, 128], F32)
    cstt = p.sb("cstt", [128, 3], F32)
    qkp = p.sb("qkp_sb", [128, 64], F32)
    vbias = p.sb("vbias_sb", [128, 1], F32)
    colsumT = p.sb("colsumT", [1, 128 * MAXSEG], F32)
    stat = p.sb("stat", [1, 4 * TBLK], F32)
    negm = p.sb("negm", [1, nblk * TBLK], F32)
    rstdb = [p.sb(f"rstdb{i}", [128, TBLK], F32) for i in range(nblk)]
    qf = [p.sb(f"qf{i}", [128, TBLK], F32) for i in range(2)]
    ob = [p.sb(f"ob{i}", [128, TBLK], BF16) for i in range(4)]
    auxb = p.sb("auxb", [128, TBLK], F32)

    mm_ps = [p.ps(f"mm{i}", [128, TBLK], F32) for i in range(4)]
    st_ps = p.ps("st", [1, 2 * TBLK], F32)
    cs_ps = p.ps("cs", [1, 512], F32)
    bc_ps = p.ps("bc", [128, TBLK], F32)

    s_w = p.sem("s_w")
    s_prm = p.sem("s_prm")
    s_x = [p.sem(f"s_x{i}") for i in range(nblk)]
    s_pe = p.sem("s_pe")
    s_ev = p.sem("s_ev")
    s_ac = p.sem("s_ac")
    s_ob = [p.sem(f"s_ob{i}") for i in range(4)]
    s_aux = p.sem("s_aux")
    s_qo = p.sem("s_qo")

    nc.vector.memset(onesb[:], 1.0)
    nc.vector.memset(onesrow_f[:], 1.0)
    nc.vector.memset(cstt[:, 0:1], 1e-5)
    nc.vector.memset(cstt[:, 1:2], -0.5)
    nc.vector.memset(cstt[:, 2:3], BASE_LR_INV)
    consts_rdy = p.inc(nc.vector.memset(auxb[:], 0.0), s_ev)

    prm = p.inc(nc.gpsimd.dma_start(out=qkp[:], in_=qkp_d), s_prm, 16)
    prm = p.inc(nc.gpsimd.dma_start(out=vbias[:], in_=vbias_d), s_prm, 16)
    ld = [None] * nblk
    for b in range(nblk):
        for d in range(16):
            ld[b] = p.inc(nc.gpsimd.dma_start(
                out=xb[b][:, d * TBLK:(d + 1) * TBLK],
                in_=xt_d[d * 128:(d + 1) * 128, b * TBLK:(b + 1) * TBLK]), s_x[b], 16)

    NM, MSQ, VAR, RSTD = 0, 1, 2, 3

    def srow(i):
        return stat[0:1, i * TBLK:(i + 1) * TBLK]

    rstd_rdy = [None] * nblk
    stps_free = None
    bcps_free = None
    for b in range(nblk):
        p.wait(nc.vector, ld[b])
        sq = None
        for d in range(16):
            sq = nc.vector.tensor_mul(xsq[:, d * TBLK:(d + 1) * TBLK],
                                      xb[b][:, d * TBLK:(d + 1) * TBLK],
                                      xb[b][:, d * TBLK:(d + 1) * TBLK])
        sq = p.inc(sq, s_ev)
        p.wait(nc.tensor, sq)
        p.wait(nc.tensor, stps_free)
        for d in range(16):
            nc.tensor.matmul(st_ps[:, 0:TBLK], onesb[:, 0:1],
                             xb[b][:, d * TBLK:(d + 1) * TBLK],
                             start=(d == 0), stop=(d == 15))
        last = None
        for d in range(16):
            last = nc.tensor.matmul(st_ps[:, TBLK:2 * TBLK], onesb[:, 0:1],
                                    xsq[:, d * TBLK:(d + 1) * TBLK],
                                    start=(d == 0), stop=(d == 15))
        t_st = p.inc(last, s_pe)

        p.wait(nc.scalar, t_st)
        nc.scalar.mul(negm[0:1, b * TBLK:(b + 1) * TBLK], st_ps[:, 0:TBLK],
                      -1.0 / DIM)
        a1 = p.inc(nc.scalar.mul(srow(MSQ), st_ps[:, TBLK:2 * TBLK], 1.0 / DIM),
                   s_ac)
        stps_free = a1
        p.wait(nc.vector, a1)
        p.wait(nc.vector, consts_rdy)
        p.fence(nc.vector, nc.vector.tensor_mul(
            srow(VAR), negm[0:1, b * TBLK:(b + 1) * TBLK],
            negm[0:1, b * TBLK:(b + 1) * TBLK]), s_ev)
        v1 = p.inc(nc.vector.tensor_sub(srow(VAR), srow(MSQ), srow(VAR)), s_ev)
        p.wait(nc.scalar, v1)
        p.fence(nc.scalar, nc.scalar.activation(srow(RSTD), srow(VAR), AF.Ln,
                                                bias=cstt[0:1, 0:1]), s_ac)
        a2 = p.inc(nc.scalar.activation(srow(RSTD), srow(RSTD), AF.Exp,
                                        scale=cstt[0:1, 1:2]), s_ac)
        p.wait(nc.tensor, a2)
        p.wait(nc.tensor, bcps_free)
        t_bc = p.inc(nc.tensor.matmul(bc_ps[:, 0:TBLK], onesrow_f[:], srow(RSTD),
                                      start=True, stop=True), s_pe)
        p.wait(nc.scalar, t_bc)
        rstd_rdy[b] = p.inc(nc.scalar.copy(rstdb[b][:], bc_ps[:, 0:TBLK]), s_ac)
        bcps_free = rstd_rdy[b]

    p.wait(nc.vector, prm)
    p.wait(nc.scalar, prm)

    bank_free = [None] * 4
    ob_free = [None] * 4
    qf_free = [None, None]
    aux_free = None
    wseg_free = None
    cs_free = None
    for (js, je) in SEGS:
        segw = (je - js) * 128
        p.wait(nc.gpsimd, wseg_free)
        w_rdy = None
        for d in range(16):
            w_rdy = p.inc(nc.gpsimd.dma_start(
                out=wb[0][:, d * segw:(d + 1) * segw],
                in_=waug_d[d * 128:(d + 1) * 128, js * 128:je * 128]), s_w, 16)

        p.wait(nc.tensor, w_rdy)
        p.wait(nc.tensor, cs_free)
        cs_rdy = None
        for seg in range(cdiv(segw, 512)):
            n0, n1 = seg * 512, min(segw, (seg + 1) * 512)
            p.wait(nc.tensor, cs_rdy)
            last = None
            for d in range(16):
                last = nc.tensor.matmul(cs_ps[:, 0:n1 - n0], onesb[:, 0:1],
                                        wb[0][:, d * segw + n0:d * segw + n1],
                                        start=(d == 0), stop=(d == 15))
            t = p.inc(last, s_pe)
            p.wait(nc.scalar, t)
            cs_rdy = p.inc(nc.scalar.copy(colsumT[0:1, n0:n1],
                                          cs_ps[:, 0:n1 - n0]), s_ac)

        last_mm = None
        for b in range(nblk):
            p.wait(nc.tensor, ld[b])
            p.wait(nc.tensor, cs_rdy)
            for j in range(js, je):
                jj = j - js
                bank = mm_ps[jj % 4]
                p.wait(nc.tensor, bank_free[jj % 4])
                for d in range(16):
                    nc.tensor.matmul(bank[:],
                                     wb[0][:, d * segw + jj * 128:d * segw + (jj + 1) * 128],
                                     xb[b][:, d * TBLK:(d + 1) * TBLK],
                                     start=(d == 0), stop=False)
                t_mm = p.inc(nc.tensor.matmul(
                    bank[:], colsumT[0:1, jj * 128:(jj + 1) * 128],
                    negm[0:1, b * TBLK:(b + 1) * TBLK], start=False, stop=True), s_pe)
                last_mm = t_mm

                f = qf[jj % 2]
                p.wait(nc.vector, t_mm)
                p.wait(nc.vector, rstd_rdy[b])
                p.wait(nc.vector, qf_free[jj % 2])
                v_ev = p.inc(nc.vector.tensor_mul(f[:], bank[:], rstdb[b][:]), s_ev)
                bank_free[jj % 4] = v_ev

                if j < 48:
                    o0i, o1i = (2 * jj) % 4, (2 * jj + 1) % 4
                    o0, o1 = ob[o0i], ob[o1i]
                    p.wait(nc.scalar, v_ev)
                    p.wait(nc.scalar, ob_free[o0i])
                    a_ev = p.inc(nc.scalar.activation(o0[:], f[:], AF.Silu), s_ac)
                    qf_free[jj % 2] = a_ev
                    nc.sync.wait_ge(*a_ev)
                    ob_free[o0i] = p.inc(nc.sync.dma_start(
                        out=qkv_o[j * 128:(j + 1) * 128, b * TBLK:(b + 1) * TBLK],
                        in_=o0[:]), s_ob[o0i], 16)
                    if j < 32:
                        qs = qkp[:, (2 * j):(2 * j) + 1]
                        qo = qkp[:, (2 * j) + 1:(2 * j) + 2]
                        p.wait(nc.vector, ob_free[o1i])
                        p.wait(nc.vector, v_ev)
                        e2 = p.inc(nc.vector.tensor_scalar(o1[:], f[:], qs, qo,
                                                           ALU.mult, ALU.add), s_ev)
                        nc.sync.wait_ge(*e2)
                        ob_free[o1i] = p.inc(nc.sync.dma_start(
                            out=aqk_o[j * 128:(j + 1) * 128,
                                      b * TBLK:(b + 1) * TBLK],
                            in_=o1[:]), s_ob[o1i], 16)
                    else:
                        p.wait(nc.vector, ob_free[o1i])
                        p.wait(nc.vector, v_ev)
                        e2 = p.inc(nc.vector.tensor_copy(o1[:], f[:]), s_ev)
                        nc.sync.wait_ge(*e2)
                        ob_free[o1i] = p.inc(nc.sync.dma_start(
                            out=av_o[(j - 32) * 128:(j - 31) * 128,
                                     b * TBLK:(b + 1) * TBLK],
                            in_=o1[:]), s_ob[o1i], 16)
                else:
                    p.wait(nc.scalar, v_ev)
                    p.wait(nc.scalar, aux_free)
                    p.wait(nc.scalar, consts_rdy)
                    p.fence(nc.scalar, nc.scalar.activation(
                        auxb[0:12, :], f[0:12, :], AF.Exp,
                        bias=cstt[0:12, 2:3]), s_ac)
                    nc.scalar.activation(auxb[0:12, :], auxb[0:12, :], AF.Ln,
                                         bias=1.0)
                    nc.scalar.activation(auxb[32:36, :], f[32:36, :], AF.Sigmoid,
                                         bias=vbias[32:36, 0:1])
                    a_ev = p.inc(nc.scalar.activation(
                        auxb[64:68, :], f[64:68, :], AF.Silu,
                        bias=vbias[64:68, 0:1]), s_ac)
                    qf_free[jj % 2] = a_ev
                    nc.sync.wait_ge(*a_ev)
                    aux_free = p.inc(nc.sync.dma_start(
                        out=aux_o[:, b * TBLK:(b + 1) * TBLK], in_=auxb[:]),
                        s_aux, 16)
        wseg_free = last_mm
        cs_free = last_mm

    for s in [s_qo, s_aux] + s_ob:
        if p.val(s):
            nc.sync.wait_ge(s, p.val(s))
    p.close()
    return p.nc



def _waug(Wqkv, Wlr, Wm, Ws):
    aug = np.zeros((DIM, 128), np.float32)
    aug[:, 0:12] = Wlr
    aug[:, 32:36] = Wm
    aug[:, 64:68] = Ws
    return np.ascontiguousarray(np.concatenate([Wqkv, aug], axis=1)).astype(np.float32)


def _qkp_pack(qk_scale, qk_offset):
    out = np.zeros((128, 64), np.float32)
    for c in range(16):
        out[:, 2 * c] = qk_scale[c * 128:(c + 1) * 128, 0]
        out[:, 2 * c + 1] = qk_offset[c * 128:(c + 1) * 128, 0]
    for c in range(16):
        out[:, 32 + 2 * c] = qk_scale[c * 128:(c + 1) * 128, 1]
        out[:, 32 + 2 * c + 1] = qk_offset[c * 128:(c + 1) * 128, 1]
    return out


def _vbias_pack(bm, bs):
    v = np.zeros((128, 1), np.float32)
    v[32:36, 0] = bm
    v[64:68, 0] = bs
    return v


def run_qkv_stage(inp):
    XT = np.ascontiguousarray(inp["L0_attn_x"][0].T).astype(np.float32)
    WAUG = _waug(inp["L0_attn_Wqkv"], inp["L0_attn_Wlr"],
                 inp["L0_attn_Wm"], inp["L0_attn_Ws"])
    QKP = _qkp_pack(inp["L0_attn_qk_scale"], inp["L0_attn_qk_offset"])
    VB = _vbias_pack(inp["L0_attn_bm"], inp["L0_attn_bs"])
    ttok = SEQ // 8
    nc = build_qkv(ttok)
    in_maps = [{
        "xt": np.ascontiguousarray(XT[:, c * ttok:(c + 1) * ttok]),
        "waug": WAUG, "qkp": QKP, "vbias": VB,
    } for c in range(8)]
    res = run_bass_kernel_spmd(nc, in_maps, core_ids=list(range(8)))
    return {k: np.concatenate([res.results[c][k] for c in range(8)], axis=1)
            for k in ["qkv", "aqk", "av", "aux"]}


def kernel(**inputs):
    inputs = {k: np.asarray(v) for k, v in inputs.items()}
    st1 = run_qkv_stage(inputs)
    st2 = run_chain_stage(st1, inputs)
    st3 = run_attn_stage(st1, st2)
    XT = np.ascontiguousarray(inputs["L0_attn_x"][0].T).astype(np.float32)
    outT = run_tail_stage(st3, inputs, XT)
    return np.ascontiguousarray(outT.T).astype(np.float32)[None]



def build_chain(nchunks=NCHUNK, phases=99):
    import os
    phases = int(os.environ.get("CHAIN_PHASES", phases))
    p = P("chain")
    nc = p.nc
    qh_d = p.dram("qh", [HD, SEQ], BF16, kind="ExternalInput")
    kh_d = p.dram("kh", [HD, SEQ], BF16, kind="ExternalInput")
    vh_d = p.dram("vh", [HD, SEQ], BF16, kind="ExternalInput")
    lrtm_d = p.dram("lrtm", [128, NCHUNK * 32], F32, kind="ExternalInput")
    mom_d = p.dram("momh", [1, SEQ], F32, kind="ExternalInput")
    scl_d = p.dram("sclh", [1, SEQ], F32, kind="ExternalInput")
    w0_d = p.dram("w0", [DH, HD], F32, kind="ExternalInput")
    w1_d = p.dram("w1", [HD, DH], F32, kind="ExternalInput")
    w2_d = p.dram("w2", [DH, HD], F32, kind="ExternalInput")
    gcol_d = p.dram("gcolh", [128, 4], F32, kind="ExternalInput")
    idt_d = p.dram("idt", [128, 128], BF16, kind="ExternalInput")
    diag_d = p.dram("diag", [128, 5 * 128], BF16, kind="ExternalInput")
    ttt_o = p.dram("ttt", [HD, SEQ], BF16, kind="ExternalOutput")

    C = CHUNK
    NT = C // 128

    qn = p.sb("qn", [128, 4 * C], BF16)
    kn = p.sb("kn", [128, 4 * C], BF16)
    vv = p.sb("vv", [128, 4 * C], BF16)
    kTM = p.sb("kTM", [128, NT * 512], BF16)
    vTM = p.sb("vTM", [128, NT * 512], BF16)
    momr = p.sb("momr", [1, C], BF16)
    sclr = p.sb("sclr", [1, C], F32)
    lrcf = [p.sb(f"lrcf{i}", [128, NT * 4], F32) for i in range(2)]
    mmc = p.sb("mmc", [128, 1], F32)
    rnc = p.sb("rnc", [128, 1], F32)
    W = [p.sb(f"W{m}", [128, 4 * 512], F32) for m in range(3)]
    Mb = [p.sb(f"Mb{m}", [128, 4 * 512], BF16) for m in range(3)]
    Wb = [p.sb(f"Wb{m}", [128, 4 * 512], BF16) for m in range(3)]
    WTb = [p.sb(f"WTb{m}", [128, 4 * 512], BF16) for m in range(3)]
    ncol = p.sb("ncol", [128, 12], F32)
    rcol = p.sb("rcol", [128, 8], F32)
    dwf = p.sb("dwf", [128, 4 * 512], F32)
    X5f = p.sb("X5f", [128, 4 * 512], F32)
    Xb = [p.sb(f"Xb{i}", [128, 4 * 512], BF16) for i in range(2)]
    XTb = [p.sb(f"XTb{i}", [128, 4 * 512], BF16) for i in range(2)]
    Ab = p.sb("Ab", [128, 4 * 512], BF16)
    Acb = p.sb("Acb", [128, 4 * 512], BF16)
    hidT = p.sb("hidT", [128, 4 * C], BF16)
    gateT = p.sb("gateT", [128, 4 * C], BF16)
    outb = p.sb("outb", [128, 4 * C], BF16)
    onesrow_b = p.sb("onesrow_b", [1, 128], BF16)
    dgb = gateT
    dhb = hidT
    hl1 = outb
    klb = [p.sb(f"klb{i}", [128, 512], BF16) for i in range(2)]
    gsc = [p.sb(f"gsc{i}", [128, 1024], F32) for i in range(5)]
    rowA = p.sb("rowA", [1, C], F32)
    rowB = p.sb("rowB", [1, C], F32)
    rqb = p.sb("rqb", [128, C], BF16)
    rsgb = p.sb("rsgb", [128, C], BF16)
    gcol = p.sb("gcol", [128, 4], F32)
    onesb = p.sb("onesb", [128, 1], BF16)
    onesf = p.sb("onesf", [128, 1], F32)
    onesrow_f = p.sb("onesrow_f", [1, 128], F32)
    idt = p.sb("idt_sb", [128, 128], BF16)
    diag = p.sb("diag_sb", [128, 5 * 128], BF16)
    cst = p.sb("cst", [128, 4], F32)
    srow = p.sb("srow_sb", [1, 8], F32)

    PSA = p.ps("PSA", [128, 2048], F32)
    PS = [PSA[:, j * 512:(j + 1) * 512] for j in range(4)]
    PS4 = p.ps("PS4", [128, 512], F32)
    PS5 = p.ps("PS5", [128, 512], F32)
    PT = [p.ps(f"PTb{i}", [128, 1024], BF16) for i in range(2)]

    s_in = [p.sem(f"s_in{i}") for i in range(2)]
    s_w = p.sem("s_wl")
    s_pe = p.sem("s_pel")
    s_ev = p.sem("s_evl")
    s_ac = p.sem("s_acl")
    s_out = p.sem("s_outl")

    V, SC, T = nc.vector, nc.scalar, nc.tensor

    ps_free = [None] * 4
    ps4_free = None
    pt_free = [[None] * 8, [None] * 8]
    pt_ctr = [0]

    def bcast_col(row_ap, out_col):
        nonlocal ps4_free
        p.wait(nc.tensor, ps4_free)
        t = p.inc(T.matmul(PS4[:, 0:1], onesrow_f[:], row_ap, start=True, stop=True),
                  s_pe)
        p.wait(nc.scalar, t)
        tok = p.inc(SC.copy(out_col, PS4[:, 0:1]), s_ac)
        ps4_free = tok
        return tok

    def bcast_row(row_f32, out_tile, n, rdy=None):
        nonlocal ps_free
        toks = []
        p.wait(nc.tensor, rdy)
        for h in range(cdiv(n, 512)):
            n0, n1 = h * 512, min(n, (h + 1) * 512)
            p.wait(nc.tensor, ps_free[h])
            t = p.inc(T.matmul(PS[h][0:128, 0:n1 - n0], onesrow_f[:],
                               row_f32[0:1, n0:n1], start=True, stop=True), s_pe)
            p.wait(nc.vector, t)
            tok = p.inc(V.tensor_copy(out_tile[:, n0:n1], PS[h][0:128, 0:n1 - n0]), s_ev)
            ps_free[h] = tok
            toks.append(tok)
        return toks[-1]

    pt_bank_free = [None, None]

    def transpose_batch(pairs):
        tok = None
        bank = pt_ctr[0] % 2
        i = 0
        while i < len(pairs):
            grp = pairs[i:i + 8]
            i += 8
            pt_ctr[0] += 1
            bank = 1 - bank
            p.wait(nc.tensor, pt_bank_free[bank])
            tt = None
            for r, (s_ap, d_ap, npart) in enumerate(grp):
                tt = p.inc(T.transpose(
                    PT[bank][:, r * 128:r * 128 + s_ap.shape[0]], s_ap,
                    idt[0:npart, 0:npart] if npart < 128 else idt[:]), s_pe)
            p.wait(nc.vector, tt)
            for r, (s_ap, d_ap, npart) in enumerate(grp):
                tok = p.inc(V.tensor_copy(
                    d_ap, PT[bank][:, r * 128:r * 128 + s_ap.shape[0]]), s_ev)
            pt_bank_free[bank] = tok
        return tok

    def transpose512(src_bf, dst_bf, rdy_tok):
        p.wait(nc.tensor, rdy_tok)
        pairs = []
        for j in range(4):
            for i in range(4):
                pairs.append((src_bf[:, j * 512 + i * 128:j * 512 + (i + 1) * 128],
                              dst_bf[:, i * 512 + j * 128:i * 512 + (j + 1) * 128],
                              128))
        return transpose_batch(pairs)

    nc.vector.memset(cst[:, 0:1], 1e-5)
    nc.vector.memset(cst[:, 1:2], -0.5)
    nc.vector.memset(cst[:, 2:3], 1e-7)
    nc.vector.memset(onesb[:], 1.0)
    nc.vector.memset(onesf[:], 1.0)
    nc.vector.memset(onesrow_f[:], 1.0)
    nc.vector.memset(onesrow_b[:], 1.0)
    for m in range(3):
        nc.vector.memset(Mb[m][:], 0.0)
    ini = p.inc(nc.vector.memset(outb[:], 0.0), s_ev)

    wl = None
    for m, wd in enumerate([w0_d, w1_d, w2_d]):
        for j in range(4):
            wl = p.inc(nc.gpsimd.dma_start(
                out=W[m][:, j * 512:(j + 1) * 512],
                in_=wd[j * 128:(j + 1) * 128, :]), s_w, 16)
    wl = p.inc(nc.gpsimd.dma_start(out=idt[:], in_=idt_d), s_w, 16)
    wl = p.inc(nc.gpsimd.dma_start(out=diag[:], in_=diag_d), s_w, 16)
    wl = p.inc(nc.gpsimd.dma_start(out=gcol[:], in_=gcol_d), s_w, 16)

    p.wait(nc.vector, wl)
    p.wait(nc.scalar, wl)
    cast0 = None
    for m in range(3):
        cast0 = p.inc(V.tensor_copy(Wb[m][:], W[m][:]), s_ev)
    nrm = None
    for m in range(3):
        for j in range(4):
            nrm = p.fence(nc.scalar, SC.activation(
                gsc[0][:, 0:512], W[m][:, j * 512:(j + 1) * 512], AF.Square,
                accum_out=ncol[:, 4 * m + j:4 * m + j + 1]), s_ac)
    nrm = p.inc(SC.sqrt(ncol[:, 0:12], ncol[:, 0:12]), s_ac)

    gc = wl

    for m in range(3):
        wt0 = transpose512(Wb[m], WTb[m], cast0)
    prev_wt = wt0

    prev_grad_done = None
    prev_chunk_end = None
    ldq_next = None
    for c in range(nchunks):
        par = c % 2
        sin = s_in[par]
        if c == 0:
            p.wait(nc.gpsimd, None)
            ldq = None
            for oc in range(4):
                for (dst, srcd) in ((qn, qh_d), (kn, kh_d), (vv, vh_d)):
                    ldq = p.inc(nc.gpsimd.dma_start(
                        out=dst[:, oc * C:(oc + 1) * C],
                        in_=srcd[oc * 128:(oc + 1) * 128, 0:C]), sin, 16)
            ldq = p.inc(nc.gpsimd.dma_start(out=lrcf[0][:], in_=lrtm_d[:, 0:32]), sin, 16)
            ldq = p.inc(nc.gpsimd.dma_start(out=momr[:], in_=mom_d[:, 0:C]), sin, 16)
            ldq = p.inc(nc.gpsimd.dma_start(out=sclr[:], in_=scl_d[:, 0:C]), sin, 16)
        else:
            ldq = ldq_next

        for tgt in ((qn, kn) if phases >= 1 else ()):
            p.wait(nc.vector, ldq)
            sq = None
            for oc in range(4):
                sq = V.tensor_mul(hidT[:, oc * C:(oc + 1) * C],
                                  tgt[:, oc * C:(oc + 1) * C],
                                  tgt[:, oc * C:(oc + 1) * C])
            sq = p.inc(sq, s_ev)
            p.wait(nc.tensor, sq)
            p.wait(nc.tensor, ini)
            last = None
            for h in range(2):
                p.wait(nc.tensor, ps_free[h])
                for oc in range(4):
                    last = T.matmul(PS[h][0:1, :], onesb[:, 0:1],
                                    hidT[:, oc * C + h * 512:oc * C + (h + 1) * 512],
                                    start=(oc == 0), stop=(oc == 3))
            tsq = p.inc(last, s_pe)
            p.wait(nc.scalar, tsq)
            a = None
            for h in range(2):
                a = p.fence(nc.scalar, SC.sqrt(rowA[0:1, h * 512:(h + 1) * 512],
                                               PS[h][0:1, :]), s_ac)
                ps_free[h] = a
            p.wait(nc.vector, a)
            rb = p.fence(nc.vector, V.tensor_scalar_add(rowA[0:1, :], rowA[0:1, :],
                                                        1e-5), s_ev)
            rb = p.fence(nc.vector, V.reciprocal(rowB[0:1, :], rowA[0:1, :]), s_ev)
            bt_ = bcast_row(rowB, rqb, C, rb)
            p.wait(nc.vector, bt_)
            rq = None
            for oc in range(4):
                rq = V.tensor_mul(tgt[:, oc * C:(oc + 1) * C],
                                  tgt[:, oc * C:(oc + 1) * C], rqb[:])
            nrm_done = p.inc(rq, s_ev)

        tok = None
        for (src, dst) in (((kn, kTM), (vv, vTM)) if phases >= 2 else ()):
            p.wait(nc.tensor, nrm_done if src is kn else ldq)
            prs = []
            for tc in range(NT):
                for oc in range(4):
                    prs.append((src[:, oc * C + tc * 128:oc * C + (tc + 1) * 128],
                                dst[:, tc * 512 + oc * 128:tc * 512 + (oc + 1) * 128],
                                128))
            tok = transpose_batch(prs)
        ktm_done = tok

        lrc = ldq
        lrcf_t = ldq
        lrcfc = lrcf[par]
        momrc = momr
        sclrc = sclr
        if phases < 3:
            for oc in range(4):
                nc.sync.wait_ge(*(ldq if phases >= 1 else ldq))
                od = p.inc(nc.sync.dma_start(
                    out=ttt_o[oc * 128:(oc + 1) * 128, c * C:(c + 1) * C],
                    in_=qn[:, oc * C:(oc + 1) * C]), s_out, 16)
            continue
        p.wait(nc.vector, ldq)
        mm1 = p.fence(nc.vector, V.reduce_sum(rowA[0:1, 0:1], momrc[:], axis=AX.X), s_ev)
        p.wait(nc.scalar, mm1)
        mm2 = p.inc(SC.mul(srow[0:1, 0:1], rowA[0:1, 0:1], 1.0 / C), s_ac)
        p.wait(nc.tensor, mm2)
        mmr = bcast_col(srow[0:1, 0:1], mmc[:, 0:1])

        if phases < 4:
            for oc in range(4):
                nc.sync.wait_ge(*lrc)
                od = p.inc(nc.sync.dma_start(
                    out=ttt_o[oc * 128:(oc + 1) * 128, c * C:(c + 1) * C],
                    in_=qn[:, oc * C:(oc + 1) * C]), s_out, 16)
            continue
        p.wait(nc.tensor, prev_wt)
        hv = gv = None
        for oc in range(4):
            for half in range(2):
                bank = (oc * 2 + half) % 4
                p.wait(nc.tensor, ps_free[bank])
                last = None
                for dch in range(4):
                    last = T.matmul(PS[bank][:],
                                    WTb[2][:, dch * 512 + oc * 128:dch * 512 + (oc + 1) * 128],
                                    qn[:, dch * C + half * 512:dch * C + (half + 1) * 512],
                                    start=(dch == 0), stop=(dch == 3))
                th = p.inc(last, s_pe)
                p.wait(nc.vector, th)
                hv = p.inc(V.tensor_copy(
                    hidT[:, oc * C + half * 512:oc * C + (half + 1) * 512],
                    PS[bank][:]), s_ev)
                p.wait(nc.tensor, hv)
                last = None
                for dch in range(4):
                    last = T.matmul(PS[bank][:],
                                    WTb[0][:, dch * 512 + oc * 128:dch * 512 + (oc + 1) * 128],
                                    qn[:, dch * C + half * 512:dch * C + (half + 1) * 512],
                                    start=(dch == 0), stop=(dch == 3))
                tg2 = p.inc(last, s_pe)
                p.wait(nc.scalar, tg2)
                gv = p.inc(SC.activation(
                    gateT[:, oc * C + half * 512:oc * C + (half + 1) * 512],
                    PS[bank][:], AF.Silu), s_ac)
                ps_free[bank] = gv
        p.wait(nc.vector, gv)
        p.wait(nc.vector, hv)
        gh = None
        for oc in range(4):
            gh = V.tensor_mul(gateT[:, oc * C:(oc + 1) * C],
                              gateT[:, oc * C:(oc + 1) * C],
                              hidT[:, oc * C:(oc + 1) * C])
        gh = p.inc(gh, s_ev)
        p.wait(nc.tensor, gh)
        ov = None
        for oc in range(4):
            for half in range(2):
                bank = (oc * 2 + half) % 4
                p.wait(nc.tensor, ps_free[bank])
                last = None
                for hch in range(4):
                    last = T.matmul(PS[bank][:],
                                    WTb[1][:, hch * 512 + oc * 128:hch * 512 + (oc + 1) * 128],
                                    gateT[:, hch * C + half * 512:hch * C + (half + 1) * 512],
                                    start=(hch == 0), stop=(hch == 3))
                to = p.inc(last, s_pe)
                last_out_mm = to
                p.wait(nc.vector, to)
                ov = p.inc(V.tensor_copy(
                    outb[:, oc * C + half * 512:oc * C + (half + 1) * 512],
                    PS[bank][:]), s_ev)
                ps_free[bank] = ov

        if phases < 5:
            for oc in range(4):
                nc.sync.wait_ge(*ov)
                od = p.inc(nc.sync.dma_start(
                    out=ttt_o[oc * 128:(oc + 1) * 128, c * C:(c + 1) * C],
                    in_=outb[:, oc * C:(oc + 1) * C]), s_out, 16)
            continue
        p.wait(nc.vector, ov)
        sq = None
        for oc in range(4):
            sq = V.tensor_mul(hidT[:, oc * C:(oc + 1) * C],
                              outb[:, oc * C:(oc + 1) * C],
                              outb[:, oc * C:(oc + 1) * C])
        sq = p.inc(sq, s_ev)
        p.wait(nc.tensor, sq)
        last = None
        for h in range(2):
            p.wait(nc.tensor, ps_free[h])
            for oc in range(4):
                last = T.matmul(PS[h][0:1, :], onesb[:, 0:1],
                                hidT[:, oc * C + h * 512:oc * C + (h + 1) * 512],
                                start=(oc == 0), stop=(oc == 3))
        tms = p.inc(last, s_pe)
        p.wait(nc.scalar, tms)
        a = None
        for h in range(2):
            a = p.fence(nc.scalar, SC.activation(rowA[0:1, h * 512:(h + 1) * 512],
                                                 PS[h][0:1, :], AF.Ln,
                                                 bias=cst[0:1, 0:1], scale=1.0 / HD), s_ac)
            ps_free[h] = a
        a = p.fence(nc.scalar, SC.activation(rowA[0:1, :], rowA[0:1, :], AF.Exp,
                                             scale=cst[0:1, 1:2]), s_ac)
        p.wait(nc.vector, a)
        rs = p.fence(nc.vector, V.tensor_mul(rowB[0:1, :], rowA[0:1, :], sclrc[:]), s_ev)
        bt2 = bcast_row(rowB, rsgb, C, rs)
        p.wait(nc.vector, bt2)
        p.wait(nc.vector, gc)
        fin = None
        for oc in range(4):
            f1 = p.fence(nc.vector, V.tensor_scalar(
                outb[:, oc * C:(oc + 1) * C], outb[:, oc * C:(oc + 1) * C],
                gcol[:, oc:oc + 1], None, ALU.mult), s_ev)
            fin = p.fence(nc.vector, V.tensor_mul(
                outb[:, oc * C:(oc + 1) * C], outb[:, oc * C:(oc + 1) * C],
                rsgb[:]), s_ev)
        for oc in range(4):
            nc.sync.wait_ge(*fin)
            od = p.inc(nc.sync.dma_start(
                out=ttt_o[oc * 128:(oc + 1) * 128, c * C:(c + 1) * C],
                in_=outb[:, oc * C:(oc + 1) * C]), s_out, 16)

        if phases < 6:
            continue
        ge = None
        for tp in range(NT // 2):
            for half in range(2):
                tc = tp * 2 + half
                for bi, wt in ((0, WTb[0]), (1, WTb[2])):
                    p.wait(nc.tensor, ps_free[half * 2 + bi] if half * 2 + bi < 4 else None)
                    last = None
                    for dch in range(4):
                        last = T.matmul(PS[half * 2 + bi][:],
                                        kn[:, dch * C + tc * 128:dch * C + (tc + 1) * 128],
                                        wt[:, dch * 512:(dch + 1) * 512],
                                        start=(dch == 0), stop=(dch == 3))
                    p.inc(last, s_pe)
            p.wait(nc.tensor, ps4_free)
            last2 = None
            for half in range(2):
                tc = tp * 2 + half
                bankap = PS4[:] if half == 0 else PS5[:]
                for och in range(4):
                    last2 = T.matmul(bankap,
                                     vv[:, och * C + tc * 128:och * C + (tc + 1) * 128],
                                     Wb[1][:, och * 512:(och + 1) * 512],
                                     start=(och == 0), stop=(och == 3))
            tgrp = p.inc(last2, s_pe)

            HBM, DH_, S_, SG, HID = 0, 1, 2, 3, 4
            t0 = tp * 2
            p.wait(nc.scalar, tgrp)
            p.wait(nc.scalar, ge)
            sA = p.fence(nc.scalar, SC.activation(gsc[S_][:, 0:512], PS[0][:],
                                                  AF.Sigmoid), s_ac)
            sA = p.fence(nc.scalar, SC.activation(gsc[S_][:, 512:1024], PS[2][:],
                                                  AF.Sigmoid), s_ac)
            p.wait(nc.vector, tgrp)
            p.wait(nc.vector, ge)
            p.wait(nc.vector, sA)
            sG = p.fence(nc.vector, V.tensor_mul(gsc[SG][:, 0:512],
                                                 gsc[S_][:, 0:512], PS[0][:]), s_ev)
            sG = p.fence(nc.vector, V.tensor_mul(gsc[SG][:, 512:1024],
                                                 gsc[S_][:, 512:1024], PS[2][:]), s_ev)
            e2 = p.fence(nc.vector, V.tensor_copy(gsc[HBM][:, 0:512], PS[1][:]), s_ev)
            e2 = p.fence(nc.vector, V.tensor_copy(gsc[HBM][:, 512:1024], PS[3][:]), s_ev)
            e3 = p.fence(nc.vector, V.tensor_copy(gsc[DH_][:, 0:512], PS4[:]), s_ev)
            e3 = p.fence(nc.vector, V.tensor_copy(gsc[DH_][:, 512:1024], PS5[:]), s_ev)
            for j in range(4):
                ps_free[j] = e2
            ps4_free = e3
            if tp == 0:
                p.wait(nc.vector, od)
                p.wait(nc.vector, tms)
                p.wait(nc.vector, last_out_mm)
            hid_ = p.fence(nc.vector, V.tensor_mul(gsc[HID][:], gsc[SG][:],
                                                   gsc[HBM][:]), s_ev)
            p.wait(nc.vector, lrcf_t)
            h1 = p.fence(nc.vector, V.tensor_scalar(
                hl1[:, t0 * 512:(t0 + 1) * 512], gsc[HID][:, 0:512],
                lrcfc[:, t0 * 4 + 1:t0 * 4 + 2], None, ALU.mult), s_ev)
            h1 = p.fence(nc.vector, V.tensor_scalar(
                hl1[:, (t0 + 1) * 512:(t0 + 2) * 512], gsc[HID][:, 512:1024],
                lrcfc[:, (t0 + 1) * 4 + 1:(t0 + 1) * 4 + 2], None, ALU.mult), s_ev)
            db = p.fence(nc.vector, V.tensor_mul(
                dhb[:, t0 * 512:(t0 + 2) * 512], gsc[DH_][:], gsc[SG][:]), s_ev)
            a_ = p.fence(nc.vector, V.tensor_mul(gsc[HID][:], gsc[SG][:],
                                                 gsc[S_][:]), s_ev)
            a_ = p.fence(nc.vector, V.tensor_sub(gsc[HID][:], gsc[SG][:],
                                                 gsc[HID][:]), s_ev)
            a_ = p.fence(nc.vector, V.tensor_add(gsc[HID][:], gsc[S_][:],
                                                 gsc[HID][:]), s_ev)
            t1 = p.fence(nc.vector, V.tensor_mul(gsc[SG][:], gsc[DH_][:],
                                                 gsc[HBM][:]), s_ev)
            ge = p.fence(nc.vector, V.tensor_mul(
                dgb[:, t0 * 512:(t0 + 2) * 512], gsc[SG][:], gsc[HID][:]), s_ev)
        grad_done = ge
        if c + 1 < nchunks:
            npar = (c + 1) % 2
            nin = s_in[npar]
            p.wait(nc.gpsimd, grad_done)
            ldq_next = None
            for oc in range(4):
                for (dst, srcd) in ((qn, qh_d), (kn, kh_d), (vv, vh_d)):
                    ldq_next = p.inc(nc.gpsimd.dma_start(
                        out=dst[:, oc * C:(oc + 1) * C],
                        in_=srcd[oc * 128:(oc + 1) * 128, (c + 1) * C:(c + 2) * C]),
                        nin, 16)
            ldq_next = p.inc(nc.gpsimd.dma_start(
                out=lrcf[npar][:], in_=lrtm_d[:, (c + 1) * 32:(c + 2) * 32]), nin, 16)
            ldq_next = p.inc(nc.gpsimd.dma_start(
                out=momr[:], in_=mom_d[:, (c + 1) * C:(c + 2) * C]), nin, 16)
            ldq_next = p.inc(nc.gpsimd.dma_start(
                out=sclr[:], in_=scl_d[:, (c + 1) * C:(c + 2) * C]), nin, 16)

        for m in (range(3) if phases >= 7 else ()):
            lhs_slab = [dgb, vTM, dhb][m]
            p.wait(nc.tensor, grad_done)
            p.wait(nc.tensor, ktm_done)
            for b in range(4):
                p.wait(nc.tensor, ps_free[b])
            tdw = None
            tdw_prev = [None, None]
            kls = [None, None]
            if m != 1:
                li = 0 if m == 0 else 2
                kls[0] = p.fence(nc.vector, V.tensor_scalar(
                    klb[0][:], kTM[:, 0:512],
                    lrcfc[:, li:li + 1], None, ALU.mult), s_ev)
            for tc in range(NT):
                kp = tc % 2
                if m != 1:
                    li = 0 if m == 0 else 2
                    if tc + 1 < NT:
                        p.wait(nc.vector, tdw_prev[1 - kp])
                        kls[1 - kp] = p.fence(nc.vector, V.tensor_scalar(
                            klb[1 - kp][:], kTM[:, (tc + 1) * 512:(tc + 2) * 512],
                            lrcfc[:, (tc + 1) * 4 + li:(tc + 1) * 4 + li + 1], None,
                            ALU.mult), s_ev)
                    p.wait(nc.tensor, kls[kp])
                    rhs_ap = klb[kp][:]
                else:
                    rhs_ap = hl1[:, tc * 512:(tc + 1) * 512]
                last = None
                for mc in range(4):
                    last = T.matmul(PS[mc][:],
                                    lhs_slab[:, tc * 512 + mc * 128:tc * 512 + (mc + 1) * 128],
                                    rhs_ap,
                                    start=(tc == 0), stop=(tc == NT - 1))
                tdw = p.inc(last, s_pe)
                tdw_prev[kp] = tdw

            p.wait(nc.vector, tdw)
            p.wait(nc.scalar, mmr)
            ev = p.fence(nc.vector, V.scalar_tensor_tensor(
                dwf[:], Mb[m][:], mmc[:, 0:1], PSA[:], ALU.mult, ALU.add), s_ev)
            for j in range(4):
                ps_free[j] = ev
            mnew = p.fence(nc.vector, V.tensor_copy(Mb[m][:], dwf[:]), s_ev)

            p.wait(nc.scalar, mnew)
            aq2 = None
            for j in range(4):
                aq2 = p.fence(nc.scalar, SC.activation(
                    gsc[0][:, 0:512], dwf[:, j * 512:(j + 1) * 512], AF.Square,
                    accum_out=rcol[:, j:j + 1]), s_ac)
            p.wait(nc.tensor, aq2)
            p.wait(nc.tensor, ps4_free)
            tnf = p.inc(T.matmul(PS4[0:1, 0:4], onesf[:, 0:1], rcol[:, 0:4],
                                 start=True, stop=True), s_pe)
            p.wait(nc.vector, tnf)
            nf = p.fence(nc.vector, V.reduce_sum(srow[0:1, 0:1], PS4[0:1, 0:4],
                                                 axis=AX.X), s_ev)
            ps4_free = nf
            p.wait(nc.scalar, nf)
            nf2 = p.fence(nc.scalar, SC.sqrt(srow[0:1, 1:2], srow[0:1, 0:1]), s_ac)
            p.wait(nc.vector, nf2)
            nf3 = p.fence(nc.vector, V.tensor_scalar_add(srow[0:1, 2:3], srow[0:1, 1:2],
                                                         1e-7), s_ev)
            nf4 = p.fence(nc.vector, V.reciprocal(srow[0:1, 3:4], srow[0:1, 2:3]), s_ev)
            p.wait(nc.tensor, nf4)
            rno = bcast_col(srow[0:1, 3:4], rnc[:, 0:1])

            p.wait(nc.vector, rno)
            x0 = None
            for j in range(4):
                x0 = p.fence(nc.vector, V.tensor_scalar(
                    Xb[0][:, j * 512:(j + 1) * 512], dwf[:, j * 512:(j + 1) * 512],
                    rnc[:, 0:1], None, ALU.mult), s_ev)
            x0t = transpose512(Xb[0], XTb[0], x0)

            cx = 0
            prev_ev = x0t
            x5 = None
            for it, (ca, cb_, cc_) in enumerate(NS_COEFFS):
                XB, XT_ = Xb[cx], XTb[cx]
                XB2, XT2 = Xb[1 - cx], XTb[1 - cx]
                p.wait(nc.tensor, prev_ev)
                for b in range(4):
                    p.wait(nc.tensor, ps_free[b])
                last = None
                for ic in range(4):
                    for dch in range(4):
                        last = T.matmul(PS[ic][:],
                                        XT_[:, dch * 512 + ic * 128:dch * 512 + (ic + 1) * 128],
                                        XT_[:, dch * 512:(dch + 1) * 512],
                                        start=(dch == 0), stop=(dch == 3))
                tA = p.inc(last, s_pe)
                p.wait(nc.scalar, tA)
                p.wait(nc.vector, tA)
                SC.copy(Ab[:, 0:1024], PSA[:, 0:1024])
                ab = p.inc(SC.activation(Acb[:, 0:1024], PSA[:, 0:1024], AF.Copy,
                                         scale=float(cc_)), s_ac)
                V.tensor_copy(Ab[:, 1024:2048], PSA[:, 1024:2048])
                acb = p.inc(V.tensor_scalar(Acb[:, 1024:2048], PSA[:, 1024:2048],
                                            float(cc_), None, ALU.mult), s_ev)
                p.wait(nc.tensor, ab)
                p.wait(nc.tensor, acb)
                last = None
                for ic in range(4):
                    for jch in range(4):
                        last = T.matmul(PS[ic][:],
                                        Ab[:, jch * 512 + ic * 128:jch * 512 + (ic + 1) * 128],
                                        Acb[:, jch * 512:(jch + 1) * 512],
                                        start=(jch == 0), stop=(jch == 3))
                tA2 = p.inc(last, s_pe)
                p.wait(nc.vector, tA2)
                bb_ = p.fence(nc.vector, V.scalar_tensor_tensor(
                    Acb[:], Ab[:], float(cb_), PSA[:], ALU.mult, ALU.add), s_ev)
                p.wait(nc.tensor, bb_)
                last = None
                for ic in range(4):
                    for jch in range(4):
                        T.matmul(PS[ic][:],
                                 Acb[:, jch * 512 + ic * 128:jch * 512 + (ic + 1) * 128],
                                 XB[:, jch * 512:(jch + 1) * 512],
                                 start=(jch == 0), stop=False)
                    last = T.matmul(PS[ic][:], diag[:, it * 128:(it + 1) * 128],
                                    XB[:, ic * 512:(ic + 1) * 512],
                                    start=False, stop=True)
                tBX = p.inc(last, s_pe)
                if it == 4:
                    p.wait(nc.scalar, tBX)
                    p.wait(nc.vector, tBX)
                    x5a = p.inc(SC.copy(X5f[:, 0:1024], PSA[:, 0:1024]), s_ac)
                    x5 = p.inc(V.tensor_copy(X5f[:, 1024:2048], PSA[:, 1024:2048]), s_ev)
                    x5 = [x5a, x5]
                    for j in range(4):
                        ps_free[j] = x5
                else:
                    p.wait(nc.scalar, tBX)
                    p.wait(nc.vector, tBX)
                    xba = p.inc(SC.copy(XB2[:, 0:1024], PSA[:, 0:1024]), s_ac)
                    xbb = p.inc(V.tensor_copy(XB2[:, 1024:2048], PSA[:, 1024:2048]), s_ev)
                    xb2 = [xba, xbb]
                    p.wait(nc.tensor, xb2)
                    last = None
                    for dc in range(4):
                        for jch in range(4):
                            T.matmul(PS[dc][:],
                                     XB[:, jch * 512 + dc * 128:jch * 512 + (dc + 1) * 128],
                                     Acb[:, jch * 512:(jch + 1) * 512],
                                     start=(jch == 0), stop=False)
                        last = T.matmul(PS[dc][:], diag[:, it * 128:(it + 1) * 128],
                                        XT_[:, dc * 512:(dc + 1) * 512],
                                        start=False, stop=True)
                    tXT = p.inc(last, s_pe)
                    p.wait(nc.vector, tXT)
                    p.wait(nc.scalar, tXT)
                    xta = p.inc(SC.copy(XT2[:, 0:1024], PSA[:, 0:1024]), s_ac)
                    xtb = p.inc(V.tensor_copy(XT2[:, 1024:2048], PSA[:, 1024:2048]), s_ev)
                    xt2 = [xta, xtb]
                    for j in range(4):
                        ps_free[j] = xt2
                    prev_ev = xt2
                    cx = 1 - cx

            p.wait(nc.vector, x5)
            tw = p.fence(nc.vector, V.tensor_add(dwf[:], W[m][:], X5f[:]), s_ev)
            p.wait(nc.scalar, tw)
            p.wait(nc.scalar, nrm)
            rn = None
            for j in range(4):
                rn = p.fence(nc.scalar, SC.activation(
                    gsc[0][:, 0:512], dwf[:, j * 512:(j + 1) * 512], AF.Square,
                    accum_out=rcol[:, j:j + 1]), s_ac)
            rn = p.fence(nc.scalar, SC.sqrt(rcol[:, 0:4], rcol[:, 0:4]), s_ac)
            p.wait(nc.vector, rn)
            sc1 = p.fence(nc.vector, V.tensor_scalar_add(rcol[:, 0:4], rcol[:, 0:4],
                                                         1e-5), s_ev)
            sc2 = p.fence(nc.vector, V.reciprocal(rcol[:, 4:8], rcol[:, 0:4]), s_ev)
            sc3 = p.fence(nc.vector, V.tensor_mul(rcol[:, 4:8], rcol[:, 4:8],
                                                  ncol[:, 4 * m:4 * m + 4]), s_ev)
            wn = None
            for j in range(4):
                wn = p.fence(nc.vector, V.tensor_scalar(
                    W[m][:, j * 512:(j + 1) * 512], dwf[:, j * 512:(j + 1) * 512],
                    rcol[:, 4 + j:5 + j], None, ALU.mult), s_ev)
            wb2 = p.fence(nc.vector, V.tensor_copy(Wb[m][:], W[m][:]), s_ev)
            chunk_last = transpose512(Wb[m], WTb[m], wb2)
        if phases >= 7:
            prev_wt = chunk_last

    nc.sync.wait_ge(s_out, p.val(s_out))
    p.close()
    return p.nc


def _lr_tm(lr3):
    out = np.zeros((128, NCHUNK * 32), np.float32)
    a = np.asarray(lr3, np.float32).reshape(3, NCHUNK, 8, 128)
    for c in range(NCHUNK):
        for tc in range(8):
            for i in range(3):
                out[:, c * 32 + tc * 4 + i] = a[i, c, tc]
    return out


def _chain_inputs(st1, inp, head):
    return {
        "qh": np.ascontiguousarray(st1["qkv"][head * HD:(head + 1) * HD]),
        "kh": np.ascontiguousarray(st1["qkv"][2048 + head * HD:2048 + (head + 1) * HD]),
        "vh": np.ascontiguousarray(st1["qkv"][4096 + head * HD:4096 + (head + 1) * HD]),
        "lrtm": _lr_tm(st1["aux"][3 * head:3 * head + 3]),
        "momh": np.ascontiguousarray(st1["aux"][32 + head:33 + head]).astype(np.float32),
        "sclh": np.ascontiguousarray(st1["aux"][64 + head:65 + head]).astype(np.float32),
        "w0": np.ascontiguousarray(inp["L0_attn_w0"][head]).astype(np.float32),
        "w1": np.ascontiguousarray(inp["L0_attn_w1"][head]).astype(np.float32),
        "w2": np.ascontiguousarray(inp["L0_attn_w2"][head]).astype(np.float32),
        "gcolh": np.ascontiguousarray(
            inp["L0_attn_o_norm_g"].reshape(4, 128).T).astype(np.float32),
        "idt": np.eye(128).astype(BF),
        "diag": np.concatenate([np.eye(128) * a for (a, b, c) in NS_COEFFS],
                               axis=1).astype(BF),
    }


def run_chain_stage(st1, inp):
    nc = build_chain()
    in_maps = [_chain_inputs(st1, inp, c % 4) for c in range(8)]
    res = run_bass_kernel_spmd(nc, in_maps, core_ids=list(range(8)))
    return np.concatenate([res.results[c]["ttt"] for c in range(4)], axis=0)



def build_attn(nheads=NAH):
    p = P("attn")
    nc = p.nc
    aq_d = p.dram("aqh", [DIM, CHUNK], BF16, kind="ExternalInput")
    ak_d = p.dram("akh", [DIM, 2 * CHUNK], BF16, kind="ExternalInput")
    av_d = p.dram("avh", [DIM, 2 * CHUNK], BF16, kind="ExternalInput")
    ttt_d = p.dram("tttc", [DIM, CHUNK], BF16, kind="ExternalInput")
    mup_d = p.dram("mup", [128, 128], F32, kind="ExternalInput")
    mlo_d = p.dram("mlo", [128, 128], F32, kind="ExternalInput")
    negc_d = p.dram("negc", [128, 1], F32, kind="ExternalInput")
    idt_d = p.dram("idt", [128, 128], BF16, kind="ExternalInput")
    mix_o = p.dram("mix", [DIM, CHUNK], BF16, kind="ExternalOutput")

    NQT = CHUNK // 128
    WINC = 12

    aqb = [p.sb(f"aqb{i}", [128, CHUNK], BF16) for i in range(2)]
    akb = [p.sb(f"akb{i}", [128, 2 * CHUNK], BF16) for i in range(2)]
    avb = [p.sb(f"avb{i}", [128, 2 * CHUNK], BF16) for i in range(2)]
    ttb = [p.sb(f"ttb{i}", [128, CHUNK], BF16) for i in range(2)]
    avTM = p.sb("avTM", [128, 16 * 128], BF16)
    probs_f = p.sb("probs_f", [128, 1536], F32)
    probsb = p.sb("probsb", [128, 1536], BF16)
    strip = p.sb("strip", [128, WINC * 512], BF16)
    mix = p.sb("mix_sb", [128, 16 * CHUNK], BF16)
    mup = p.sb("mup_sb", [128, 128], F32)
    mlo = p.sb("mlo_sb", [128, 128], F32)
    negc = p.sb("negc_sb", [128, 1], F32)
    idt = p.sb("idt_sb", [128, 128], BF16)
    mcol = p.sb("mcol", [128, 8], F32)
    zcol = p.sb("zcol", [128, 8], F32)

    PS = [p.ps(f"PS{i}", [128, 512], F32) for i in range(4)]
    PT = [p.ps(f"PTb{i}", [128, 1024], BF16) for i in range(2)]

    s_in = [p.sem(f"s_ina{i}") for i in range(2)]
    s_w = p.sem("s_wa")
    s_pe = p.sem("s_pea")
    s_ev = p.sem("s_eva")
    s_ac = p.sem("s_aca")
    s_out = p.sem("s_outa")

    V, SC, T = nc.vector, nc.scalar, nc.tensor

    ps_free = [None] * 4
    pt_bank_free = [None, None]
    pt_ctr = [0]

    def transpose_batch(pairs):
        tok = None
        bank = pt_ctr[0] % 2
        i = 0
        while i < len(pairs):
            grp = pairs[i:i + 8]
            i += 8
            pt_ctr[0] += 1
            bank = 1 - bank
            p.wait(nc.tensor, pt_bank_free[bank])
            tt = None
            for r, (s_ap, d_ap) in enumerate(grp):
                tt = p.inc(T.transpose(PT[bank][:, r * 128:(r + 1) * 128], s_ap,
                                       idt[:]), s_pe)
            p.wait(nc.vector, tt)
            tok = None
            for r, (s_ap, d_ap) in enumerate(grp):
                tok = p.inc(V.tensor_copy(d_ap, PT[bank][:, r * 128:(r + 1) * 128]),
                            s_ev)
            pt_bank_free[bank] = tok
        return tok

    wl = p.inc(nc.gpsimd.dma_start(out=mup[:], in_=mup_d), s_w, 16)
    wl = p.inc(nc.gpsimd.dma_start(out=mlo[:], in_=mlo_d), s_w, 16)
    wl = p.inc(nc.gpsimd.dma_start(out=negc[:], in_=negc_d), s_w, 16)
    wl = p.inc(nc.gpsimd.dma_start(out=idt[:], in_=idt_d), s_w, 16)

    head_done = [None, None]
    ld = [None, None]
    for a in range(nheads + 1):
        par = a % 2
        if a < nheads:
            p.wait(nc.gpsimd, head_done[par])
            t = None
            t = p.inc(nc.gpsimd.dma_start(
                out=aqb[par][:], in_=aq_d[a * 128:(a + 1) * 128, :]), s_in[par], 16)
            t = p.inc(nc.gpsimd.dma_start(
                out=akb[par][:], in_=ak_d[a * 128:(a + 1) * 128, :]), s_in[par], 16)
            t = p.inc(nc.gpsimd.dma_start(
                out=avb[par][:], in_=av_d[a * 128:(a + 1) * 128, :]), s_in[par], 16)
            t = p.inc(nc.gpsimd.dma_start(
                out=ttb[par][:], in_=ttt_d[a * 128:(a + 1) * 128, :]), s_in[par], 16)
            ld[par] = t
        if a == 0:
            continue
        h = a - 1
        bpar = 1 - par
        aq, ak, av, tb = aqb[bpar], akb[bpar], avb[bpar], ttb[bpar]
        p.wait(nc.vector, ld[bpar])
        aqs = p.fence(nc.vector, V.tensor_scalar(
            aq[:], aq[:], float(1.0 / np.sqrt(AHD)), None, ALU.mult), s_ev)
        p.wait(nc.tensor, ld[bpar])
        p.wait(nc.tensor, aqs)
        p.wait(nc.tensor, wl)
        avt = transpose_batch([(av[:, t_ * 128:(t_ + 1) * 128],
                                avTM[:, t_ * 128:(t_ + 1) * 128]) for t_ in range(16)])
        mix_ev = None
        for grp in range(2):
            p.wait(nc.vector, None)
            zs = p.fence(nc.vector, V.memset(strip[:], 0.0), s_ev)
            trs = None
            for qt in range(4):
                qg = grp * 4 + qt
                base = 128 * qg
                widths = (512, 512, 128)
                for i3 in range(3):
                    p.wait(nc.tensor, ps_free[i3])
                    off = base + (512 * i3)
                    tS = p.inc(T.matmul(
                        PS[i3][:, 0:widths[i3]], aq[:, qg * 128:(qg + 1) * 128],
                        ak[:, off:off + widths[i3]], start=True, stop=True), s_pe)
                    ps_free[i3] = None
                p.wait(nc.vector, tS)
                mk = p.fence(nc.vector, V.tensor_add(PS[0][:, 0:128], PS[0][:, 0:128],
                                                     mup[:]), s_ev)
                mk = p.fence(nc.vector, V.tensor_add(PS[2][:, 0:128], PS[2][:, 0:128],
                                                     mlo[:]), s_ev)
                pref = 1024 - 128 * qg
                if pref > 0:
                    w0 = min(pref, 512)
                    mk = p.fence(nc.vector, V.tensor_scalar(
                        PS[0][:, 0:w0], PS[0][:, 0:w0], negc[:, 0:1], None,
                        ALU.add), s_ev)
                    if pref > 512:
                        mk = p.fence(nc.vector, V.tensor_scalar(
                            PS[1][:, 0:pref - 512], PS[1][:, 0:pref - 512],
                            negc[:, 0:1], None, ALU.add), s_ev)
                for i3 in range(3):
                    mk = p.fence(nc.vector, V.reduce_max(
                        mcol[:, i3:i3 + 1], PS[i3][:, 0:widths[i3]], axis=AX.X), s_ev)
                mk = p.fence(nc.vector, V.reduce_max(mcol[:, 3:4], mcol[:, 0:3],
                                                     axis=AX.X, negate=True), s_ev)
                p.wait(nc.scalar, mk)
                az = None
                for i3 in range(3):
                    az = p.fence(nc.scalar, SC.activation(
                        probs_f[:, 512 * i3:512 * i3 + widths[i3]],
                        PS[i3][:, 0:widths[i3]], AF.Exp, bias=mcol[:, 3:4],
                        accum_out=zcol[:, i3:i3 + 1]), s_ac)
                p.wait(nc.vector, az)
                zz = p.fence(nc.vector, V.tensor_add(zcol[:, 3:4], zcol[:, 0:1],
                                                     zcol[:, 1:2]), s_ev)
                zz = p.fence(nc.vector, V.tensor_add(zcol[:, 3:4], zcol[:, 3:4],
                                                     zcol[:, 2:3]), s_ev)
                zz = p.fence(nc.vector, V.reciprocal(zcol[:, 4:5], zcol[:, 3:4]), s_ev)
                p.wait(nc.scalar, zz)
                pb = p.inc(SC.activation(probsb[:, 0:1152], probs_f[:, 0:1152],
                                         AF.Copy, scale=zcol[:, 4:5]), s_ac)
                for i3 in range(3):
                    ps_free[i3] = pb
                p.wait(nc.tensor, pb)
                p.wait(nc.tensor, zs)
                trs = transpose_batch(
                    [(probsb[:, j * 128:(j + 1) * 128],
                      strip[:, (qt + j) * 512 + qt * 128:(qt + j) * 512 + (qt + 1) * 128])
                     for j in range(9)])
            p.wait(nc.tensor, trs)
            p.wait(nc.tensor, avt)
            p.wait(nc.tensor, ps_free[3])
            last = None
            for kc in range(WINC):
                ktile = grp * 4 + kc
                last = T.matmul(PS[3][:], avTM[:, ktile * 128:(ktile + 1) * 128],
                                strip[:, kc * 512:(kc + 1) * 512],
                                start=(kc == 0), stop=(kc == WINC - 1))
            tpv = p.inc(last, s_pe)
            p.wait(nc.vector, tpv)
            p.wait(nc.vector, ld[bpar])
            mix_ev = p.inc(V.tensor_add(
                mix[:, h * CHUNK + grp * 512:h * CHUNK + (grp + 1) * 512],
                tb[:, grp * 512:(grp + 1) * 512], PS[3][:]), s_ev)
            ps_free[3] = mix_ev
        nc.sync.wait_ge(*mix_ev)
        od = p.inc(nc.sync.dma_start(
            out=mix_o[h * 128:(h + 1) * 128, :],
            in_=mix[:, h * CHUNK:(h + 1) * CHUNK]), s_out, 16)
        head_done[bpar] = [od, avt]

    nc.sync.wait_ge(s_out, p.val(s_out))
    p.close()
    return p.nc


def _attn_inputs(st1, st2, chunk):
    c0 = chunk * CHUNK
    ak = np.zeros((DIM, 2 * CHUNK), dtype=BF)
    av = np.zeros((DIM, 2 * CHUNK), dtype=BF)
    lo = max(0, c0 - CHUNK)
    n = c0 - lo
    if n > 0:
        ak[:, CHUNK - n:CHUNK] = st1["aqk"][DIM:, lo:c0]
        av[:, CHUNK - n:CHUNK] = st1["av"][:, lo:c0]
    ak[:, CHUNK:] = st1["aqk"][DIM:, c0:c0 + CHUNK]
    av[:, CHUNK:] = st1["av"][:, c0:c0 + CHUNK]
    qi = np.arange(128)[:, None]
    kj = np.arange(128)[None, :]
    mup = np.where(kj >= qi, 0.0, NEG_BIG).astype(np.float32)
    mlo = np.where(kj <= qi, 0.0, NEG_BIG).astype(np.float32)
    return {
        "aqh": np.ascontiguousarray(st1["aqk"][:DIM, c0:c0 + CHUNK]),
        "akh": ak, "avh": av,
        "tttc": np.ascontiguousarray(st2[:, c0:c0 + CHUNK]),
        "mup": mup, "mlo": mlo,
        "negc": np.full((128, 1), NEG_BIG if chunk == 0 else 0.0, np.float32),
        "idt": np.eye(128).astype(BF),
    }


def run_attn_stage(st1, st2):
    nc = build_attn()
    in_maps = [_attn_inputs(st1, st2, c) for c in range(8)]
    res = run_bass_kernel_spmd(nc, in_maps, core_ids=list(range(8)))
    return np.concatenate([res.results[c]["mix"] for c in range(8)], axis=1)



def build_tail(ttok=1024):
    p = P("tail")
    nc = p.nc
    mix_d = p.dram("mixc", [DIM, ttok], BF16, kind="ExternalInput")
    x0_d = p.dram("x0t", [DIM, ttok], F32, kind="ExternalInput")
    wo_d = p.dram("wo", [DIM, DIM], F32, kind="ExternalInput")
    w1_d = p.dram("w1t", [DIM, FFN_H], F32, kind="ExternalInput")
    w3_d = p.dram("w3t", [DIM, FFN_H], F32, kind="ExternalInput")
    w2_d = p.dram("w2t", [FFN_H, DIM], F32, kind="ExternalInput")
    hblk_d = p.dram("hblk", [128, 128], BF16, kind="ExternalInput")
    h128_d = p.dram("h128", [128, 128], BF16, kind="ExternalInput")
    out_d = p.dram("outt", [DIM, ttok], F32, kind="ExternalOutput")
    w2pp_d = p.dram("w2pp", [FFN_H, DIM], BF16, kind="Internal")
    y1_d = p.dram("y1d", [FFN_H, ttok], BF16, kind="Internal")

    NTB = ttok // 512
    NSUB = ttok // 512

    mix = p.sb("mixs", [128, 16 * ttok], BF16)
    ab = p.sb("abs_", [128, 16 * ttok], BF16)
    x1b = p.sb("x1b", [128, 16 * ttok], BF16)
    wbuf = p.sb("wbuf", [128, 32768], BF16)
    wsl = [p.sb(f"wsl{i}", [128, 2048], BF16) for i in range(2)]
    fb = p.sb("fb", [128, 16 * 512], BF16)
    outf = p.sb("outf", [128, 2 * 512], F32)
    hblk = p.sb("hblk_sb", [128, 128], BF16)
    h128 = p.sb("h128_sb", [128, 128], BF16)
    hsb = p.sb("hsb", [128, 512], BF16)
    sfb = p.sb("sfb", [128, 512], BF16)
    rows = p.sb("rows", [1, 3 * 1024], F32)
    bcA = p.sb("bcA", [128, 1024], BF16)
    bcB = p.sb("bcB", [128, 1024], BF16)
    onesb = p.sb("onesb", [128, 1], BF16)
    onesrow_b = p.sb("onesrow_b", [1, 128], BF16)
    cst = p.sb("cst", [128, 2], F32)
    y1tmp = p.sb("y1tmp", [128, 512], BF16)

    PS = [p.ps(f"PS{i}", [128, 512], F32) for i in range(8)]

    s_w = [p.sem(f"s_wt{i}") for i in range(2)]
    s_x = p.sem("s_xt")
    s_mixl = p.sem("s_mixl")
    s_wol = p.sem("s_wol")
    s_x0l = p.sem("s_x0l")
    s_prm = p.sem("s_prmt")
    s_pe = p.sem("s_pet")
    s_ev = p.sem("s_evt")
    s_ac = p.sem("s_act")
    s_out = [p.sem(f"s_outt{i}") for i in range(2)]
    s_y1 = p.sem("s_y1")
    s_pp = p.sem("s_pp")

    V, SC, T = nc.vector, nc.scalar, nc.tensor
    ps_free = [None] * 8

    nc.vector.memset(cst[:, 0:1], 1e-5)
    nc.vector.memset(onesb[:], 1.0)
    ini = p.inc(nc.vector.memset(onesrow_b[:], 1.0), s_ev)

    wl = p.inc(nc.gpsimd.dma_start(out=hblk[:], in_=hblk_d), s_prm, 16)
    wl = p.inc(nc.gpsimd.dma_start(out=h128[:], in_=h128_d), s_prm, 16)
    mld = None
    for dch in range(16):
        mld = p.inc(nc.gpsimd.dma_start(
            out=mix[:, dch * ttok:(dch + 1) * ttok],
            in_=mix_d[dch * 128:(dch + 1) * 128, :]), s_mixl, 16)

    w2r = w2_d.rearrange("(ih il) d -> il ih d", il=64)
    wld = [None, None]
    buf_cons = [None, None]
    pp_done = None
    pp_dmas = []
    for il in range(64):
        par = il % 2
        p.wait(nc.gpsimd, buf_cons[par])
        wld[par] = p.inc(nc.gpsimd.dma_start(out=wsl[par][:], in_=w2r[il]),
                         s_w[par], 16)
        p.wait(nc.tensor, wld[par])
        p.wait(nc.tensor, wl)
        for q in range(4):
            bank = q
            p.wait(nc.tensor, ps_free[bank])
            tq = p.inc(T.matmul(PS[bank][:], h128[:],
                                wsl[par][:, q * 512:(q + 1) * 512],
                                start=True, stop=True), s_pe)
            buf_cons[par] = tq
            p.wait(nc.vector, tq)
            p.wait(nc.vector, pp_done)
            ev = p.inc(V.tensor_copy(fb[:, q * 1024:q * 1024 + 512], PS[bank][:]),
                       s_ev)
            ps_free[bank] = ev
            nc.sync.wait_ge(*ev)
            pp_done = p.inc(nc.sync.dma_start(
                out=w2pp_d.rearrange("(jh il) d -> il jh d", il=64)[il][:, q * 512:(q + 1) * 512],
                in_=fb[:, q * 1024:q * 1024 + 512]), s_pp, 16)

    wol = None
    for ic in range(16):
        wol = p.inc(nc.gpsimd.dma_start(
            out=wbuf[:, ic * 2048:(ic + 1) * 2048],
            in_=wo_d[ic * 128:(ic + 1) * 128, :]), s_wol, 16)
    p.wait(nc.tensor, wol)
    p.wait(nc.tensor, mld)
    ab_ev = None
    for tb in range(NTB):
        for oc in range(16):
            bank = oc % 8
            p.wait(nc.tensor, ps_free[bank])
            last = None
            for ic in range(16):
                last = T.matmul(PS[bank][:],
                                wbuf[:, ic * 2048 + oc * 128:ic * 2048 + (oc + 1) * 128],
                                mix[:, ic * ttok + tb * 512:ic * ttok + (tb + 1) * 512],
                                start=(ic == 0), stop=(ic == 15))
            t_mm = p.inc(last, s_pe)
            p.wait(nc.vector, t_mm)
            ab_ev = p.inc(V.tensor_copy(
                ab[:, oc * ttok + tb * 512:oc * ttok + (tb + 1) * 512], PS[bank][:]),
                s_ev)
            ps_free[bank] = ab_ev

    def ln_stats(slab, sqslab, sq_rdy, row_off):
        sq = None
        p.wait(nc.vector, sq_rdy)
        for dch in range(16):
            sq = V.tensor_mul(sqslab[:, dch * ttok:(dch + 1) * ttok],
                              slab[:, dch * ttok:(dch + 1) * ttok],
                              slab[:, dch * ttok:(dch + 1) * ttok])
        sq = p.inc(sq, s_ev)
        p.wait(nc.tensor, sq)
        p.wait(nc.tensor, ini)
        toks = []
        for h in range(ttok // 512):
            bank = h % 4
            p.wait(nc.tensor, ps_free[bank])
            p.wait(nc.tensor, ps_free[4 + bank])
            last = None
            for dch in range(16):
                last = T.matmul(PS[bank][0:1, :], onesb[:, 0:1],
                                slab[:, dch * ttok + h * 512:dch * ttok + (h + 1) * 512],
                                start=(dch == 0), stop=(dch == 15))
            for dch in range(16):
                last = T.matmul(PS[4 + bank][0:1, :], onesb[:, 0:1],
                                sqslab[:, dch * ttok + h * 512:dch * ttok + (h + 1) * 512],
                                start=(dch == 0), stop=(dch == 15))
            tt = p.inc(last, s_pe)
            p.wait(nc.scalar, tt)
            a1 = p.fence(nc.scalar, SC.mul(rows[0:1, row_off + h * 512:row_off + (h + 1) * 512],
                                           PS[bank][0:1, :], -1.0 / DIM), s_ac)
            a1 = p.fence(nc.scalar, SC.mul(rows[0:1, row_off + ttok + h * 512:row_off + ttok + (h + 1) * 512],
                                           PS[4 + bank][0:1, :], 1.0 / DIM), s_ac)
            ps_free[bank] = a1
            ps_free[4 + bank] = a1
            toks.append(a1)
        nm = rows[0:1, row_off:row_off + ttok]
        msq = rows[0:1, row_off + ttok:row_off + 2 * ttok]
        p.wait(nc.vector, toks[-1])
        v1 = p.fence(nc.vector, V.tensor_mul(rows[0:1, row_off + 2 * ttok:row_off + 3 * ttok],
                                             nm, nm), s_ev)
        v1 = p.fence(nc.vector, V.tensor_sub(msq, msq,
                                             rows[0:1, row_off + 2 * ttok:row_off + 3 * ttok]),
                     s_ev)
        p.wait(nc.scalar, v1)
        a2 = p.fence(nc.scalar, SC.activation(msq, msq, AF.Ln, bias=cst[0:1, 0:1]), s_ac)
        a2 = p.fence(nc.scalar, SC.activation(msq, msq, AF.Exp, scale=-0.5), s_ac)
        return a2

    def ln_bcast(row_off, rdy):
        p.wait(nc.vector, rdy)
        p.wait(nc.scalar, rdy)
        c1 = p.fence(nc.vector, V.tensor_copy(fb[0:1, 0:ttok],
                                              rows[0:1, row_off:row_off + ttok]), s_ev)
        c2 = p.fence(nc.scalar, SC.copy(bcB[0:1, 0:ttok],
                                        rows[0:1, row_off + ttok:row_off + 2 * ttok]), s_ac)
        toks = []
        for h in range(ttok // 512):
            bank = h % 4
            p.wait(nc.tensor, [ps_free[bank], c1, c2])
            t1 = p.inc(T.matmul(PS[bank][:], onesrow_b[:],
                                fb[0:1, h * 512:(h + 1) * 512], start=True, stop=True),
                       s_pe)
            p.wait(nc.vector, t1)
            e1 = p.inc(V.tensor_copy(bcA[:, h * 512:(h + 1) * 512], PS[bank][:]), s_ev)
            ps_free[bank] = e1
            p.wait(nc.tensor, e1)
            t2 = p.inc(T.matmul(PS[bank][:], onesrow_b[:],
                                bcB[0:1, h * 512:(h + 1) * 512], start=True, stop=True),
                       s_pe)
            p.wait(nc.vector, t2)
            e2 = p.inc(V.tensor_copy(bcB[:, h * 512:(h + 1) * 512], PS[bank][:]), s_ev)
            ps_free[bank] = e2
            toks.append(e2)
        return toks[-1]

    st2tok = ln_stats(ab, mix, ab_ev, 0)
    bc2 = ln_bcast(0, st2tok)
    x0ld = None
    p.wait(nc.gpsimd, bc2)
    for dch in range(16):
        x0ld = p.inc(nc.gpsimd.dma_start(
            out=mix[:, dch * ttok:(dch + 1) * ttok],
            in_=x0_d[dch * 128:(dch + 1) * 128, :]), s_x0l, 16)
    x1tok = None
    p.wait(nc.vector, bc2)
    p.wait(nc.vector, x0ld)
    for dch in range(16):
        x1tok = p.fence(nc.vector, V.tensor_add(
            x1b[:, dch * ttok:(dch + 1) * ttok], ab[:, dch * ttok:(dch + 1) * ttok],
            bcA[:, 0:ttok]), s_ev)
        x1tok = p.fence(nc.vector, V.tensor_mul(
            x1b[:, dch * ttok:(dch + 1) * ttok], x1b[:, dch * ttok:(dch + 1) * ttok],
            bcB[:, 0:ttok]), s_ev)
        x1tok = p.fence(nc.vector, V.tensor_add(
            x1b[:, dch * ttok:(dch + 1) * ttok], x1b[:, dch * ttok:(dch + 1) * ttok],
            mix[:, dch * ttok:(dch + 1) * ttok]), s_ev)
    st3tok = ln_stats(x1b, ab, x1tok, 0)
    bc3 = ln_bcast(0, st3tok)
    xn_tok = None
    p.wait(nc.vector, bc3)
    for dch in range(16):
        xn_tok = p.fence(nc.vector, V.tensor_add(
            ab[:, dch * ttok:(dch + 1) * ttok], x1b[:, dch * ttok:(dch + 1) * ttok],
            bcA[:, 0:ttok]), s_ev)
        xn_tok = p.fence(nc.vector, V.tensor_mul(
            ab[:, dch * ttok:(dch + 1) * ttok], ab[:, dch * ttok:(dch + 1) * ttok],
            bcB[:, 0:ttok]), s_ev)
    x1n = ab

    p.wait(nc.tensor, xn_tok)
    wload = [None, None]
    y1_done = None
    y1tmp_free = None
    hsb_free = None
    sfb_free = None
    for hb in range(16):
        par = hb % 2
        p.wait(nc.gpsimd, wload[par])
        wt = None
        for dch in range(16):
            wt = p.inc(nc.gpsimd.dma_start(
                out=wbuf[:, par * 16384 + dch * 512:par * 16384 + (dch + 1) * 512],
                in_=w1_d[dch * 128:(dch + 1) * 128, hb * 512:(hb + 1) * 512]),
                s_w[par], 16)
            wt = p.inc(nc.gpsimd.dma_start(
                out=wbuf[:, par * 16384 + 8192 + dch * 512:par * 16384 + 8192 + (dch + 1) * 512],
                in_=w3_d[dch * 128:(dch + 1) * 128, hb * 512:(hb + 1) * 512]),
                s_w[par], 16)
        p.wait(nc.tensor, wt)
        lastcons = None
        for hh in range(4):
            for tb in range(NTB):
                b_up, b_li = (hh * 2 + tb) % 4, 4 + (hh * 2 + tb) % 4
                p.wait(nc.tensor, ps_free[b_up])
                p.wait(nc.tensor, ps_free[b_li])
                for dch in range(16):
                    T.matmul(PS[b_up][:],
                             wbuf[:, par * 16384 + dch * 512 + hh * 128:par * 16384 + dch * 512 + (hh + 1) * 128],
                             x1n[:, dch * ttok + tb * 512:dch * ttok + (tb + 1) * 512],
                             start=(dch == 0), stop=(dch == 15))
                last = None
                for dch in range(16):
                    last = T.matmul(PS[b_li][:],
                                    wbuf[:, par * 16384 + 8192 + dch * 512 + hh * 128:par * 16384 + 8192 + dch * 512 + (hh + 1) * 128],
                                    x1n[:, dch * ttok + tb * 512:dch * ttok + (tb + 1) * 512],
                                    start=(dch == 0), stop=(dch == 15))
                tmm = p.inc(last, s_pe)
                lastcons = tmm
                p.wait(nc.scalar, tmm)
                p.wait(nc.scalar, sfb_free)
                sfe = p.inc(SC.activation(sfb[:], PS[b_up][:], AF.Silu), s_ac)
                p.wait(nc.vector, sfe)
                p.wait(nc.vector, hsb_free)
                hbe = p.fence(nc.vector, V.tensor_mul(hsb[:], sfb[:], PS[b_li][:]), s_ev)
                sfb_free = hbe
                ps_free[b_up] = [sfe, hbe]
                ps_free[b_li] = hbe
                p.wait(nc.tensor, hbe)
                bank1 = 6 + (hh + tb) % 2
                p.wait(nc.tensor, ps_free[bank1])
                ts1 = p.inc(T.matmul(PS[bank1][:], hblk[:], hsb[:],
                                     start=True, stop=True), s_pe)
                hsb_free = ts1
                p.wait(nc.vector, ts1)
                p.wait(nc.vector, y1tmp_free)
                y1e = p.inc(V.tensor_copy(y1tmp[:], PS[bank1][:]), s_ev)
                ps_free[bank1] = y1e
                nc.sync.wait_ge(*y1e)
                y1_done = p.inc(nc.sync.dma_start(
                    out=y1_d[hb * 512 + hh * 128:hb * 512 + (hh + 1) * 128,
                             tb * 512:(tb + 1) * 512],
                    in_=y1tmp[:]), s_y1, 16)
                y1tmp_free = y1_done
        wload[par] = lastcons

    pp_par_cons = [None, None]
    prev_b_done = None
    out_free = [None, None]
    for tsub in range(NSUB):
        yld = None
        p.wait(nc.gpsimd, wload[0])
        p.wait(nc.gpsimd, wload[1])
        p.wait(nc.gpsimd, y1_done)
        p.wait(nc.gpsimd, prev_b_done if tsub > 0 else None)
        for hc in range(64):
            yld = p.inc(nc.gpsimd.dma_start(
                out=wbuf[:, hc * 512:(hc + 1) * 512],
                in_=y1_d[hc * 128:(hc + 1) * 128, tsub * 512:(tsub + 1) * 512]),
                s_x, 16)
        p.wait(nc.tensor, yld)
        fev = None
        for opass in range(2):
            for b8 in range(8):
                p.wait(nc.tensor, ps_free[b8])
            last = None
            for hc in range(64):
                par = hc % 2
                p.wait(nc.gpsimd, pp_par_cons[par])
                if tsub == 0 and opass == 0:
                    p.wait(nc.gpsimd, pp_done)
                wld2 = p.inc(nc.gpsimd.dma_start(
                    out=wsl[par][:],
                    in_=w2pp_d[hc * 128:(hc + 1) * 128, :]), s_w[par], 16)
                p.wait(nc.tensor, wld2)
                for b8 in range(8):
                    oc = opass * 8 + b8
                    last = T.matmul(PS[b8][:],
                                    wsl[par][:, oc * 128:(oc + 1) * 128],
                                    wbuf[:, hc * 512:(hc + 1) * 512],
                                    start=(hc == 0), stop=(hc == 63))
                tfm = p.inc(last, s_pe)
                pp_par_cons[par] = tfm
            p.wait(nc.vector, tfm)
            for b8 in range(8):
                oc = opass * 8 + b8
                fev = p.inc(V.tensor_copy(fb[:, oc * 512:(oc + 1) * 512], PS[b8][:]),
                            s_ev)
                ps_free[b8] = fev
        sq = None
        p.wait(nc.vector, fev)
        for dch in range(16):
            sq = V.tensor_mul(mix[:, dch * 512:(dch + 1) * 512],
                              fb[:, dch * 512:(dch + 1) * 512],
                              fb[:, dch * 512:(dch + 1) * 512])
        sq = p.inc(sq, s_ev)
        p.wait(nc.tensor, sq)
        p.wait(nc.tensor, ps_free[0])
        p.wait(nc.tensor, ps_free[4])
        last = None
        for dch in range(16):
            T.matmul(PS[0][0:1, :], onesb[:, 0:1],
                     fb[:, dch * 512:(dch + 1) * 512], start=(dch == 0), stop=(dch == 15))
        for dch in range(16):
            last = T.matmul(PS[4][0:1, :], onesb[:, 0:1],
                            mix[:, dch * 512:(dch + 1) * 512],
                            start=(dch == 0), stop=(dch == 15))
        tst = p.inc(last, s_pe)
        p.wait(nc.scalar, tst)
        a1 = p.fence(nc.scalar, SC.mul(rows[0:1, 0:512], PS[0][0:1, :], -1.0 / DIM),
                     s_ac)
        a1 = p.fence(nc.scalar, SC.mul(rows[0:1, 512:1024], PS[4][0:1, :], 1.0 / DIM),
                     s_ac)
        ps_free[0] = a1
        ps_free[4] = a1
        p.wait(nc.vector, a1)
        v1 = p.fence(nc.vector, V.tensor_mul(rows[0:1, 1024:1536], rows[0:1, 0:512],
                                             rows[0:1, 0:512]), s_ev)
        v1 = p.fence(nc.vector, V.tensor_sub(rows[0:1, 512:1024], rows[0:1, 512:1024],
                                             rows[0:1, 1024:1536]), s_ev)
        p.wait(nc.scalar, v1)
        a2 = p.fence(nc.scalar, SC.activation(rows[0:1, 512:1024], rows[0:1, 512:1024],
                                              AF.Ln, bias=cst[0:1, 0:1]), s_ac)
        a2 = p.fence(nc.scalar, SC.activation(rows[0:1, 512:1024], rows[0:1, 512:1024],
                                              AF.Exp, scale=-0.5), s_ac)
        p.wait(nc.vector, a2)
        c1 = p.fence(nc.vector, V.tensor_copy(bcA[0:1, 0:512], rows[0:1, 0:512]), s_ev)
        c1 = p.fence(nc.vector, V.tensor_copy(bcB[0:1, 0:512], rows[0:1, 512:1024]),
                     s_ev)
        p.wait(nc.tensor, [ps_free[0], ps_free[1], c1])
        t1 = p.inc(T.matmul(PS[0][:], onesrow_b[:], bcA[0:1, 0:512],
                            start=True, stop=True), s_pe)
        t1 = p.inc(T.matmul(PS[1][:], onesrow_b[:], bcB[0:1, 0:512],
                            start=True, stop=True), s_pe)
        p.wait(nc.vector, t1)
        e1 = p.inc(V.tensor_copy(bcA[:, 0:512], PS[0][:]), s_ev)
        e1 = p.inc(V.tensor_copy(bcB[:, 0:512], PS[1][:]), s_ev)
        ps_free[0] = e1
        ps_free[1] = e1
        ot = None
        fin = None
        p.wait(nc.vector, e1)
        for dch in range(16):
            opar = dch % 2
            ot = p.fence(nc.vector, V.tensor_add(
                fb[:, dch * 512:(dch + 1) * 512], fb[:, dch * 512:(dch + 1) * 512],
                bcA[:, 0:512]), s_ev)
            ot = p.fence(nc.vector, V.tensor_mul(
                fb[:, dch * 512:(dch + 1) * 512], fb[:, dch * 512:(dch + 1) * 512],
                bcB[:, 0:512]), s_ev)
            p.wait(nc.vector, out_free[opar])
            ot = p.fence(nc.vector, V.tensor_add(
                outf[:, opar * 512:(opar + 1) * 512], fb[:, dch * 512:(dch + 1) * 512],
                x1b[:, dch * ttok + tsub * 512:dch * ttok + (tsub + 1) * 512]), s_ev)
            nc.sync.wait_ge(*ot)
            fin = p.inc(nc.sync.dma_start(
                out=out_d[dch * 128:(dch + 1) * 128, tsub * 512:(tsub + 1) * 512],
                in_=outf[:, opar * 512:(opar + 1) * 512]), s_out[opar], 16)
            out_free[opar] = fin
        prev_b_done = fin

    for s in s_out:
        nc.sync.wait_ge(s, p.val(s))
    p.close()
    return p.nc


def _hadamard(n):
    h = np.array([[1.0]])
    while h.shape[0] < n:
        h = np.block([[h, h], [h, -h]])
    return h


def _tail_inputs(st3mix, inp, XT, chunk, ttok=1024):
    c0 = chunk * ttok
    hb = np.zeros((128, 128), np.float64)
    h64 = _hadamard(64)
    hb[0:64, 0:64] = h64
    hb[64:128, 64:128] = h64
    return {
        "mixc": np.ascontiguousarray(st3mix[:, c0:c0 + ttok]),
        "x0t": np.ascontiguousarray(XT[:, c0:c0 + ttok]).astype(np.float32),
        "wo": np.ascontiguousarray(inp["L0_attn_Wo"]).astype(np.float32),
        "w1t": np.ascontiguousarray(inp["L0_ffn_W1"].T).astype(np.float32),
        "w3t": np.ascontiguousarray(inp["L0_ffn_W3"].T).astype(np.float32),
        "w2t": np.ascontiguousarray(inp["L0_ffn_W2"].T).astype(np.float32),
        "hblk": (hb / np.sqrt(2.0)).astype(BF),
        "h128": (_hadamard(128) / 64.0).astype(BF),
    }


def run_tail_stage(st3mix, inp, XT):
    nc = build_tail()
    in_maps = [_tail_inputs(st3mix, inp, XT, c) for c in range(8)]
    res = run_bass_kernel_spmd(nc, in_maps, core_ids=list(range(8)))
    return np.concatenate([res.results[c]["outt"] for c in range(8)], axis=1)


# version 03
# speedup vs baseline: 1.2320x; 1.2320x over previous
import numpy as np
import ml_dtypes

import concourse.bass as bass
import concourse.mybir as mybir
from concourse.bass_utils import run_bass_kernel_spmd

F32 = mybir.dt.float32
BF16 = mybir.dt.bfloat16
AF = mybir.ActivationFunctionType
ALU = mybir.AluOpType
AX = mybir.AxisListType
BF = ml_dtypes.bfloat16

DIM = 2048
SEQ = 8192
NH = 4
HD = 512
DH = 512
NAH = 16
AHD = 128
CHUNK = 1024
NCHUNK = SEQ // CHUNK
WIN = 1024
FFN_H = 8192
BASE_LR = 0.01
BASE_LR_INV = float(BASE_LR + np.log(-np.expm1(-BASE_LR)))
NS_COEFFS = [(4.0848, -6.8946, 2.927), (3.9505, -6.3029, 2.6377),
             (3.7418, -5.5913, 2.3037), (2.8769, -3.1427, 1.2046),
             (2.8366, -3.0525, 1.2012)]
NEG_BIG = -1e30


class P:

    def __init__(self, name):
        self.nc = bass.Bass("TRN2", target_bir_lowering=False)
        self.name = name
        self._sem_ct = {}
        self._ctx = []

    def dram(self, name, shape, dtype, kind="Internal"):
        return self.nc.dram_tensor(name, list(shape), dtype, kind=kind).ap()

    def sb(self, name, shape, dtype):
        t = self.nc.sbuf_tensor(name, list(shape), dtype)
        h = t.__enter__()
        self._ctx.append(t)
        return h

    def ps(self, name, shape, dtype=F32):
        t = self.nc.psum_tensor(name, list(shape), dtype)
        h = t.__enter__()
        self._ctx.append(t)
        return h

    def sem(self, name):
        t = self.nc.semaphore(name)
        h = t.__enter__()
        self._ctx.append(t)
        self._sem_ct[h] = 0
        return h

    def close(self):
        for t in reversed(self._ctx):
            t.__exit__(None, None, None)
        self._ctx = []

    def inc(self, instr, sem, n=1):
        instr.then_inc(sem, n)
        self._sem_ct[sem] += n
        return (sem, self._sem_ct[sem])

    def wait(self, eng, tok):
        if tok is None:
            return
        if isinstance(tok, list):
            for t in tok:
                self.wait(eng, t)
            return
        sem, val = tok
        if val > 0:
            eng.wait_ge(sem, val)

    def val(self, sem):
        return self._sem_ct[sem]

    def fence(self, eng, instr, sem):
        tok = self.inc(instr, sem)
        self.wait(eng, tok)
        return tok


def cdiv(a, b):
    return (a + b - 1) // b



AUGW = 6144 + 128
N_OC = AUGW // 128
TBLK = 512


def build_qkv(ttok):
    p = P("qkv")
    nc = p.nc
    xt_d = p.dram("xt", [DIM, ttok], F32, kind="ExternalInput")
    waug_d = p.dram("waug", [DIM, AUGW], F32, kind="ExternalInput")
    qkp_d = p.dram("qkp", [128, 64], F32, kind="ExternalInput")
    vbias_d = p.dram("vbias", [128, 1], F32, kind="ExternalInput")
    qkv_o = p.dram("qkv", [3 * DIM, ttok], BF16, kind="ExternalOutput")
    aqk_o = p.dram("aqk", [2 * DIM, ttok], BF16, kind="ExternalOutput")
    av_o = p.dram("av", [DIM, ttok], BF16, kind="ExternalOutput")
    aux_o = p.dram("aux", [128, ttok], F32, kind="ExternalOutput")

    nblk = ttok // TBLK
    SEGS = [(0, 16), (16, 32), (32, N_OC)]
    MAXSEG = max(e - s for s, e in SEGS)

    xb = [p.sb(f"xb{i}", [128, 16 * TBLK], BF16) for i in range(nblk)]
    xsq = p.sb("xsq", [128, 16 * TBLK], BF16)
    wb = [p.sb(f"wb{i}", [128, 16 * 128 * MAXSEG], BF16) for i in range(1)]
    onesb = p.sb("onesb", [128, 1], BF16)
    onesrow_f = p.sb("onesrow_f", [1, 128], F32)
    cstt = p.sb("cstt", [128, 3], F32)
    qkp = p.sb("qkp_sb", [128, 64], F32)
    vbias = p.sb("vbias_sb", [128, 1], F32)
    colsumT = p.sb("colsumT", [1, 128 * MAXSEG], F32)
    stat = p.sb("stat", [1, 4 * TBLK], F32)
    negm = p.sb("negm", [1, nblk * TBLK], F32)
    rstdb = [p.sb(f"rstdb{i}", [128, TBLK], F32) for i in range(nblk)]
    qf = [p.sb(f"qf{i}", [128, TBLK], F32) for i in range(2)]
    ob = [p.sb(f"ob{i}", [128, TBLK], BF16) for i in range(4)]
    auxb = p.sb("auxb", [128, TBLK], F32)

    mm_ps = [p.ps(f"mm{i}", [128, TBLK], F32) for i in range(4)]
    st_ps = p.ps("st", [1, 2 * TBLK], F32)
    cs_ps = p.ps("cs", [1, 512], F32)
    bc_ps = p.ps("bc", [128, TBLK], F32)

    s_w = p.sem("s_w")
    s_prm = p.sem("s_prm")
    s_x = [p.sem(f"s_x{i}") for i in range(nblk)]
    s_pe = p.sem("s_pe")
    s_ev = p.sem("s_ev")
    s_ac = p.sem("s_ac")
    s_ob = [p.sem(f"s_ob{i}") for i in range(4)]
    s_aux = p.sem("s_aux")
    s_qo = p.sem("s_qo")

    nc.vector.memset(onesb[:], 1.0)
    nc.vector.memset(onesrow_f[:], 1.0)
    nc.vector.memset(cstt[:, 0:1], 1e-5)
    nc.vector.memset(cstt[:, 1:2], -0.5)
    nc.vector.memset(cstt[:, 2:3], BASE_LR_INV)
    consts_rdy = p.inc(nc.vector.memset(auxb[:], 0.0), s_ev)

    prm = p.inc(nc.gpsimd.dma_start(out=qkp[:], in_=qkp_d), s_prm, 16)
    prm = p.inc(nc.gpsimd.dma_start(out=vbias[:], in_=vbias_d), s_prm, 16)
    ld = [None] * nblk
    for b in range(nblk):
        for d in range(16):
            ld[b] = p.inc(nc.gpsimd.dma_start(
                out=xb[b][:, d * TBLK:(d + 1) * TBLK],
                in_=xt_d[d * 128:(d + 1) * 128, b * TBLK:(b + 1) * TBLK]), s_x[b], 16)

    NM, MSQ, VAR, RSTD = 0, 1, 2, 3

    def srow(i):
        return stat[0:1, i * TBLK:(i + 1) * TBLK]

    rstd_rdy = [None] * nblk
    stps_free = None
    bcps_free = None
    for b in range(nblk):
        p.wait(nc.vector, ld[b])
        sq = None
        for d in range(16):
            sq = nc.vector.tensor_mul(xsq[:, d * TBLK:(d + 1) * TBLK],
                                      xb[b][:, d * TBLK:(d + 1) * TBLK],
                                      xb[b][:, d * TBLK:(d + 1) * TBLK])
        sq = p.inc(sq, s_ev)
        p.wait(nc.tensor, sq)
        p.wait(nc.tensor, stps_free)
        for d in range(16):
            nc.tensor.matmul(st_ps[:, 0:TBLK], onesb[:, 0:1],
                             xb[b][:, d * TBLK:(d + 1) * TBLK],
                             start=(d == 0), stop=(d == 15))
        last = None
        for d in range(16):
            last = nc.tensor.matmul(st_ps[:, TBLK:2 * TBLK], onesb[:, 0:1],
                                    xsq[:, d * TBLK:(d + 1) * TBLK],
                                    start=(d == 0), stop=(d == 15))
        t_st = p.inc(last, s_pe)

        p.wait(nc.scalar, t_st)
        nc.scalar.mul(negm[0:1, b * TBLK:(b + 1) * TBLK], st_ps[:, 0:TBLK],
                      -1.0 / DIM)
        a1 = p.inc(nc.scalar.mul(srow(MSQ), st_ps[:, TBLK:2 * TBLK], 1.0 / DIM),
                   s_ac)
        stps_free = a1
        p.wait(nc.vector, a1)
        p.wait(nc.vector, consts_rdy)
        p.fence(nc.vector, nc.vector.tensor_mul(
            srow(VAR), negm[0:1, b * TBLK:(b + 1) * TBLK],
            negm[0:1, b * TBLK:(b + 1) * TBLK]), s_ev)
        v1 = p.inc(nc.vector.tensor_sub(srow(VAR), srow(MSQ), srow(VAR)), s_ev)
        p.wait(nc.scalar, v1)
        p.fence(nc.scalar, nc.scalar.activation(srow(RSTD), srow(VAR), AF.Ln,
                                                bias=cstt[0:1, 0:1]), s_ac)
        a2 = p.inc(nc.scalar.activation(srow(RSTD), srow(RSTD), AF.Exp,
                                        scale=cstt[0:1, 1:2]), s_ac)
        p.wait(nc.tensor, a2)
        p.wait(nc.tensor, bcps_free)
        t_bc = p.inc(nc.tensor.matmul(bc_ps[:, 0:TBLK], onesrow_f[:], srow(RSTD),
                                      start=True, stop=True), s_pe)
        p.wait(nc.scalar, t_bc)
        rstd_rdy[b] = p.inc(nc.scalar.copy(rstdb[b][:], bc_ps[:, 0:TBLK]), s_ac)
        bcps_free = rstd_rdy[b]

    p.wait(nc.vector, prm)
    p.wait(nc.scalar, prm)

    bank_free = [None] * 4
    ob_free = [None] * 4
    qf_free = [None, None]
    aux_free = None
    wseg_free = None
    cs_free = None
    for (js, je) in SEGS:
        segw = (je - js) * 128
        p.wait(nc.gpsimd, wseg_free)
        w_rdy = None
        for d in range(16):
            w_rdy = p.inc(nc.gpsimd.dma_start(
                out=wb[0][:, d * segw:(d + 1) * segw],
                in_=waug_d[d * 128:(d + 1) * 128, js * 128:je * 128]), s_w, 16)

        p.wait(nc.tensor, w_rdy)
        p.wait(nc.tensor, cs_free)
        cs_rdy = None
        for seg in range(cdiv(segw, 512)):
            n0, n1 = seg * 512, min(segw, (seg + 1) * 512)
            p.wait(nc.tensor, cs_rdy)
            last = None
            for d in range(16):
                last = nc.tensor.matmul(cs_ps[:, 0:n1 - n0], onesb[:, 0:1],
                                        wb[0][:, d * segw + n0:d * segw + n1],
                                        start=(d == 0), stop=(d == 15))
            t = p.inc(last, s_pe)
            p.wait(nc.scalar, t)
            cs_rdy = p.inc(nc.scalar.copy(colsumT[0:1, n0:n1],
                                          cs_ps[:, 0:n1 - n0]), s_ac)

        last_mm = None
        for b in range(nblk):
            p.wait(nc.tensor, ld[b])
            p.wait(nc.tensor, cs_rdy)
            for j in range(js, je):
                jj = j - js
                bank = mm_ps[jj % 4]
                p.wait(nc.tensor, bank_free[jj % 4])
                for d in range(16):
                    nc.tensor.matmul(bank[:],
                                     wb[0][:, d * segw + jj * 128:d * segw + (jj + 1) * 128],
                                     xb[b][:, d * TBLK:(d + 1) * TBLK],
                                     start=(d == 0), stop=False)
                t_mm = p.inc(nc.tensor.matmul(
                    bank[:], colsumT[0:1, jj * 128:(jj + 1) * 128],
                    negm[0:1, b * TBLK:(b + 1) * TBLK], start=False, stop=True), s_pe)
                last_mm = t_mm

                f = qf[jj % 2]
                p.wait(nc.vector, t_mm)
                p.wait(nc.vector, rstd_rdy[b])
                p.wait(nc.vector, qf_free[jj % 2])
                v_ev = p.inc(nc.vector.tensor_mul(f[:], bank[:], rstdb[b][:]), s_ev)
                bank_free[jj % 4] = v_ev

                if j < 48:
                    o0i, o1i = (2 * jj) % 4, (2 * jj + 1) % 4
                    o0, o1 = ob[o0i], ob[o1i]
                    p.wait(nc.scalar, v_ev)
                    p.wait(nc.scalar, ob_free[o0i])
                    a_ev = p.inc(nc.scalar.activation(o0[:], f[:], AF.Silu), s_ac)
                    qf_free[jj % 2] = a_ev
                    nc.sync.wait_ge(*a_ev)
                    ob_free[o0i] = p.inc(nc.sync.dma_start(
                        out=qkv_o[j * 128:(j + 1) * 128, b * TBLK:(b + 1) * TBLK],
                        in_=o0[:]), s_ob[o0i], 16)
                    if j < 32:
                        qs = qkp[:, (2 * j):(2 * j) + 1]
                        qo = qkp[:, (2 * j) + 1:(2 * j) + 2]
                        p.wait(nc.vector, ob_free[o1i])
                        p.wait(nc.vector, v_ev)
                        e2 = p.inc(nc.vector.tensor_scalar(o1[:], f[:], qs, qo,
                                                           ALU.mult, ALU.add), s_ev)
                        nc.sync.wait_ge(*e2)
                        ob_free[o1i] = p.inc(nc.sync.dma_start(
                            out=aqk_o[j * 128:(j + 1) * 128,
                                      b * TBLK:(b + 1) * TBLK],
                            in_=o1[:]), s_ob[o1i], 16)
                    else:
                        p.wait(nc.vector, ob_free[o1i])
                        p.wait(nc.vector, v_ev)
                        e2 = p.inc(nc.vector.tensor_copy(o1[:], f[:]), s_ev)
                        nc.sync.wait_ge(*e2)
                        ob_free[o1i] = p.inc(nc.sync.dma_start(
                            out=av_o[(j - 32) * 128:(j - 31) * 128,
                                     b * TBLK:(b + 1) * TBLK],
                            in_=o1[:]), s_ob[o1i], 16)
                else:
                    p.wait(nc.scalar, v_ev)
                    p.wait(nc.scalar, aux_free)
                    p.wait(nc.scalar, consts_rdy)
                    p.fence(nc.scalar, nc.scalar.activation(
                        auxb[0:12, :], f[0:12, :], AF.Exp,
                        bias=cstt[0:12, 2:3]), s_ac)
                    nc.scalar.activation(auxb[0:12, :], auxb[0:12, :], AF.Ln,
                                         bias=1.0)
                    nc.scalar.activation(auxb[32:36, :], f[32:36, :], AF.Sigmoid,
                                         bias=vbias[32:36, 0:1])
                    a_ev = p.inc(nc.scalar.activation(
                        auxb[64:68, :], f[64:68, :], AF.Silu,
                        bias=vbias[64:68, 0:1]), s_ac)
                    qf_free[jj % 2] = a_ev
                    nc.sync.wait_ge(*a_ev)
                    aux_free = p.inc(nc.sync.dma_start(
                        out=aux_o[:, b * TBLK:(b + 1) * TBLK], in_=auxb[:]),
                        s_aux, 16)
        wseg_free = last_mm
        cs_free = last_mm

    for s in [s_qo, s_aux] + s_ob:
        if p.val(s):
            nc.sync.wait_ge(s, p.val(s))
    p.close()
    return p.nc



def _waug(Wqkv, Wlr, Wm, Ws):
    aug = np.zeros((DIM, 128), np.float32)
    aug[:, 0:12] = Wlr
    aug[:, 32:36] = Wm
    aug[:, 64:68] = Ws
    return np.ascontiguousarray(np.concatenate([Wqkv, aug], axis=1)).astype(np.float32)


def _qkp_pack(qk_scale, qk_offset):
    out = np.zeros((128, 64), np.float32)
    for c in range(16):
        out[:, 2 * c] = qk_scale[c * 128:(c + 1) * 128, 0]
        out[:, 2 * c + 1] = qk_offset[c * 128:(c + 1) * 128, 0]
    for c in range(16):
        out[:, 32 + 2 * c] = qk_scale[c * 128:(c + 1) * 128, 1]
        out[:, 32 + 2 * c + 1] = qk_offset[c * 128:(c + 1) * 128, 1]
    return out


def _vbias_pack(bm, bs):
    v = np.zeros((128, 1), np.float32)
    v[32:36, 0] = bm
    v[64:68, 0] = bs
    return v


def run_qkv_stage(inp):
    XT = np.ascontiguousarray(inp["L0_attn_x"][0].T).astype(np.float32)
    WAUG = _waug(inp["L0_attn_Wqkv"], inp["L0_attn_Wlr"],
                 inp["L0_attn_Wm"], inp["L0_attn_Ws"])
    QKP = _qkp_pack(inp["L0_attn_qk_scale"], inp["L0_attn_qk_offset"])
    VB = _vbias_pack(inp["L0_attn_bm"], inp["L0_attn_bs"])
    ttok = SEQ // 8
    nc = build_qkv(ttok)
    in_maps = [{
        "xt": np.ascontiguousarray(XT[:, c * ttok:(c + 1) * ttok]),
        "waug": WAUG, "qkp": QKP, "vbias": VB,
    } for c in range(8)]
    res = run_bass_kernel_spmd(nc, in_maps, core_ids=list(range(8)))
    return {k: np.concatenate([res.results[c][k] for c in range(8)], axis=1)
            for k in ["qkv", "aqk", "av", "aux"]}


def kernel(**inputs):
    inputs = {k: np.asarray(v) for k, v in inputs.items()}
    st1 = run_qkv_stage(inputs)
    st2 = run_chain_stage(st1, inputs)
    st3 = run_attn_stage(st1, st2)
    XT = np.ascontiguousarray(inputs["L0_attn_x"][0].T).astype(np.float32)
    outT = run_tail_stage(st3, inputs, XT)
    return np.ascontiguousarray(outT.T).astype(np.float32)[None]



def build_chain(nchunks=NCHUNK, phases=99):
    import os
    phases = int(os.environ.get("CHAIN_PHASES", phases))
    p = P("chain")
    nc = p.nc
    qh_d = p.dram("qh", [HD, SEQ], BF16, kind="ExternalInput")
    kh_d = p.dram("kh", [HD, SEQ], BF16, kind="ExternalInput")
    vh_d = p.dram("vh", [HD, SEQ], BF16, kind="ExternalInput")
    lrtm_d = p.dram("lrtm", [128, NCHUNK * 32], F32, kind="ExternalInput")
    mom_d = p.dram("momh", [1, SEQ], F32, kind="ExternalInput")
    scl_d = p.dram("sclh", [1, SEQ], F32, kind="ExternalInput")
    w0_d = p.dram("w0", [DH, HD], F32, kind="ExternalInput")
    w1_d = p.dram("w1", [HD, DH], F32, kind="ExternalInput")
    w2_d = p.dram("w2", [DH, HD], F32, kind="ExternalInput")
    gcol_d = p.dram("gcolh", [128, 4], F32, kind="ExternalInput")
    idt_d = p.dram("idt", [128, 128], BF16, kind="ExternalInput")
    diag_d = p.dram("diag", [128, 5 * 128], BF16, kind="ExternalInput")
    ttt_o = p.dram("ttt", [HD, SEQ], BF16, kind="ExternalOutput")

    C = CHUNK
    NT = C // 128

    qn = p.sb("qn", [128, 4 * C], BF16)
    kn = p.sb("kn", [128, 4 * C], BF16)
    vv = p.sb("vv", [128, 4 * C], BF16)
    kTM = p.sb("kTM", [128, NT * 512], BF16)
    vTM = p.sb("vTM", [128, NT * 512], BF16)
    momr = p.sb("momr", [1, C], BF16)
    sclr = p.sb("sclr", [1, C], F32)
    lrcf = [p.sb(f"lrcf{i}", [128, NT * 4], F32) for i in range(2)]
    mmc = p.sb("mmc", [128, 1], F32)
    rnc = p.sb("rnc", [128, 1], F32)
    W = [p.sb(f"W{m}", [128, 4 * 512], F32) for m in range(3)]
    Mb = [p.sb(f"Mb{m}", [128, 4 * 512], BF16) for m in range(3)]
    Wb = [p.sb(f"Wb{m}", [128, 4 * 512], BF16) for m in range(3)]
    WTb = [p.sb(f"WTb{m}", [128, 4 * 512], BF16) for m in range(3)]
    ncol = p.sb("ncol", [128, 12], F32)
    rcol = p.sb("rcol", [128, 8], F32)
    dwf = p.sb("dwf", [128, 4 * 512], F32)
    X5f = p.sb("X5f", [128, 4 * 512], F32)
    Xb = [p.sb(f"Xb{i}", [128, 4 * 512], BF16) for i in range(2)]
    XTb = [p.sb(f"XTb{i}", [128, 4 * 512], BF16) for i in range(2)]
    Ab = p.sb("Ab", [128, 4 * 512], BF16)
    Acb = p.sb("Acb", [128, 4 * 512], BF16)
    hidT = p.sb("hidT", [128, 4 * C], BF16)
    gateT = p.sb("gateT", [128, 4 * C], BF16)
    outb = p.sb("outb", [128, 4 * C], BF16)
    onesrow_b = p.sb("onesrow_b", [1, 128], BF16)
    dgb = gateT
    dhb = hidT
    hl1 = outb
    klb = [p.sb(f"klb{i}", [128, 512], BF16) for i in range(2)]
    gsc = [p.sb(f"gsc{i}", [128, 1024], F32) for i in range(5)]
    rowA = p.sb("rowA", [1, C], F32)
    rowB = p.sb("rowB", [1, C], F32)
    rqb = p.sb("rqb", [128, C], BF16)
    rsgb = p.sb("rsgb", [128, C], BF16)
    gcol = p.sb("gcol", [128, 4], F32)
    onesb = p.sb("onesb", [128, 1], BF16)
    onesf = p.sb("onesf", [128, 1], F32)
    onesrow_f = p.sb("onesrow_f", [1, 128], F32)
    idt = p.sb("idt_sb", [128, 128], BF16)
    diag = p.sb("diag_sb", [128, 5 * 128], BF16)
    cst = p.sb("cst", [128, 4], F32)
    srow = p.sb("srow_sb", [1, 8], F32)

    PSA = p.ps("PSA", [128, 2048], F32)
    PS = [PSA[:, j * 512:(j + 1) * 512] for j in range(4)]
    PS4 = p.ps("PS4", [128, 512], F32)
    PS5 = p.ps("PS5", [128, 512], F32)
    PT = [p.ps(f"PTb{i}", [128, 1024], BF16) for i in range(2)]

    s_in = [p.sem(f"s_in{i}") for i in range(2)]
    s_w = p.sem("s_wl")
    s_pe = p.sem("s_pel")
    s_ev = p.sem("s_evl")
    s_ac = p.sem("s_acl")
    s_out = p.sem("s_outl")

    V, SC, T = nc.vector, nc.scalar, nc.tensor

    ps_free = [None] * 4
    ps4_free = None
    pt_free = [[None] * 8, [None] * 8]
    pt_ctr = [0]

    def bcast_col(row_ap, out_col):
        nonlocal ps4_free
        p.wait(nc.tensor, ps4_free)
        t = p.inc(T.matmul(PS4[:, 0:1], onesrow_f[:], row_ap, start=True, stop=True),
                  s_pe)
        p.wait(nc.scalar, t)
        tok = p.inc(SC.copy(out_col, PS4[:, 0:1]), s_ac)
        ps4_free = tok
        return tok

    def bcast_row(row_f32, out_tile, n, rdy=None):
        nonlocal ps_free
        toks = []
        p.wait(nc.tensor, rdy)
        for h in range(cdiv(n, 512)):
            n0, n1 = h * 512, min(n, (h + 1) * 512)
            p.wait(nc.tensor, ps_free[h])
            t = p.inc(T.matmul(PS[h][0:128, 0:n1 - n0], onesrow_f[:],
                               row_f32[0:1, n0:n1], start=True, stop=True), s_pe)
            p.wait(nc.vector, t)
            tok = p.inc(V.tensor_copy(out_tile[:, n0:n1], PS[h][0:128, 0:n1 - n0]), s_ev)
            ps_free[h] = tok
            toks.append(tok)
        return toks[-1]

    pt_bank_free = [None, None]

    def transpose_batch(pairs):
        tok = None
        bank = pt_ctr[0] % 2
        i = 0
        while i < len(pairs):
            grp = pairs[i:i + 8]
            i += 8
            pt_ctr[0] += 1
            bank = 1 - bank
            p.wait(nc.tensor, pt_bank_free[bank])
            tt = None
            for r, (s_ap, d_ap, npart) in enumerate(grp):
                tt = p.inc(T.transpose(
                    PT[bank][:, r * 128:r * 128 + s_ap.shape[0]], s_ap,
                    idt[0:npart, 0:npart] if npart < 128 else idt[:]), s_pe)
            p.wait(nc.vector, tt)
            for r, (s_ap, d_ap, npart) in enumerate(grp):
                tok = p.inc(V.tensor_copy(
                    d_ap, PT[bank][:, r * 128:r * 128 + s_ap.shape[0]]), s_ev)
            pt_bank_free[bank] = tok
        return tok

    def transpose512(src_bf, dst_bf, rdy_tok):
        p.wait(nc.tensor, rdy_tok)
        pairs = []
        for j in range(4):
            for i in range(4):
                pairs.append((src_bf[:, j * 512 + i * 128:j * 512 + (i + 1) * 128],
                              dst_bf[:, i * 512 + j * 128:i * 512 + (j + 1) * 128],
                              128))
        return transpose_batch(pairs)

    nc.vector.memset(cst[:, 0:1], 1e-5)
    nc.vector.memset(cst[:, 1:2], -0.5)
    nc.vector.memset(cst[:, 2:3], 1e-7)
    nc.vector.memset(onesb[:], 1.0)
    nc.vector.memset(onesf[:], 1.0)
    nc.vector.memset(onesrow_f[:], 1.0)
    nc.vector.memset(onesrow_b[:], 1.0)
    for m in range(3):
        nc.vector.memset(Mb[m][:], 0.0)
    ini = p.inc(nc.vector.memset(outb[:], 0.0), s_ev)

    wl = None
    for m, wd in enumerate([w0_d, w1_d, w2_d]):
        for j in range(4):
            wl = p.inc(nc.gpsimd.dma_start(
                out=W[m][:, j * 512:(j + 1) * 512],
                in_=wd[j * 128:(j + 1) * 128, :]), s_w, 16)
    wl = p.inc(nc.gpsimd.dma_start(out=idt[:], in_=idt_d), s_w, 16)
    wl = p.inc(nc.gpsimd.dma_start(out=diag[:], in_=diag_d), s_w, 16)
    wl = p.inc(nc.gpsimd.dma_start(out=gcol[:], in_=gcol_d), s_w, 16)

    p.wait(nc.vector, wl)
    p.wait(nc.scalar, wl)
    cast0 = None
    for m in range(3):
        cast0 = p.inc(V.tensor_copy(Wb[m][:], W[m][:]), s_ev)
    nrm = None
    for m in range(3):
        for j in range(4):
            nrm = p.fence(nc.scalar, SC.activation(
                gsc[0][:, 0:512], W[m][:, j * 512:(j + 1) * 512], AF.Square,
                accum_out=ncol[:, 4 * m + j:4 * m + j + 1]), s_ac)
    nrm = p.inc(SC.sqrt(ncol[:, 0:12], ncol[:, 0:12]), s_ac)

    gc = wl

    for m in range(3):
        wt0 = transpose512(Wb[m], WTb[m], cast0)
    prev_wt = wt0

    prev_grad_done = None
    prev_chunk_end = None
    ldq_next = None
    for c in range(nchunks):
        par = c % 2
        sin = s_in[par]
        if c == 0:
            p.wait(nc.gpsimd, None)
            ldq = None
            for oc in range(4):
                for (dst, srcd) in ((qn, qh_d), (kn, kh_d), (vv, vh_d)):
                    ldq = p.inc(nc.gpsimd.dma_start(
                        out=dst[:, oc * C:(oc + 1) * C],
                        in_=srcd[oc * 128:(oc + 1) * 128, 0:C]), sin, 16)
            ldq = p.inc(nc.gpsimd.dma_start(out=lrcf[0][:], in_=lrtm_d[:, 0:32]), sin, 16)
            ldq = p.inc(nc.gpsimd.dma_start(out=momr[:], in_=mom_d[:, 0:C]), sin, 16)
            ldq = p.inc(nc.gpsimd.dma_start(out=sclr[:], in_=scl_d[:, 0:C]), sin, 16)
        else:
            ldq = ldq_next

        for tgt in ((qn, kn) if phases >= 1 else ()):
            p.wait(nc.vector, ldq)
            sq = None
            for oc in range(4):
                sq = V.tensor_mul(hidT[:, oc * C:(oc + 1) * C],
                                  tgt[:, oc * C:(oc + 1) * C],
                                  tgt[:, oc * C:(oc + 1) * C])
            sq = p.inc(sq, s_ev)
            p.wait(nc.tensor, sq)
            p.wait(nc.tensor, ini)
            last = None
            for h in range(2):
                p.wait(nc.tensor, ps_free[h])
                for oc in range(4):
                    last = T.matmul(PS[h][0:1, :], onesb[:, 0:1],
                                    hidT[:, oc * C + h * 512:oc * C + (h + 1) * 512],
                                    start=(oc == 0), stop=(oc == 3))
            tsq = p.inc(last, s_pe)
            p.wait(nc.scalar, tsq)
            a = None
            for h in range(2):
                a = p.fence(nc.scalar, SC.sqrt(rowA[0:1, h * 512:(h + 1) * 512],
                                               PS[h][0:1, :]), s_ac)
                ps_free[h] = a
            p.wait(nc.vector, a)
            rb = p.fence(nc.vector, V.tensor_scalar_add(rowA[0:1, :], rowA[0:1, :],
                                                        1e-5), s_ev)
            rb = p.fence(nc.vector, V.reciprocal(rowB[0:1, :], rowA[0:1, :]), s_ev)
            bt_ = bcast_row(rowB, rqb, C, rb)
            p.wait(nc.vector, bt_)
            rq = None
            for oc in range(4):
                rq = V.tensor_mul(tgt[:, oc * C:(oc + 1) * C],
                                  tgt[:, oc * C:(oc + 1) * C], rqb[:])
            nrm_done = p.inc(rq, s_ev)

        tok = None
        for (src, dst) in (((kn, kTM), (vv, vTM)) if phases >= 2 else ()):
            p.wait(nc.tensor, nrm_done if src is kn else ldq)
            prs = []
            for tc in range(NT):
                for oc in range(4):
                    prs.append((src[:, oc * C + tc * 128:oc * C + (tc + 1) * 128],
                                dst[:, tc * 512 + oc * 128:tc * 512 + (oc + 1) * 128],
                                128))
            tok = transpose_batch(prs)
        ktm_done = tok

        lrc = ldq
        lrcf_t = ldq
        lrcfc = lrcf[par]
        momrc = momr
        sclrc = sclr
        if phases < 3:
            for oc in range(4):
                nc.sync.wait_ge(*(ldq if phases >= 1 else ldq))
                od = p.inc(nc.sync.dma_start(
                    out=ttt_o[oc * 128:(oc + 1) * 128, c * C:(c + 1) * C],
                    in_=qn[:, oc * C:(oc + 1) * C]), s_out, 16)
            continue
        p.wait(nc.vector, ldq)
        mm1 = p.fence(nc.vector, V.reduce_sum(rowA[0:1, 0:1], momrc[:], axis=AX.X), s_ev)
        p.wait(nc.scalar, mm1)
        mm2 = p.inc(SC.mul(srow[0:1, 0:1], rowA[0:1, 0:1], 1.0 / C), s_ac)
        p.wait(nc.tensor, mm2)
        mmr = bcast_col(srow[0:1, 0:1], mmc[:, 0:1])

        if phases < 4:
            for oc in range(4):
                nc.sync.wait_ge(*lrc)
                od = p.inc(nc.sync.dma_start(
                    out=ttt_o[oc * 128:(oc + 1) * 128, c * C:(c + 1) * C],
                    in_=qn[:, oc * C:(oc + 1) * C]), s_out, 16)
            continue
        p.wait(nc.tensor, prev_wt)
        hv = gv = None
        for oc in range(4):
            for half in range(2):
                bank = (oc * 2 + half) % 4
                p.wait(nc.tensor, ps_free[bank])
                last = None
                for dch in range(4):
                    last = T.matmul(PS[bank][:],
                                    WTb[2][:, dch * 512 + oc * 128:dch * 512 + (oc + 1) * 128],
                                    qn[:, dch * C + half * 512:dch * C + (half + 1) * 512],
                                    start=(dch == 0), stop=(dch == 3))
                th = p.inc(last, s_pe)
                p.wait(nc.vector, th)
                hv = p.inc(V.tensor_copy(
                    hidT[:, oc * C + half * 512:oc * C + (half + 1) * 512],
                    PS[bank][:]), s_ev)
                p.wait(nc.tensor, hv)
                last = None
                for dch in range(4):
                    last = T.matmul(PS[bank][:],
                                    WTb[0][:, dch * 512 + oc * 128:dch * 512 + (oc + 1) * 128],
                                    qn[:, dch * C + half * 512:dch * C + (half + 1) * 512],
                                    start=(dch == 0), stop=(dch == 3))
                tg2 = p.inc(last, s_pe)
                p.wait(nc.scalar, tg2)
                gv = p.inc(SC.activation(
                    gateT[:, oc * C + half * 512:oc * C + (half + 1) * 512],
                    PS[bank][:], AF.Silu), s_ac)
                ps_free[bank] = gv
        p.wait(nc.vector, gv)
        p.wait(nc.vector, hv)
        gh = None
        for oc in range(4):
            gh = V.tensor_mul(gateT[:, oc * C:(oc + 1) * C],
                              gateT[:, oc * C:(oc + 1) * C],
                              hidT[:, oc * C:(oc + 1) * C])
        gh = p.inc(gh, s_ev)
        p.wait(nc.tensor, gh)
        ov = None
        for oc in range(4):
            for half in range(2):
                bank = (oc * 2 + half) % 4
                p.wait(nc.tensor, ps_free[bank])
                last = None
                for hch in range(4):
                    last = T.matmul(PS[bank][:],
                                    WTb[1][:, hch * 512 + oc * 128:hch * 512 + (oc + 1) * 128],
                                    gateT[:, hch * C + half * 512:hch * C + (half + 1) * 512],
                                    start=(hch == 0), stop=(hch == 3))
                to = p.inc(last, s_pe)
                last_out_mm = to
                p.wait(nc.vector, to)
                ov = p.inc(V.tensor_copy(
                    outb[:, oc * C + half * 512:oc * C + (half + 1) * 512],
                    PS[bank][:]), s_ev)
                ps_free[bank] = ov

        if phases < 5:
            for oc in range(4):
                nc.sync.wait_ge(*ov)
                od = p.inc(nc.sync.dma_start(
                    out=ttt_o[oc * 128:(oc + 1) * 128, c * C:(c + 1) * C],
                    in_=outb[:, oc * C:(oc + 1) * C]), s_out, 16)
            continue
        p.wait(nc.vector, ov)
        sq = None
        for oc in range(4):
            sq = V.tensor_mul(hidT[:, oc * C:(oc + 1) * C],
                              outb[:, oc * C:(oc + 1) * C],
                              outb[:, oc * C:(oc + 1) * C])
        sq = p.inc(sq, s_ev)
        p.wait(nc.tensor, sq)
        last = None
        for h in range(2):
            p.wait(nc.tensor, ps_free[h])
            for oc in range(4):
                last = T.matmul(PS[h][0:1, :], onesb[:, 0:1],
                                hidT[:, oc * C + h * 512:oc * C + (h + 1) * 512],
                                start=(oc == 0), stop=(oc == 3))
        tms = p.inc(last, s_pe)
        p.wait(nc.scalar, tms)
        a = None
        for h in range(2):
            a = p.fence(nc.scalar, SC.activation(rowA[0:1, h * 512:(h + 1) * 512],
                                                 PS[h][0:1, :], AF.Ln,
                                                 bias=cst[0:1, 0:1], scale=1.0 / HD), s_ac)
            ps_free[h] = a
        a = p.fence(nc.scalar, SC.activation(rowA[0:1, :], rowA[0:1, :], AF.Exp,
                                             scale=cst[0:1, 1:2]), s_ac)
        p.wait(nc.vector, a)
        rs = p.fence(nc.vector, V.tensor_mul(rowB[0:1, :], rowA[0:1, :], sclrc[:]), s_ev)
        bt2 = bcast_row(rowB, rsgb, C, rs)
        p.wait(nc.vector, bt2)
        p.wait(nc.vector, gc)
        fin = None
        for oc in range(4):
            f1 = p.fence(nc.vector, V.tensor_scalar(
                outb[:, oc * C:(oc + 1) * C], outb[:, oc * C:(oc + 1) * C],
                gcol[:, oc:oc + 1], None, ALU.mult), s_ev)
            fin = p.fence(nc.vector, V.tensor_mul(
                outb[:, oc * C:(oc + 1) * C], outb[:, oc * C:(oc + 1) * C],
                rsgb[:]), s_ev)
        for oc in range(4):
            nc.sync.wait_ge(*fin)
            od = p.inc(nc.sync.dma_start(
                out=ttt_o[oc * 128:(oc + 1) * 128, c * C:(c + 1) * C],
                in_=outb[:, oc * C:(oc + 1) * C]), s_out, 16)

        if phases < 6:
            continue
        ge = None
        for tp in range(NT // 2):
            for half in range(2):
                tc = tp * 2 + half
                for bi, wt in ((0, WTb[0]), (1, WTb[2])):
                    p.wait(nc.tensor, ps_free[half * 2 + bi] if half * 2 + bi < 4 else None)
                    last = None
                    for dch in range(4):
                        last = T.matmul(PS[half * 2 + bi][:],
                                        kn[:, dch * C + tc * 128:dch * C + (tc + 1) * 128],
                                        wt[:, dch * 512:(dch + 1) * 512],
                                        start=(dch == 0), stop=(dch == 3))
                    p.inc(last, s_pe)
            p.wait(nc.tensor, ps4_free)
            last2 = None
            for half in range(2):
                tc = tp * 2 + half
                bankap = PS4[:] if half == 0 else PS5[:]
                for och in range(4):
                    last2 = T.matmul(bankap,
                                     vv[:, och * C + tc * 128:och * C + (tc + 1) * 128],
                                     Wb[1][:, och * 512:(och + 1) * 512],
                                     start=(och == 0), stop=(och == 3))
            tgrp = p.inc(last2, s_pe)

            HBM, DH_, S_, SG, HID = 0, 1, 2, 3, 4
            t0 = tp * 2
            p.wait(nc.scalar, tgrp)
            p.wait(nc.scalar, ge)
            sA = p.fence(nc.scalar, SC.activation(gsc[S_][:, 0:512], PS[0][:],
                                                  AF.Sigmoid), s_ac)
            sA = p.fence(nc.scalar, SC.activation(gsc[S_][:, 512:1024], PS[2][:],
                                                  AF.Sigmoid), s_ac)
            p.wait(nc.vector, tgrp)
            p.wait(nc.vector, ge)
            p.wait(nc.vector, sA)
            sG = p.fence(nc.vector, V.tensor_mul(gsc[SG][:, 0:512],
                                                 gsc[S_][:, 0:512], PS[0][:]), s_ev)
            sG = p.fence(nc.vector, V.tensor_mul(gsc[SG][:, 512:1024],
                                                 gsc[S_][:, 512:1024], PS[2][:]), s_ev)
            e2 = p.fence(nc.vector, V.tensor_copy(gsc[HBM][:, 0:512], PS[1][:]), s_ev)
            e2 = p.fence(nc.vector, V.tensor_copy(gsc[HBM][:, 512:1024], PS[3][:]), s_ev)
            e3 = p.fence(nc.vector, V.tensor_copy(gsc[DH_][:, 0:512], PS4[:]), s_ev)
            e3 = p.fence(nc.vector, V.tensor_copy(gsc[DH_][:, 512:1024], PS5[:]), s_ev)
            for j in range(4):
                ps_free[j] = e2
            ps4_free = e3
            if tp == 0:
                p.wait(nc.vector, od)
                p.wait(nc.vector, tms)
                p.wait(nc.vector, last_out_mm)
            hid_ = p.fence(nc.vector, V.tensor_mul(gsc[HID][:], gsc[SG][:],
                                                   gsc[HBM][:]), s_ev)
            p.wait(nc.vector, lrcf_t)
            h1 = p.fence(nc.vector, V.tensor_scalar(
                hl1[:, t0 * 512:(t0 + 1) * 512], gsc[HID][:, 0:512],
                lrcfc[:, t0 * 4 + 1:t0 * 4 + 2], None, ALU.mult), s_ev)
            h1 = p.fence(nc.vector, V.tensor_scalar(
                hl1[:, (t0 + 1) * 512:(t0 + 2) * 512], gsc[HID][:, 512:1024],
                lrcfc[:, (t0 + 1) * 4 + 1:(t0 + 1) * 4 + 2], None, ALU.mult), s_ev)
            db = p.fence(nc.vector, V.tensor_mul(
                dhb[:, t0 * 512:(t0 + 2) * 512], gsc[DH_][:], gsc[SG][:]), s_ev)
            a_ = p.fence(nc.vector, V.tensor_mul(gsc[HID][:], gsc[SG][:],
                                                 gsc[S_][:]), s_ev)
            a_ = p.fence(nc.vector, V.tensor_sub(gsc[HID][:], gsc[SG][:],
                                                 gsc[HID][:]), s_ev)
            a_ = p.fence(nc.vector, V.tensor_add(gsc[HID][:], gsc[S_][:],
                                                 gsc[HID][:]), s_ev)
            t1 = p.fence(nc.vector, V.tensor_mul(gsc[SG][:], gsc[DH_][:],
                                                 gsc[HBM][:]), s_ev)
            ge = p.fence(nc.vector, V.tensor_mul(
                dgb[:, t0 * 512:(t0 + 2) * 512], gsc[SG][:], gsc[HID][:]), s_ev)
        grad_done = ge
        if c + 1 < nchunks:
            npar = (c + 1) % 2
            nin = s_in[npar]
            p.wait(nc.gpsimd, grad_done)
            ldq_next = None
            for oc in range(4):
                for (dst, srcd) in ((qn, qh_d), (kn, kh_d), (vv, vh_d)):
                    ldq_next = p.inc(nc.gpsimd.dma_start(
                        out=dst[:, oc * C:(oc + 1) * C],
                        in_=srcd[oc * 128:(oc + 1) * 128, (c + 1) * C:(c + 2) * C]),
                        nin, 16)
            ldq_next = p.inc(nc.gpsimd.dma_start(
                out=lrcf[npar][:], in_=lrtm_d[:, (c + 1) * 32:(c + 2) * 32]), nin, 16)
            ldq_next = p.inc(nc.gpsimd.dma_start(
                out=momr[:], in_=mom_d[:, (c + 1) * C:(c + 2) * C]), nin, 16)
            ldq_next = p.inc(nc.gpsimd.dma_start(
                out=sclr[:], in_=scl_d[:, (c + 1) * C:(c + 2) * C]), nin, 16)

        for m in (range(3) if phases >= 7 else ()):
            lhs_slab = [dgb, vTM, dhb][m]
            p.wait(nc.tensor, grad_done)
            p.wait(nc.tensor, ktm_done)
            for b in range(4):
                p.wait(nc.tensor, ps_free[b])
            tdw = None
            tdw_prev = [None, None]
            kls = [None, None]
            if m != 1:
                li = 0 if m == 0 else 2
                kls[0] = p.fence(nc.vector, V.tensor_scalar(
                    klb[0][:], kTM[:, 0:512],
                    lrcfc[:, li:li + 1], None, ALU.mult), s_ev)
            for tc in range(NT):
                kp = tc % 2
                if m != 1:
                    li = 0 if m == 0 else 2
                    if tc + 1 < NT:
                        p.wait(nc.vector, tdw_prev[1 - kp])
                        kls[1 - kp] = p.fence(nc.vector, V.tensor_scalar(
                            klb[1 - kp][:], kTM[:, (tc + 1) * 512:(tc + 2) * 512],
                            lrcfc[:, (tc + 1) * 4 + li:(tc + 1) * 4 + li + 1], None,
                            ALU.mult), s_ev)
                    p.wait(nc.tensor, kls[kp])
                    rhs_ap = klb[kp][:]
                else:
                    rhs_ap = hl1[:, tc * 512:(tc + 1) * 512]
                last = None
                for mc in range(4):
                    last = T.matmul(PS[mc][:],
                                    lhs_slab[:, tc * 512 + mc * 128:tc * 512 + (mc + 1) * 128],
                                    rhs_ap,
                                    start=(tc == 0), stop=(tc == NT - 1))
                tdw = p.inc(last, s_pe)
                tdw_prev[kp] = tdw

            p.wait(nc.vector, tdw)
            p.wait(nc.scalar, mmr)
            ev = p.fence(nc.vector, V.scalar_tensor_tensor(
                dwf[:], Mb[m][:], mmc[:, 0:1], PSA[:], ALU.mult, ALU.add), s_ev)
            for j in range(4):
                ps_free[j] = ev
            mnew = p.fence(nc.vector, V.tensor_copy(Mb[m][:], dwf[:]), s_ev)

            p.wait(nc.scalar, mnew)
            aq2 = None
            for j in range(4):
                aq2 = p.fence(nc.scalar, SC.activation(
                    gsc[0][:, 0:512], dwf[:, j * 512:(j + 1) * 512], AF.Square,
                    accum_out=rcol[:, j:j + 1]), s_ac)
            p.wait(nc.tensor, aq2)
            p.wait(nc.tensor, ps4_free)
            tnf = p.inc(T.matmul(PS4[0:1, 0:4], onesf[:, 0:1], rcol[:, 0:4],
                                 start=True, stop=True), s_pe)
            p.wait(nc.vector, tnf)
            nf = p.fence(nc.vector, V.reduce_sum(srow[0:1, 0:1], PS4[0:1, 0:4],
                                                 axis=AX.X), s_ev)
            ps4_free = nf
            p.wait(nc.scalar, nf)
            nf2 = p.fence(nc.scalar, SC.sqrt(srow[0:1, 1:2], srow[0:1, 0:1]), s_ac)
            p.wait(nc.vector, nf2)
            nf3 = p.fence(nc.vector, V.tensor_scalar_add(srow[0:1, 2:3], srow[0:1, 1:2],
                                                         1e-7), s_ev)
            nf4 = p.fence(nc.vector, V.reciprocal(srow[0:1, 3:4], srow[0:1, 2:3]), s_ev)
            p.wait(nc.tensor, nf4)
            rno = bcast_col(srow[0:1, 3:4], rnc[:, 0:1])

            p.wait(nc.vector, rno)
            x0 = None
            for j in range(4):
                x0 = p.fence(nc.vector, V.tensor_scalar(
                    Xb[0][:, j * 512:(j + 1) * 512], dwf[:, j * 512:(j + 1) * 512],
                    rnc[:, 0:1], None, ALU.mult), s_ev)
            x0t = transpose512(Xb[0], XTb[0], x0)

            cx = 0
            prev_ev = x0t
            x5 = None
            for it, (ca, cb_, cc_) in enumerate(NS_COEFFS):
                XB, XT_ = Xb[cx], XTb[cx]
                XB2, XT2 = Xb[1 - cx], XTb[1 - cx]
                p.wait(nc.tensor, prev_ev)
                for b in range(4):
                    p.wait(nc.tensor, ps_free[b])
                last = None
                for ic in range(4):
                    for dch in range(4):
                        last = T.matmul(PS[ic][:],
                                        XT_[:, dch * 512 + ic * 128:dch * 512 + (ic + 1) * 128],
                                        XT_[:, dch * 512:(dch + 1) * 512],
                                        start=(dch == 0), stop=(dch == 3))
                tA = p.inc(last, s_pe)
                p.wait(nc.scalar, tA)
                p.wait(nc.vector, tA)
                SC.copy(Ab[:, 0:1024], PSA[:, 0:1024])
                ab = p.inc(SC.activation(Acb[:, 0:1024], PSA[:, 0:1024], AF.Copy,
                                         scale=float(cc_)), s_ac)
                V.tensor_copy(Ab[:, 1024:2048], PSA[:, 1024:2048])
                acb = p.inc(V.tensor_scalar(Acb[:, 1024:2048], PSA[:, 1024:2048],
                                            float(cc_), None, ALU.mult), s_ev)
                p.wait(nc.tensor, ab)
                p.wait(nc.tensor, acb)
                last = None
                for ic in range(4):
                    for jch in range(4):
                        last = T.matmul(PS[ic][:],
                                        Ab[:, jch * 512 + ic * 128:jch * 512 + (ic + 1) * 128],
                                        Acb[:, jch * 512:(jch + 1) * 512],
                                        start=(jch == 0), stop=(jch == 3))
                tA2 = p.inc(last, s_pe)
                p.wait(nc.vector, tA2)
                bb_ = p.fence(nc.vector, V.scalar_tensor_tensor(
                    Acb[:], Ab[:], float(cb_), PSA[:], ALU.mult, ALU.add), s_ev)
                p.wait(nc.tensor, bb_)
                last = None
                for ic in range(4):
                    for jch in range(4):
                        T.matmul(PS[ic][:],
                                 Acb[:, jch * 512 + ic * 128:jch * 512 + (ic + 1) * 128],
                                 XB[:, jch * 512:(jch + 1) * 512],
                                 start=(jch == 0), stop=False)
                    last = T.matmul(PS[ic][:], diag[:, it * 128:(it + 1) * 128],
                                    XB[:, ic * 512:(ic + 1) * 512],
                                    start=False, stop=True)
                tBX = p.inc(last, s_pe)
                if it == 4:
                    p.wait(nc.scalar, tBX)
                    p.wait(nc.vector, tBX)
                    x5a = p.inc(SC.copy(X5f[:, 0:1024], PSA[:, 0:1024]), s_ac)
                    x5 = p.inc(V.tensor_copy(X5f[:, 1024:2048], PSA[:, 1024:2048]), s_ev)
                    x5 = [x5a, x5]
                    for j in range(4):
                        ps_free[j] = x5
                else:
                    p.wait(nc.scalar, tBX)
                    p.wait(nc.vector, tBX)
                    xba = p.inc(SC.copy(XB2[:, 0:1024], PSA[:, 0:1024]), s_ac)
                    xbb = p.inc(V.tensor_copy(XB2[:, 1024:2048], PSA[:, 1024:2048]), s_ev)
                    xb2 = [xba, xbb]
                    p.wait(nc.tensor, xb2)
                    last = None
                    for dc in range(4):
                        for jch in range(4):
                            T.matmul(PS[dc][:],
                                     XB[:, jch * 512 + dc * 128:jch * 512 + (dc + 1) * 128],
                                     Acb[:, jch * 512:(jch + 1) * 512],
                                     start=(jch == 0), stop=False)
                        last = T.matmul(PS[dc][:], diag[:, it * 128:(it + 1) * 128],
                                        XT_[:, dc * 512:(dc + 1) * 512],
                                        start=False, stop=True)
                    tXT = p.inc(last, s_pe)
                    p.wait(nc.vector, tXT)
                    p.wait(nc.scalar, tXT)
                    xta = p.inc(SC.copy(XT2[:, 0:1024], PSA[:, 0:1024]), s_ac)
                    xtb = p.inc(V.tensor_copy(XT2[:, 1024:2048], PSA[:, 1024:2048]), s_ev)
                    xt2 = [xta, xtb]
                    for j in range(4):
                        ps_free[j] = xt2
                    prev_ev = xt2
                    cx = 1 - cx

            p.wait(nc.vector, x5)
            tw = p.fence(nc.vector, V.tensor_add(dwf[:], W[m][:], X5f[:]), s_ev)
            p.wait(nc.scalar, tw)
            p.wait(nc.scalar, nrm)
            rn = None
            for j in range(4):
                rn = p.fence(nc.scalar, SC.activation(
                    gsc[0][:, 0:512], dwf[:, j * 512:(j + 1) * 512], AF.Square,
                    accum_out=rcol[:, j:j + 1]), s_ac)
            rn = p.fence(nc.scalar, SC.sqrt(rcol[:, 0:4], rcol[:, 0:4]), s_ac)
            p.wait(nc.vector, rn)
            sc1 = p.fence(nc.vector, V.tensor_scalar_add(rcol[:, 0:4], rcol[:, 0:4],
                                                         1e-5), s_ev)
            sc2 = p.fence(nc.vector, V.reciprocal(rcol[:, 4:8], rcol[:, 0:4]), s_ev)
            sc3 = p.fence(nc.vector, V.tensor_mul(rcol[:, 4:8], rcol[:, 4:8],
                                                  ncol[:, 4 * m:4 * m + 4]), s_ev)
            wn = None
            for j in range(4):
                wn = p.fence(nc.vector, V.tensor_scalar(
                    W[m][:, j * 512:(j + 1) * 512], dwf[:, j * 512:(j + 1) * 512],
                    rcol[:, 4 + j:5 + j], None, ALU.mult), s_ev)
            wb2 = p.fence(nc.vector, V.tensor_copy(Wb[m][:], W[m][:]), s_ev)
            chunk_last = transpose512(Wb[m], WTb[m], wb2)
        if phases >= 7:
            prev_wt = chunk_last

    nc.sync.wait_ge(s_out, p.val(s_out))
    p.close()
    return p.nc


def _lr_tm(lr3):
    out = np.zeros((128, NCHUNK * 32), np.float32)
    a = np.asarray(lr3, np.float32).reshape(3, NCHUNK, 8, 128)
    for c in range(NCHUNK):
        for tc in range(8):
            for i in range(3):
                out[:, c * 32 + tc * 4 + i] = a[i, c, tc]
    return out


def _chain_inputs(st1, inp, head):
    return {
        "qh": np.ascontiguousarray(st1["qkv"][head * HD:(head + 1) * HD]),
        "kh": np.ascontiguousarray(st1["qkv"][2048 + head * HD:2048 + (head + 1) * HD]),
        "vh": np.ascontiguousarray(st1["qkv"][4096 + head * HD:4096 + (head + 1) * HD]),
        "lrtm": _lr_tm(st1["aux"][3 * head:3 * head + 3]),
        "momh": np.ascontiguousarray(st1["aux"][32 + head:33 + head]).astype(np.float32),
        "sclh": np.ascontiguousarray(st1["aux"][64 + head:65 + head]).astype(np.float32),
        "w0": np.ascontiguousarray(inp["L0_attn_w0"][head]).astype(np.float32),
        "w1": np.ascontiguousarray(inp["L0_attn_w1"][head]).astype(np.float32),
        "w2": np.ascontiguousarray(inp["L0_attn_w2"][head]).astype(np.float32),
        "gcolh": np.ascontiguousarray(
            inp["L0_attn_o_norm_g"].reshape(4, 128).T).astype(np.float32),
        "idt": np.eye(128).astype(BF),
        "diag": np.concatenate([np.eye(128) * a for (a, b, c) in NS_COEFFS],
                               axis=1).astype(BF),
    }


def run_chain_stage(st1, inp):
    nc = build_chain()
    in_maps = [_chain_inputs(st1, inp, c % 4) for c in range(8)]
    res = run_bass_kernel_spmd(nc, in_maps, core_ids=list(range(8)))
    return np.concatenate([res.results[c]["ttt"] for c in range(4)], axis=0)



def build_attn(nheads=NAH):
    p = P("attn")
    nc = p.nc
    aq_d = p.dram("aqh", [DIM, CHUNK], BF16, kind="ExternalInput")
    ak_d = p.dram("akh", [DIM, 2 * CHUNK], BF16, kind="ExternalInput")
    av_d = p.dram("avh", [DIM, 2 * CHUNK], BF16, kind="ExternalInput")
    ttt_d = p.dram("tttc", [DIM, CHUNK], BF16, kind="ExternalInput")
    mup_d = p.dram("mup", [128, 128], F32, kind="ExternalInput")
    mlo_d = p.dram("mlo", [128, 128], F32, kind="ExternalInput")
    negc_d = p.dram("negc", [128, 1], F32, kind="ExternalInput")
    idt_d = p.dram("idt", [128, 128], BF16, kind="ExternalInput")
    mix_o = p.dram("mix", [DIM, CHUNK], BF16, kind="ExternalOutput")

    NQT = CHUNK // 128
    WINC = 12

    aqb = [p.sb(f"aqb{i}", [128, CHUNK], BF16) for i in range(2)]
    akb = [p.sb(f"akb{i}", [128, 2 * CHUNK], BF16) for i in range(2)]
    avb = [p.sb(f"avb{i}", [128, 2 * CHUNK], BF16) for i in range(2)]
    ttb = [p.sb(f"ttb{i}", [128, CHUNK], BF16) for i in range(2)]
    avTM = p.sb("avTM", [128, 16 * 128], BF16)
    probs_f = [p.sb(f"probs_f{i}", [128, 1536], F32) for i in range(2)]
    probsb = [p.sb(f"probsb{i}", [128, 1536], BF16) for i in range(2)]
    mcol2 = [p.sb(f"mcol2_{i}", [128, 8], F32) for i in range(2)]
    zcol2 = [p.sb(f"zcol2_{i}", [128, 8], F32) for i in range(2)]
    strip = p.sb("strip", [128, WINC * 512], BF16)
    mix = p.sb("mix_sb", [128, 16 * CHUNK], BF16)
    mup = p.sb("mup_sb", [128, 128], F32)
    mlo = p.sb("mlo_sb", [128, 128], F32)
    negc = p.sb("negc_sb", [128, 1], F32)
    idt = p.sb("idt_sb", [128, 128], BF16)
    mcol = p.sb("mcol", [128, 8], F32)
    zcol = p.sb("zcol", [128, 8], F32)

    PSQ = [p.ps(f"PSQ{i}", [128, 1536], F32) for i in range(2)]
    PV = p.ps("PVb", [128, 512], F32)
    PT = [p.ps("PTb0", [128, 1024], BF16)]

    s_in = [p.sem(f"s_ina{i}") for i in range(2)]
    s_w = p.sem("s_wa")
    s_pe = p.sem("s_pea")
    s_ev = p.sem("s_eva")
    s_ac = p.sem("s_aca")
    s_out = p.sem("s_outa")

    V, SC, T = nc.vector, nc.scalar, nc.tensor

    psq_free = [None, None]
    pv_free = None
    pt_bank_free = [None]
    pt_ctr = [0]

    def transpose_batch(pairs):
        tok = None
        i = 0
        while i < len(pairs):
            grp = pairs[i:i + 8]
            i += 8
            p.wait(nc.tensor, pt_bank_free[0])
            tt = None
            for r, (s_ap, d_ap) in enumerate(grp):
                tt = p.inc(T.transpose(PT[0][:, r * 128:(r + 1) * 128], s_ap,
                                       idt[:]), s_pe)
            p.wait(nc.vector, tt)
            tok = None
            for r, (s_ap, d_ap) in enumerate(grp):
                tok = p.inc(V.tensor_copy(d_ap, PT[0][:, r * 128:(r + 1) * 128]),
                            s_ev)
            pt_bank_free[0] = tok
        return tok

    wl = p.inc(nc.gpsimd.dma_start(out=mup[:], in_=mup_d), s_w, 16)
    wl = p.inc(nc.gpsimd.dma_start(out=mlo[:], in_=mlo_d), s_w, 16)
    wl = p.inc(nc.gpsimd.dma_start(out=negc[:], in_=negc_d), s_w, 16)
    wl = p.inc(nc.gpsimd.dma_start(out=idt[:], in_=idt_d), s_w, 16)

    head_done = [None, None]
    ld = [None, None]
    for a in range(nheads + 1):
        par = a % 2
        if a < nheads:
            p.wait(nc.gpsimd, head_done[par])
            t = None
            t = p.inc(nc.gpsimd.dma_start(
                out=aqb[par][:], in_=aq_d[a * 128:(a + 1) * 128, :]), s_in[par], 16)
            t = p.inc(nc.gpsimd.dma_start(
                out=akb[par][:], in_=ak_d[a * 128:(a + 1) * 128, :]), s_in[par], 16)
            t = p.inc(nc.gpsimd.dma_start(
                out=avb[par][:], in_=av_d[a * 128:(a + 1) * 128, :]), s_in[par], 16)
            t = p.inc(nc.gpsimd.dma_start(
                out=ttb[par][:], in_=ttt_d[a * 128:(a + 1) * 128, :]), s_in[par], 16)
            ld[par] = t
        if a == 0:
            continue
        h = a - 1
        bpar = 1 - par
        aq, ak, av, tb = aqb[bpar], akb[bpar], avb[bpar], ttb[bpar]
        p.wait(nc.vector, ld[bpar])
        aqs = p.fence(nc.vector, V.tensor_scalar(
            aq[:], aq[:], float(1.0 / np.sqrt(AHD)), None, ALU.mult), s_ev)
        p.wait(nc.tensor, ld[bpar])
        p.wait(nc.tensor, aqs)
        p.wait(nc.tensor, wl)
        avt = transpose_batch([(av[:, t_ * 128:(t_ + 1) * 128],
                                avTM[:, t_ * 128:(t_ + 1) * 128]) for t_ in range(16)])
        mix_ev = None
        for grp in range(2):
            p.wait(nc.vector, None)
            zs = p.fence(nc.vector, V.memset(strip[:], 0.0), s_ev)
            trs = None

            def phaseA(qt):
                qg = grp * 4 + qt
                par2 = qt % 2
                sbank = PSQ[par2]
                base = 128 * qg
                widths = (512, 512, 128)
                p.wait(nc.tensor, psq_free[par2])
                tS = None
                for i3 in range(3):
                    off = base + (512 * i3)
                    tS = p.inc(T.matmul(
                        sbank[:, i3 * 512:i3 * 512 + widths[i3]],
                        aq[:, qg * 128:(qg + 1) * 128],
                        ak[:, off:off + widths[i3]], start=True, stop=True), s_pe)
                p.wait(nc.vector, tS)
                mk = p.fence(nc.vector, V.tensor_add(sbank[:, 0:128], sbank[:, 0:128],
                                                     mup[:]), s_ev)
                mk = p.fence(nc.vector, V.tensor_add(sbank[:, 1024:1152],
                                                     sbank[:, 1024:1152], mlo[:]), s_ev)
                pref = 1024 - 128 * qg
                if pref > 0:
                    w0 = min(pref, 512)
                    mk = p.fence(nc.vector, V.tensor_scalar(
                        sbank[:, 0:w0], sbank[:, 0:w0], negc[:, 0:1], None,
                        ALU.add), s_ev)
                    if pref > 512:
                        mk = p.fence(nc.vector, V.tensor_scalar(
                            sbank[:, 512:512 + pref - 512],
                            sbank[:, 512:512 + pref - 512],
                            negc[:, 0:1], None, ALU.add), s_ev)
                mc = mcol2[par2]
                mk = p.fence(nc.vector, V.reduce_max(mc[:, 3:4], sbank[:, 0:1152],
                                                     axis=AX.X, negate=True), s_ev)
                return mk

            def phaseB(qt, mk):
                nonlocal trs
                par2 = qt % 2
                sbank = PSQ[par2]
                widths = (512, 512, 128)
                mc, zc = mcol2[par2], zcol2[par2]
                pf, pb2 = probs_f[par2], probsb[par2]
                p.wait(nc.scalar, mk)
                az = p.fence(nc.scalar, SC.activation(
                    pf[:, 0:1152], sbank[:, 0:1152], AF.Exp,
                    bias=mc[:, 3:4], accum_out=zc[:, 3:4]), s_ac)
                p.wait(nc.vector, az)
                zz = p.fence(nc.vector, V.reciprocal(zc[:, 4:5], zc[:, 3:4]), s_ev)
                p.wait(nc.scalar, zz)
                pb = p.inc(SC.activation(pb2[:, 0:1152], pf[:, 0:1152],
                                         AF.Copy, scale=zc[:, 4:5]), s_ac)
                psq_free[par2] = pb
                p.wait(nc.tensor, pb)
                p.wait(nc.tensor, zs)
                trs = transpose_batch(
                    [(pb2[:, j * 128:(j + 1) * 128],
                      strip[:, (qt + j) * 512 + qt * 128:(qt + j) * 512 + (qt + 1) * 128])
                     for j in range(9)])

            mks = {}
            mks[0] = phaseA(0)
            mks[1] = phaseA(1)
            phaseB(0, mks[0])
            mks[2] = phaseA(2)
            phaseB(1, mks[1])
            mks[3] = phaseA(3)
            phaseB(2, mks[2])
            phaseB(3, mks[3])

            p.wait(nc.tensor, trs)
            p.wait(nc.tensor, avt)
            p.wait(nc.tensor, pv_free)
            last = None
            for kc in range(WINC):
                ktile = grp * 4 + kc
                last = T.matmul(PV[:], avTM[:, ktile * 128:(ktile + 1) * 128],
                                strip[:, kc * 512:(kc + 1) * 512],
                                start=(kc == 0), stop=(kc == WINC - 1))
            tpv = p.inc(last, s_pe)
            p.wait(nc.vector, tpv)
            p.wait(nc.vector, ld[bpar])
            mix_ev = p.inc(V.tensor_add(
                mix[:, h * CHUNK + grp * 512:h * CHUNK + (grp + 1) * 512],
                tb[:, grp * 512:(grp + 1) * 512], PV[:]), s_ev)
            pv_free = mix_ev
        nc.sync.wait_ge(*mix_ev)
        od = p.inc(nc.sync.dma_start(
            out=mix_o[h * 128:(h + 1) * 128, :],
            in_=mix[:, h * CHUNK:(h + 1) * CHUNK]), s_out, 16)
        head_done[bpar] = [od, avt]

    nc.sync.wait_ge(s_out, p.val(s_out))
    p.close()
    return p.nc


def _attn_inputs(st1, st2, chunk):
    c0 = chunk * CHUNK
    ak = np.zeros((DIM, 2 * CHUNK), dtype=BF)
    av = np.zeros((DIM, 2 * CHUNK), dtype=BF)
    lo = max(0, c0 - CHUNK)
    n = c0 - lo
    if n > 0:
        ak[:, CHUNK - n:CHUNK] = st1["aqk"][DIM:, lo:c0]
        av[:, CHUNK - n:CHUNK] = st1["av"][:, lo:c0]
    ak[:, CHUNK:] = st1["aqk"][DIM:, c0:c0 + CHUNK]
    av[:, CHUNK:] = st1["av"][:, c0:c0 + CHUNK]
    qi = np.arange(128)[:, None]
    kj = np.arange(128)[None, :]
    mup = np.where(kj >= qi, 0.0, NEG_BIG).astype(np.float32)
    mlo = np.where(kj <= qi, 0.0, NEG_BIG).astype(np.float32)
    return {
        "aqh": np.ascontiguousarray(st1["aqk"][:DIM, c0:c0 + CHUNK]),
        "akh": ak, "avh": av,
        "tttc": np.ascontiguousarray(st2[:, c0:c0 + CHUNK]),
        "mup": mup, "mlo": mlo,
        "negc": np.full((128, 1), NEG_BIG if chunk == 0 else 0.0, np.float32),
        "idt": np.eye(128).astype(BF),
    }


def run_attn_stage(st1, st2):
    nc = build_attn()
    in_maps = [_attn_inputs(st1, st2, c) for c in range(8)]
    res = run_bass_kernel_spmd(nc, in_maps, core_ids=list(range(8)))
    return np.concatenate([res.results[c]["mix"] for c in range(8)], axis=1)



def build_tail(ttok=1024):
    p = P("tail")
    nc = p.nc
    mix_d = p.dram("mixc", [DIM, ttok], BF16, kind="ExternalInput")
    x0_d = p.dram("x0t", [DIM, ttok], F32, kind="ExternalInput")
    wo_d = p.dram("wo", [DIM, DIM], F32, kind="ExternalInput")
    w1_d = p.dram("w1t", [DIM, FFN_H], F32, kind="ExternalInput")
    w3_d = p.dram("w3t", [DIM, FFN_H], F32, kind="ExternalInput")
    w2_d = p.dram("w2t", [FFN_H, DIM], F32, kind="ExternalInput")
    hblk_d = p.dram("hblk", [128, 128], BF16, kind="ExternalInput")
    h128_d = p.dram("h128", [128, 128], BF16, kind="ExternalInput")
    out_d = p.dram("outt", [DIM, ttok], F32, kind="ExternalOutput")
    w2pp_d = p.dram("w2pp", [FFN_H, DIM], BF16, kind="Internal")
    y1_d = p.dram("y1d", [FFN_H, ttok], BF16, kind="Internal")

    NTB = ttok // 512
    NSUB = ttok // 512

    mix = p.sb("mixs", [128, 16 * ttok], BF16)
    ab = p.sb("abs_", [128, 16 * ttok], BF16)
    x1b = p.sb("x1b", [128, 16 * ttok], BF16)
    wbuf = p.sb("wbuf", [128, 32768], BF16)
    wsl = [p.sb(f"wsl{i}", [128, 2048], BF16) for i in range(2)]
    fb = p.sb("fb", [128, 16 * 512], BF16)
    outf = p.sb("outf", [128, 2 * 512], F32)
    hblk = p.sb("hblk_sb", [128, 128], BF16)
    h128 = p.sb("h128_sb", [128, 128], BF16)
    hsb = p.sb("hsb", [128, 512], BF16)
    sfb = p.sb("sfb", [128, 512], BF16)
    rows = p.sb("rows", [1, 3 * 1024], F32)
    bcA = p.sb("bcA", [128, 1024], BF16)
    bcB = p.sb("bcB", [128, 1024], BF16)
    onesb = p.sb("onesb", [128, 1], BF16)
    onesrow_b = p.sb("onesrow_b", [1, 128], BF16)
    cst = p.sb("cst", [128, 2], F32)
    y1tmp = p.sb("y1tmp", [128, 512], BF16)

    PS = [p.ps(f"PS{i}", [128, 512], F32) for i in range(8)]

    s_w = [p.sem(f"s_wt{i}") for i in range(2)]
    s_x = p.sem("s_xt")
    s_mixl = p.sem("s_mixl")
    s_wol = p.sem("s_wol")
    s_x0l = p.sem("s_x0l")
    s_prm = p.sem("s_prmt")
    s_pe = p.sem("s_pet")
    s_ev = p.sem("s_evt")
    s_ac = p.sem("s_act")
    s_out = [p.sem(f"s_outt{i}") for i in range(2)]
    s_y1 = p.sem("s_y1")
    s_pp = p.sem("s_pp")

    V, SC, T = nc.vector, nc.scalar, nc.tensor
    ps_free = [None] * 8

    nc.vector.memset(cst[:, 0:1], 1e-5)
    nc.vector.memset(onesb[:], 1.0)
    ini = p.inc(nc.vector.memset(onesrow_b[:], 1.0), s_ev)

    wl = p.inc(nc.gpsimd.dma_start(out=hblk[:], in_=hblk_d), s_prm, 16)
    wl = p.inc(nc.gpsimd.dma_start(out=h128[:], in_=h128_d), s_prm, 16)
    mld = None
    for dch in range(16):
        mld = p.inc(nc.gpsimd.dma_start(
            out=mix[:, dch * ttok:(dch + 1) * ttok],
            in_=mix_d[dch * 128:(dch + 1) * 128, :]), s_mixl, 16)

    w2r = w2_d.rearrange("(ih il) d -> il ih d", il=64)
    wld = [None, None]
    buf_cons = [None, None]
    pp_done = None
    pp_dmas = []
    for il in range(64):
        par = il % 2
        p.wait(nc.gpsimd, buf_cons[par])
        wld[par] = p.inc(nc.gpsimd.dma_start(out=wsl[par][:], in_=w2r[il]),
                         s_w[par], 16)
        p.wait(nc.tensor, wld[par])
        p.wait(nc.tensor, wl)
        for q in range(4):
            bank = q
            p.wait(nc.tensor, ps_free[bank])
            tq = p.inc(T.matmul(PS[bank][:], h128[:],
                                wsl[par][:, q * 512:(q + 1) * 512],
                                start=True, stop=True), s_pe)
            buf_cons[par] = tq
            p.wait(nc.vector, tq)
            p.wait(nc.vector, pp_done)
            ev = p.inc(V.tensor_copy(fb[:, q * 1024:q * 1024 + 512], PS[bank][:]),
                       s_ev)
            ps_free[bank] = ev
            nc.sync.wait_ge(*ev)
            pp_done = p.inc(nc.sync.dma_start(
                out=w2pp_d.rearrange("(jh il) d -> il jh d", il=64)[il][:, q * 512:(q + 1) * 512],
                in_=fb[:, q * 1024:q * 1024 + 512]), s_pp, 16)

    wol = None
    for ic in range(16):
        wol = p.inc(nc.gpsimd.dma_start(
            out=wbuf[:, ic * 2048:(ic + 1) * 2048],
            in_=wo_d[ic * 128:(ic + 1) * 128, :]), s_wol, 16)
    p.wait(nc.tensor, wol)
    p.wait(nc.tensor, mld)
    ab_ev = None
    for tb in range(NTB):
        for oc in range(16):
            bank = oc % 8
            p.wait(nc.tensor, ps_free[bank])
            last = None
            for ic in range(16):
                last = T.matmul(PS[bank][:],
                                wbuf[:, ic * 2048 + oc * 128:ic * 2048 + (oc + 1) * 128],
                                mix[:, ic * ttok + tb * 512:ic * ttok + (tb + 1) * 512],
                                start=(ic == 0), stop=(ic == 15))
            t_mm = p.inc(last, s_pe)
            p.wait(nc.vector, t_mm)
            ab_ev = p.inc(V.tensor_copy(
                ab[:, oc * ttok + tb * 512:oc * ttok + (tb + 1) * 512], PS[bank][:]),
                s_ev)
            ps_free[bank] = ab_ev

    def ln_stats(slab, sqslab, sq_rdy, row_off):
        sq = None
        p.wait(nc.vector, sq_rdy)
        for dch in range(16):
            sq = V.tensor_mul(sqslab[:, dch * ttok:(dch + 1) * ttok],
                              slab[:, dch * ttok:(dch + 1) * ttok],
                              slab[:, dch * ttok:(dch + 1) * ttok])
        sq = p.inc(sq, s_ev)
        p.wait(nc.tensor, sq)
        p.wait(nc.tensor, ini)
        toks = []
        for h in range(ttok // 512):
            bank = h % 4
            p.wait(nc.tensor, ps_free[bank])
            p.wait(nc.tensor, ps_free[4 + bank])
            last = None
            for dch in range(16):
                last = T.matmul(PS[bank][0:1, :], onesb[:, 0:1],
                                slab[:, dch * ttok + h * 512:dch * ttok + (h + 1) * 512],
                                start=(dch == 0), stop=(dch == 15))
            for dch in range(16):
                last = T.matmul(PS[4 + bank][0:1, :], onesb[:, 0:1],
                                sqslab[:, dch * ttok + h * 512:dch * ttok + (h + 1) * 512],
                                start=(dch == 0), stop=(dch == 15))
            tt = p.inc(last, s_pe)
            p.wait(nc.scalar, tt)
            a1 = p.fence(nc.scalar, SC.mul(rows[0:1, row_off + h * 512:row_off + (h + 1) * 512],
                                           PS[bank][0:1, :], -1.0 / DIM), s_ac)
            a1 = p.fence(nc.scalar, SC.mul(rows[0:1, row_off + ttok + h * 512:row_off + ttok + (h + 1) * 512],
                                           PS[4 + bank][0:1, :], 1.0 / DIM), s_ac)
            ps_free[bank] = a1
            ps_free[4 + bank] = a1
            toks.append(a1)
        nm = rows[0:1, row_off:row_off + ttok]
        msq = rows[0:1, row_off + ttok:row_off + 2 * ttok]
        p.wait(nc.vector, toks[-1])
        v1 = p.fence(nc.vector, V.tensor_mul(rows[0:1, row_off + 2 * ttok:row_off + 3 * ttok],
                                             nm, nm), s_ev)
        v1 = p.fence(nc.vector, V.tensor_sub(msq, msq,
                                             rows[0:1, row_off + 2 * ttok:row_off + 3 * ttok]),
                     s_ev)
        p.wait(nc.scalar, v1)
        a2 = p.fence(nc.scalar, SC.activation(msq, msq, AF.Ln, bias=cst[0:1, 0:1]), s_ac)
        a2 = p.fence(nc.scalar, SC.activation(msq, msq, AF.Exp, scale=-0.5), s_ac)
        return a2

    def ln_bcast(row_off, rdy):
        p.wait(nc.vector, rdy)
        p.wait(nc.scalar, rdy)
        c1 = p.fence(nc.vector, V.tensor_copy(fb[0:1, 0:ttok],
                                              rows[0:1, row_off:row_off + ttok]), s_ev)
        c2 = p.fence(nc.scalar, SC.copy(bcB[0:1, 0:ttok],
                                        rows[0:1, row_off + ttok:row_off + 2 * ttok]), s_ac)
        toks = []
        for h in range(ttok // 512):
            bank = h % 4
            p.wait(nc.tensor, [ps_free[bank], c1, c2])
            t1 = p.inc(T.matmul(PS[bank][:], onesrow_b[:],
                                fb[0:1, h * 512:(h + 1) * 512], start=True, stop=True),
                       s_pe)
            p.wait(nc.vector, t1)
            e1 = p.inc(V.tensor_copy(bcA[:, h * 512:(h + 1) * 512], PS[bank][:]), s_ev)
            ps_free[bank] = e1
            p.wait(nc.tensor, e1)
            t2 = p.inc(T.matmul(PS[bank][:], onesrow_b[:],
                                bcB[0:1, h * 512:(h + 1) * 512], start=True, stop=True),
                       s_pe)
            p.wait(nc.vector, t2)
            e2 = p.inc(V.tensor_copy(bcB[:, h * 512:(h + 1) * 512], PS[bank][:]), s_ev)
            ps_free[bank] = e2
            toks.append(e2)
        return toks[-1]

    st2tok = ln_stats(ab, mix, ab_ev, 0)
    bc2 = ln_bcast(0, st2tok)
    x0ld = None
    p.wait(nc.gpsimd, bc2)
    for dch in range(16):
        x0ld = p.inc(nc.gpsimd.dma_start(
            out=mix[:, dch * ttok:(dch + 1) * ttok],
            in_=x0_d[dch * 128:(dch + 1) * 128, :]), s_x0l, 16)
    x1tok = None
    p.wait(nc.vector, bc2)
    p.wait(nc.vector, x0ld)
    for dch in range(16):
        x1tok = p.fence(nc.vector, V.tensor_add(
            x1b[:, dch * ttok:(dch + 1) * ttok], ab[:, dch * ttok:(dch + 1) * ttok],
            bcA[:, 0:ttok]), s_ev)
        x1tok = p.fence(nc.vector, V.tensor_mul(
            x1b[:, dch * ttok:(dch + 1) * ttok], x1b[:, dch * ttok:(dch + 1) * ttok],
            bcB[:, 0:ttok]), s_ev)
        x1tok = p.fence(nc.vector, V.tensor_add(
            x1b[:, dch * ttok:(dch + 1) * ttok], x1b[:, dch * ttok:(dch + 1) * ttok],
            mix[:, dch * ttok:(dch + 1) * ttok]), s_ev)
    st3tok = ln_stats(x1b, ab, x1tok, 0)
    bc3 = ln_bcast(0, st3tok)
    xn_tok = None
    p.wait(nc.vector, bc3)
    for dch in range(16):
        xn_tok = p.fence(nc.vector, V.tensor_add(
            ab[:, dch * ttok:(dch + 1) * ttok], x1b[:, dch * ttok:(dch + 1) * ttok],
            bcA[:, 0:ttok]), s_ev)
        xn_tok = p.fence(nc.vector, V.tensor_mul(
            ab[:, dch * ttok:(dch + 1) * ttok], ab[:, dch * ttok:(dch + 1) * ttok],
            bcB[:, 0:ttok]), s_ev)
    x1n = ab

    p.wait(nc.tensor, xn_tok)
    wload = [None, None]
    y1_done = None
    y1tmp_free = None
    hsb_free = None
    sfb_free = None
    for hb in range(16):
        par = hb % 2
        p.wait(nc.gpsimd, wload[par])
        wt = None
        for dch in range(16):
            wt = p.inc(nc.gpsimd.dma_start(
                out=wbuf[:, par * 16384 + dch * 512:par * 16384 + (dch + 1) * 512],
                in_=w1_d[dch * 128:(dch + 1) * 128, hb * 512:(hb + 1) * 512]),
                s_w[par], 16)
            wt = p.inc(nc.gpsimd.dma_start(
                out=wbuf[:, par * 16384 + 8192 + dch * 512:par * 16384 + 8192 + (dch + 1) * 512],
                in_=w3_d[dch * 128:(dch + 1) * 128, hb * 512:(hb + 1) * 512]),
                s_w[par], 16)
        p.wait(nc.tensor, wt)
        lastcons = None

        def upA(hh, tb):
            b_up, b_li = (hh * 2 + tb) % 3, 3 + (hh * 2 + tb) % 3
            p.wait(nc.tensor, ps_free[b_up])
            p.wait(nc.tensor, ps_free[b_li])
            for dch in range(16):
                T.matmul(PS[b_up][:],
                         wbuf[:, par * 16384 + dch * 512 + hh * 128:par * 16384 + dch * 512 + (hh + 1) * 128],
                         x1n[:, dch * ttok + tb * 512:dch * ttok + (tb + 1) * 512],
                         start=(dch == 0), stop=(dch == 15))
            last = None
            for dch in range(16):
                last = T.matmul(PS[b_li][:],
                                wbuf[:, par * 16384 + 8192 + dch * 512 + hh * 128:par * 16384 + 8192 + dch * 512 + (hh + 1) * 128],
                                x1n[:, dch * ttok + tb * 512:dch * ttok + (tb + 1) * 512],
                                start=(dch == 0), stop=(dch == 15))
            return p.inc(last, s_pe)

        def upB(hh, tb, tmm):
            nonlocal lastcons, y1_done, y1tmp_free, hsb_free, sfb_free
            b_up, b_li = (hh * 2 + tb) % 3, 3 + (hh * 2 + tb) % 3
            lastcons = tmm
            p.wait(nc.scalar, tmm)
            p.wait(nc.scalar, sfb_free)
            sfe = p.inc(SC.activation(sfb[:], PS[b_up][:], AF.Silu), s_ac)
            p.wait(nc.vector, sfe)
            p.wait(nc.vector, hsb_free)
            hbe = p.fence(nc.vector, V.tensor_mul(hsb[:], sfb[:], PS[b_li][:]), s_ev)
            sfb_free = hbe
            ps_free[b_up] = [sfe, hbe]
            ps_free[b_li] = hbe
            p.wait(nc.tensor, hbe)
            bank1 = 6 + (hh + tb) % 2
            p.wait(nc.tensor, ps_free[bank1])
            ts1 = p.inc(T.matmul(PS[bank1][:], hblk[:], hsb[:],
                                 start=True, stop=True), s_pe)
            hsb_free = ts1
            p.wait(nc.vector, ts1)
            p.wait(nc.vector, y1tmp_free)
            y1e = p.inc(V.tensor_copy(y1tmp[:], PS[bank1][:]), s_ev)
            ps_free[bank1] = y1e
            nc.sync.wait_ge(*y1e)
            y1_done = p.inc(nc.sync.dma_start(
                out=y1_d[hb * 512 + hh * 128:hb * 512 + (hh + 1) * 128,
                         tb * 512:(tb + 1) * 512],
                in_=y1tmp[:]), s_y1, 16)
            y1tmp_free = y1_done

        units = [(hh, tb) for hh in range(4) for tb in range(NTB)]
        pend = None
        for i, (hh, tb) in enumerate(units):
            t = upA(hh, tb)
            if pend is not None:
                upB(*pend)
            pend = (hh, tb, t)
        upB(*pend)
        wload[par] = lastcons

    pp_par_cons = [None, None]
    prev_b_done = None
    out_free = [None, None]
    for tsub in range(NSUB):
        yld = None
        p.wait(nc.gpsimd, wload[0])
        p.wait(nc.gpsimd, wload[1])
        p.wait(nc.gpsimd, y1_done)
        p.wait(nc.gpsimd, pp_par_cons[0])
        p.wait(nc.gpsimd, pp_par_cons[1])
        for hc in range(64):
            yld = p.inc(nc.gpsimd.dma_start(
                out=wbuf[:, hc * 512:(hc + 1) * 512],
                in_=y1_d[hc * 128:(hc + 1) * 128, tsub * 512:(tsub + 1) * 512]),
                s_x, 16)
        p.wait(nc.tensor, yld)
        fev = None
        for opass in range(2):
            for b8 in range(8):
                p.wait(nc.tensor, ps_free[b8])
            last = None
            for hc in range(64):
                par = hc % 2
                p.wait(nc.gpsimd, pp_par_cons[par])
                if tsub == 0 and opass == 0:
                    p.wait(nc.gpsimd, pp_done)
                wld2 = p.inc(nc.gpsimd.dma_start(
                    out=wsl[par][:],
                    in_=w2pp_d[hc * 128:(hc + 1) * 128, :]), s_w[par], 16)
                p.wait(nc.tensor, wld2)
                for b8 in range(8):
                    oc = opass * 8 + b8
                    last = T.matmul(PS[b8][:],
                                    wsl[par][:, oc * 128:(oc + 1) * 128],
                                    wbuf[:, hc * 512:(hc + 1) * 512],
                                    start=(hc == 0), stop=(hc == 63))
                tfm = p.inc(last, s_pe)
                pp_par_cons[par] = tfm
            p.wait(nc.vector, tfm)
            for b8 in range(8):
                oc = opass * 8 + b8
                fev = p.inc(V.tensor_copy(fb[:, oc * 512:(oc + 1) * 512], PS[b8][:]),
                            s_ev)
                ps_free[b8] = fev
        sq = None
        p.wait(nc.vector, fev)
        for dch in range(16):
            sq = V.tensor_mul(mix[:, dch * 512:(dch + 1) * 512],
                              fb[:, dch * 512:(dch + 1) * 512],
                              fb[:, dch * 512:(dch + 1) * 512])
        sq = p.inc(sq, s_ev)
        p.wait(nc.tensor, sq)
        p.wait(nc.tensor, ps_free[0])
        p.wait(nc.tensor, ps_free[4])
        last = None
        for dch in range(16):
            T.matmul(PS[0][0:1, :], onesb[:, 0:1],
                     fb[:, dch * 512:(dch + 1) * 512], start=(dch == 0), stop=(dch == 15))
        for dch in range(16):
            last = T.matmul(PS[4][0:1, :], onesb[:, 0:1],
                            mix[:, dch * 512:(dch + 1) * 512],
                            start=(dch == 0), stop=(dch == 15))
        tst = p.inc(last, s_pe)
        p.wait(nc.scalar, tst)
        a1 = p.fence(nc.scalar, SC.mul(rows[0:1, 0:512], PS[0][0:1, :], -1.0 / DIM),
                     s_ac)
        a1 = p.fence(nc.scalar, SC.mul(rows[0:1, 512:1024], PS[4][0:1, :], 1.0 / DIM),
                     s_ac)
        ps_free[0] = a1
        ps_free[4] = a1
        p.wait(nc.vector, a1)
        v1 = p.fence(nc.vector, V.tensor_mul(rows[0:1, 1024:1536], rows[0:1, 0:512],
                                             rows[0:1, 0:512]), s_ev)
        v1 = p.fence(nc.vector, V.tensor_sub(rows[0:1, 512:1024], rows[0:1, 512:1024],
                                             rows[0:1, 1024:1536]), s_ev)
        p.wait(nc.scalar, v1)
        a2 = p.fence(nc.scalar, SC.activation(rows[0:1, 512:1024], rows[0:1, 512:1024],
                                              AF.Ln, bias=cst[0:1, 0:1]), s_ac)
        a2 = p.fence(nc.scalar, SC.activation(rows[0:1, 512:1024], rows[0:1, 512:1024],
                                              AF.Exp, scale=-0.5), s_ac)
        p.wait(nc.vector, a2)
        c1 = p.fence(nc.vector, V.tensor_copy(bcA[0:1, 0:512], rows[0:1, 0:512]), s_ev)
        c1 = p.fence(nc.vector, V.tensor_copy(bcB[0:1, 0:512], rows[0:1, 512:1024]),
                     s_ev)
        p.wait(nc.tensor, [ps_free[0], ps_free[1], c1])
        t1 = p.inc(T.matmul(PS[0][:], onesrow_b[:], bcA[0:1, 0:512],
                            start=True, stop=True), s_pe)
        t1 = p.inc(T.matmul(PS[1][:], onesrow_b[:], bcB[0:1, 0:512],
                            start=True, stop=True), s_pe)
        p.wait(nc.vector, t1)
        e1 = p.inc(V.tensor_copy(bcA[:, 0:512], PS[0][:]), s_ev)
        e1 = p.inc(V.tensor_copy(bcB[:, 0:512], PS[1][:]), s_ev)
        ps_free[0] = e1
        ps_free[1] = e1
        ot = None
        fin = None
        p.wait(nc.vector, e1)
        for dch in range(16):
            opar = dch % 2
            ot = p.fence(nc.vector, V.tensor_add(
                fb[:, dch * 512:(dch + 1) * 512], fb[:, dch * 512:(dch + 1) * 512],
                bcA[:, 0:512]), s_ev)
            ot = p.fence(nc.vector, V.tensor_mul(
                fb[:, dch * 512:(dch + 1) * 512], fb[:, dch * 512:(dch + 1) * 512],
                bcB[:, 0:512]), s_ev)
            p.wait(nc.vector, out_free[opar])
            ot = p.fence(nc.vector, V.tensor_add(
                outf[:, opar * 512:(opar + 1) * 512], fb[:, dch * 512:(dch + 1) * 512],
                x1b[:, dch * ttok + tsub * 512:dch * ttok + (tsub + 1) * 512]), s_ev)
            nc.sync.wait_ge(*ot)
            fin = p.inc(nc.sync.dma_start(
                out=out_d[dch * 128:(dch + 1) * 128, tsub * 512:(tsub + 1) * 512],
                in_=outf[:, opar * 512:(opar + 1) * 512]), s_out[opar], 16)
            out_free[opar] = fin
        prev_b_done = fin

    for s in s_out:
        nc.sync.wait_ge(s, p.val(s))
    p.close()
    return p.nc


def _hadamard(n):
    h = np.array([[1.0]])
    while h.shape[0] < n:
        h = np.block([[h, h], [h, -h]])
    return h


def _tail_inputs(st3mix, inp, XT, chunk, ttok=1024):
    c0 = chunk * ttok
    hb = np.zeros((128, 128), np.float64)
    h64 = _hadamard(64)
    hb[0:64, 0:64] = h64
    hb[64:128, 64:128] = h64
    return {
        "mixc": np.ascontiguousarray(st3mix[:, c0:c0 + ttok]),
        "x0t": np.ascontiguousarray(XT[:, c0:c0 + ttok]).astype(np.float32),
        "wo": np.ascontiguousarray(inp["L0_attn_Wo"]).astype(np.float32),
        "w1t": np.ascontiguousarray(inp["L0_ffn_W1"].T).astype(np.float32),
        "w3t": np.ascontiguousarray(inp["L0_ffn_W3"].T).astype(np.float32),
        "w2t": np.ascontiguousarray(inp["L0_ffn_W2"].T).astype(np.float32),
        "hblk": (hb / np.sqrt(2.0)).astype(BF),
        "h128": (_hadamard(128) / 64.0).astype(BF),
    }


def run_tail_stage(st3mix, inp, XT):
    nc = build_tail()
    in_maps = [_tail_inputs(st3mix, inp, XT, c) for c in range(8)]
    res = run_bass_kernel_spmd(nc, in_maps, core_ids=list(range(8)))
    return np.concatenate([res.results[c]["outt"] for c in range(8)], axis=1)


# version 04
# speedup vs baseline: 1.2585x; 1.0215x over previous
import numpy as np
import ml_dtypes

import concourse.bass as bass
import concourse.mybir as mybir
from concourse.bass_utils import run_bass_kernel_spmd

F32 = mybir.dt.float32
BF16 = mybir.dt.bfloat16
AF = mybir.ActivationFunctionType
ALU = mybir.AluOpType
AX = mybir.AxisListType
BF = ml_dtypes.bfloat16

DIM = 2048
SEQ = 8192
NH = 4
HD = 512
DH = 512
NAH = 16
AHD = 128
CHUNK = 1024
NCHUNK = SEQ // CHUNK
WIN = 1024
FFN_H = 8192
BASE_LR = 0.01
BASE_LR_INV = float(BASE_LR + np.log(-np.expm1(-BASE_LR)))
NS_COEFFS = [(4.0848, -6.8946, 2.927), (3.9505, -6.3029, 2.6377),
             (3.7418, -5.5913, 2.3037), (2.8769, -3.1427, 1.2046),
             (2.8366, -3.0525, 1.2012)]
NEG_BIG = -1e30


class P:

    def __init__(self, name):
        self.nc = bass.Bass("TRN2", target_bir_lowering=False)
        self.name = name
        self._sem_ct = {}
        self._ctx = []

    def dram(self, name, shape, dtype, kind="Internal"):
        return self.nc.dram_tensor(name, list(shape), dtype, kind=kind).ap()

    def sb(self, name, shape, dtype):
        t = self.nc.sbuf_tensor(name, list(shape), dtype)
        h = t.__enter__()
        self._ctx.append(t)
        return h

    def ps(self, name, shape, dtype=F32):
        t = self.nc.psum_tensor(name, list(shape), dtype)
        h = t.__enter__()
        self._ctx.append(t)
        return h

    def sem(self, name):
        t = self.nc.semaphore(name)
        h = t.__enter__()
        self._ctx.append(t)
        self._sem_ct[h] = 0
        return h

    def close(self):
        for t in reversed(self._ctx):
            t.__exit__(None, None, None)
        self._ctx = []

    def inc(self, instr, sem, n=1):
        instr.then_inc(sem, n)
        self._sem_ct[sem] += n
        return (sem, self._sem_ct[sem])

    def wait(self, eng, tok):
        if tok is None:
            return
        if isinstance(tok, list):
            for t in tok:
                self.wait(eng, t)
            return
        sem, val = tok
        if val > 0:
            eng.wait_ge(sem, val)

    def val(self, sem):
        return self._sem_ct[sem]

    def fence(self, eng, instr, sem):
        tok = self.inc(instr, sem)
        self.wait(eng, tok)
        return tok


def cdiv(a, b):
    return (a + b - 1) // b



AUGW = 6144 + 128
N_OC = AUGW // 128
TBLK = 512


def build_qkv(ttok):
    p = P("qkv")
    nc = p.nc
    xt_d = p.dram("xt", [DIM, ttok], F32, kind="ExternalInput")
    waug_d = p.dram("waug", [DIM, AUGW], F32, kind="ExternalInput")
    qkp_d = p.dram("qkp", [128, 64], F32, kind="ExternalInput")
    vbias_d = p.dram("vbias", [128, 1], F32, kind="ExternalInput")
    qkv_o = p.dram("qkv", [3 * DIM, ttok], BF16, kind="ExternalOutput")
    aqk_o = p.dram("aqk", [2 * DIM, ttok], BF16, kind="ExternalOutput")
    av_o = p.dram("av", [DIM, ttok], BF16, kind="ExternalOutput")
    aux_o = p.dram("aux", [128, ttok], F32, kind="ExternalOutput")

    nblk = ttok // TBLK
    SEGS = [(0, 16), (16, 32), (32, N_OC)]
    MAXSEG = max(e - s for s, e in SEGS)

    xb = [p.sb(f"xb{i}", [128, 16 * TBLK], BF16) for i in range(nblk)]
    xsq = p.sb("xsq", [128, 16 * TBLK], BF16)
    wb = [p.sb(f"wb{i}", [128, 16 * 128 * MAXSEG], BF16) for i in range(1)]
    onesb = p.sb("onesb", [128, 1], BF16)
    onesrow_f = p.sb("onesrow_f", [1, 128], F32)
    cstt = p.sb("cstt", [128, 3], F32)
    qkp = p.sb("qkp_sb", [128, 64], F32)
    vbias = p.sb("vbias_sb", [128, 1], F32)
    colsumT = p.sb("colsumT", [1, 128 * MAXSEG], F32)
    stat = p.sb("stat", [1, 4 * TBLK], F32)
    negm = p.sb("negm", [1, nblk * TBLK], F32)
    rstdb = [p.sb(f"rstdb{i}", [128, TBLK], F32) for i in range(nblk)]
    qf = [p.sb(f"qf{i}", [128, TBLK], F32) for i in range(2)]
    ob = [p.sb(f"ob{i}", [128, TBLK], BF16) for i in range(4)]
    auxb = p.sb("auxb", [128, TBLK], F32)

    mm_ps = [p.ps(f"mm{i}", [128, TBLK], F32) for i in range(4)]
    st_ps = p.ps("st", [1, 2 * TBLK], F32)
    cs_ps = p.ps("cs", [1, 512], F32)
    bc_ps = p.ps("bc", [128, TBLK], F32)

    s_w = p.sem("s_w")
    s_prm = p.sem("s_prm")
    s_x = [p.sem(f"s_x{i}") for i in range(nblk)]
    s_pe = p.sem("s_pe")
    s_ev = p.sem("s_ev")
    s_ac = p.sem("s_ac")
    s_ob = [p.sem(f"s_ob{i}") for i in range(4)]
    s_aux = p.sem("s_aux")
    s_qo = p.sem("s_qo")

    nc.vector.memset(onesb[:], 1.0)
    nc.vector.memset(onesrow_f[:], 1.0)
    nc.vector.memset(cstt[:, 0:1], 1e-5)
    nc.vector.memset(cstt[:, 1:2], -0.5)
    nc.vector.memset(cstt[:, 2:3], BASE_LR_INV)
    consts_rdy = p.inc(nc.vector.memset(auxb[:], 0.0), s_ev)

    prm = p.inc(nc.gpsimd.dma_start(out=qkp[:], in_=qkp_d), s_prm, 16)
    prm = p.inc(nc.gpsimd.dma_start(out=vbias[:], in_=vbias_d), s_prm, 16)
    ld = [None] * nblk
    for b in range(nblk):
        for d in range(16):
            ld[b] = p.inc(nc.gpsimd.dma_start(
                out=xb[b][:, d * TBLK:(d + 1) * TBLK],
                in_=xt_d[d * 128:(d + 1) * 128, b * TBLK:(b + 1) * TBLK]), s_x[b], 16)

    NM, MSQ, VAR, RSTD = 0, 1, 2, 3

    def srow(i):
        return stat[0:1, i * TBLK:(i + 1) * TBLK]

    rstd_rdy = [None] * nblk
    stps_free = None
    bcps_free = None
    for b in range(nblk):
        p.wait(nc.vector, ld[b])
        sq = None
        for d in range(16):
            sq = nc.vector.tensor_mul(xsq[:, d * TBLK:(d + 1) * TBLK],
                                      xb[b][:, d * TBLK:(d + 1) * TBLK],
                                      xb[b][:, d * TBLK:(d + 1) * TBLK])
        sq = p.inc(sq, s_ev)
        p.wait(nc.tensor, sq)
        p.wait(nc.tensor, stps_free)
        for d in range(16):
            nc.tensor.matmul(st_ps[:, 0:TBLK], onesb[:, 0:1],
                             xb[b][:, d * TBLK:(d + 1) * TBLK],
                             start=(d == 0), stop=(d == 15))
        last = None
        for d in range(16):
            last = nc.tensor.matmul(st_ps[:, TBLK:2 * TBLK], onesb[:, 0:1],
                                    xsq[:, d * TBLK:(d + 1) * TBLK],
                                    start=(d == 0), stop=(d == 15))
        t_st = p.inc(last, s_pe)

        p.wait(nc.scalar, t_st)
        nc.scalar.mul(negm[0:1, b * TBLK:(b + 1) * TBLK], st_ps[:, 0:TBLK],
                      -1.0 / DIM)
        a1 = p.inc(nc.scalar.mul(srow(MSQ), st_ps[:, TBLK:2 * TBLK], 1.0 / DIM),
                   s_ac)
        stps_free = a1
        p.wait(nc.vector, a1)
        p.wait(nc.vector, consts_rdy)
        p.fence(nc.vector, nc.vector.tensor_mul(
            srow(VAR), negm[0:1, b * TBLK:(b + 1) * TBLK],
            negm[0:1, b * TBLK:(b + 1) * TBLK]), s_ev)
        v1 = p.inc(nc.vector.tensor_sub(srow(VAR), srow(MSQ), srow(VAR)), s_ev)
        p.wait(nc.scalar, v1)
        p.fence(nc.scalar, nc.scalar.activation(srow(RSTD), srow(VAR), AF.Ln,
                                                bias=cstt[0:1, 0:1]), s_ac)
        a2 = p.inc(nc.scalar.activation(srow(RSTD), srow(RSTD), AF.Exp,
                                        scale=cstt[0:1, 1:2]), s_ac)
        p.wait(nc.tensor, a2)
        p.wait(nc.tensor, bcps_free)
        t_bc = p.inc(nc.tensor.matmul(bc_ps[:, 0:TBLK], onesrow_f[:], srow(RSTD),
                                      start=True, stop=True), s_pe)
        p.wait(nc.scalar, t_bc)
        rstd_rdy[b] = p.inc(nc.scalar.copy(rstdb[b][:], bc_ps[:, 0:TBLK]), s_ac)
        bcps_free = rstd_rdy[b]

    p.wait(nc.vector, prm)
    p.wait(nc.scalar, prm)

    bank_free = [None] * 4
    ob_free = [None] * 4
    qf_free = [None, None]
    aux_free = None
    wseg_free = None
    cs_free = None
    for (js, je) in SEGS:
        segw = (je - js) * 128
        p.wait(nc.gpsimd, wseg_free)
        w_rdy = None
        for d in range(16):
            w_rdy = p.inc(nc.gpsimd.dma_start(
                out=wb[0][:, d * segw:(d + 1) * segw],
                in_=waug_d[d * 128:(d + 1) * 128, js * 128:je * 128]), s_w, 16)

        p.wait(nc.tensor, w_rdy)
        p.wait(nc.tensor, cs_free)
        cs_rdy = None
        for seg in range(cdiv(segw, 512)):
            n0, n1 = seg * 512, min(segw, (seg + 1) * 512)
            p.wait(nc.tensor, cs_rdy)
            last = None
            for d in range(16):
                last = nc.tensor.matmul(cs_ps[:, 0:n1 - n0], onesb[:, 0:1],
                                        wb[0][:, d * segw + n0:d * segw + n1],
                                        start=(d == 0), stop=(d == 15))
            t = p.inc(last, s_pe)
            p.wait(nc.scalar, t)
            cs_rdy = p.inc(nc.scalar.copy(colsumT[0:1, n0:n1],
                                          cs_ps[:, 0:n1 - n0]), s_ac)

        last_mm = None
        for b in range(nblk):
            p.wait(nc.tensor, ld[b])
            p.wait(nc.tensor, cs_rdy)
            for j in range(js, je):
                jj = j - js
                bank = mm_ps[jj % 4]
                p.wait(nc.tensor, bank_free[jj % 4])
                for d in range(16):
                    nc.tensor.matmul(bank[:],
                                     wb[0][:, d * segw + jj * 128:d * segw + (jj + 1) * 128],
                                     xb[b][:, d * TBLK:(d + 1) * TBLK],
                                     start=(d == 0), stop=False)
                t_mm = p.inc(nc.tensor.matmul(
                    bank[:], colsumT[0:1, jj * 128:(jj + 1) * 128],
                    negm[0:1, b * TBLK:(b + 1) * TBLK], start=False, stop=True), s_pe)
                last_mm = t_mm

                f = qf[jj % 2]
                p.wait(nc.vector, t_mm)
                p.wait(nc.vector, rstd_rdy[b])
                p.wait(nc.vector, qf_free[jj % 2])
                v_ev = p.inc(nc.vector.tensor_mul(f[:], bank[:], rstdb[b][:]), s_ev)
                bank_free[jj % 4] = v_ev

                if j < 48:
                    o0i, o1i = (2 * jj) % 4, (2 * jj + 1) % 4
                    o0, o1 = ob[o0i], ob[o1i]
                    p.wait(nc.scalar, v_ev)
                    p.wait(nc.scalar, ob_free[o0i])
                    a_ev = p.inc(nc.scalar.activation(o0[:], f[:], AF.Silu), s_ac)
                    qf_free[jj % 2] = a_ev
                    nc.sync.wait_ge(*a_ev)
                    ob_free[o0i] = p.inc(nc.sync.dma_start(
                        out=qkv_o[j * 128:(j + 1) * 128, b * TBLK:(b + 1) * TBLK],
                        in_=o0[:]), s_ob[o0i], 16)
                    if j < 32:
                        qs = qkp[:, (2 * j):(2 * j) + 1]
                        qo = qkp[:, (2 * j) + 1:(2 * j) + 2]
                        p.wait(nc.vector, ob_free[o1i])
                        p.wait(nc.vector, v_ev)
                        e2 = p.inc(nc.vector.tensor_scalar(o1[:], f[:], qs, qo,
                                                           ALU.mult, ALU.add), s_ev)
                        nc.sync.wait_ge(*e2)
                        ob_free[o1i] = p.inc(nc.sync.dma_start(
                            out=aqk_o[j * 128:(j + 1) * 128,
                                      b * TBLK:(b + 1) * TBLK],
                            in_=o1[:]), s_ob[o1i], 16)
                    else:
                        p.wait(nc.vector, ob_free[o1i])
                        p.wait(nc.vector, v_ev)
                        e2 = p.inc(nc.vector.tensor_copy(o1[:], f[:]), s_ev)
                        nc.sync.wait_ge(*e2)
                        ob_free[o1i] = p.inc(nc.sync.dma_start(
                            out=av_o[(j - 32) * 128:(j - 31) * 128,
                                     b * TBLK:(b + 1) * TBLK],
                            in_=o1[:]), s_ob[o1i], 16)
                else:
                    p.wait(nc.scalar, v_ev)
                    p.wait(nc.scalar, aux_free)
                    p.wait(nc.scalar, consts_rdy)
                    p.fence(nc.scalar, nc.scalar.activation(
                        auxb[0:12, :], f[0:12, :], AF.Exp,
                        bias=cstt[0:12, 2:3]), s_ac)
                    nc.scalar.activation(auxb[0:12, :], auxb[0:12, :], AF.Ln,
                                         bias=1.0)
                    nc.scalar.activation(auxb[32:36, :], f[32:36, :], AF.Sigmoid,
                                         bias=vbias[32:36, 0:1])
                    a_ev = p.inc(nc.scalar.activation(
                        auxb[64:68, :], f[64:68, :], AF.Silu,
                        bias=vbias[64:68, 0:1]), s_ac)
                    qf_free[jj % 2] = a_ev
                    nc.sync.wait_ge(*a_ev)
                    aux_free = p.inc(nc.sync.dma_start(
                        out=aux_o[:, b * TBLK:(b + 1) * TBLK], in_=auxb[:]),
                        s_aux, 16)
        wseg_free = last_mm
        cs_free = last_mm

    for s in [s_qo, s_aux] + s_ob:
        if p.val(s):
            nc.sync.wait_ge(s, p.val(s))
    p.close()
    return p.nc



def _waug(Wqkv, Wlr, Wm, Ws):
    aug = np.zeros((DIM, 128), np.float32)
    aug[:, 0:12] = Wlr
    aug[:, 32:36] = Wm
    aug[:, 64:68] = Ws
    return np.ascontiguousarray(np.concatenate([Wqkv, aug], axis=1)).astype(np.float32)


def _qkp_pack(qk_scale, qk_offset):
    out = np.zeros((128, 64), np.float32)
    for c in range(16):
        out[:, 2 * c] = qk_scale[c * 128:(c + 1) * 128, 0]
        out[:, 2 * c + 1] = qk_offset[c * 128:(c + 1) * 128, 0]
    for c in range(16):
        out[:, 32 + 2 * c] = qk_scale[c * 128:(c + 1) * 128, 1]
        out[:, 32 + 2 * c + 1] = qk_offset[c * 128:(c + 1) * 128, 1]
    return out


def _vbias_pack(bm, bs):
    v = np.zeros((128, 1), np.float32)
    v[32:36, 0] = bm
    v[64:68, 0] = bs
    return v


def run_qkv_stage(inp):
    XT = np.ascontiguousarray(inp["L0_attn_x"][0].T).astype(np.float32)
    WAUG = _waug(inp["L0_attn_Wqkv"], inp["L0_attn_Wlr"],
                 inp["L0_attn_Wm"], inp["L0_attn_Ws"])
    QKP = _qkp_pack(inp["L0_attn_qk_scale"], inp["L0_attn_qk_offset"])
    VB = _vbias_pack(inp["L0_attn_bm"], inp["L0_attn_bs"])
    ttok = SEQ // 8
    nc = build_qkv(ttok)
    in_maps = [{
        "xt": np.ascontiguousarray(XT[:, c * ttok:(c + 1) * ttok]),
        "waug": WAUG, "qkp": QKP, "vbias": VB,
    } for c in range(8)]
    res = run_bass_kernel_spmd(nc, in_maps, core_ids=list(range(8)))
    return {k: np.concatenate([res.results[c][k] for c in range(8)], axis=1)
            for k in ["qkv", "aqk", "av", "aux"]}


def kernel(**inputs):
    inputs = {k: np.asarray(v) for k, v in inputs.items()}
    st1 = run_qkv_stage(inputs)
    st2 = run_chain_stage(st1, inputs)
    st3 = run_attn_stage(st1, st2)
    XT = np.ascontiguousarray(inputs["L0_attn_x"][0].T).astype(np.float32)
    outT = run_tail_stage(st3, inputs, XT)
    return np.ascontiguousarray(outT.T).astype(np.float32)[None]



def build_chain(nchunks=NCHUNK, phases=99):
    import os
    phases = int(os.environ.get("CHAIN_PHASES", phases))
    p = P("chain")
    nc = p.nc
    qh_d = p.dram("qh", [HD, SEQ], BF16, kind="ExternalInput")
    kh_d = p.dram("kh", [HD, SEQ], BF16, kind="ExternalInput")
    vh_d = p.dram("vh", [HD, SEQ], BF16, kind="ExternalInput")
    lrtm_d = p.dram("lrtm", [128, NCHUNK * 32], F32, kind="ExternalInput")
    mom_d = p.dram("momh", [1, SEQ], F32, kind="ExternalInput")
    scl_d = p.dram("sclh", [1, SEQ], F32, kind="ExternalInput")
    w0_d = p.dram("w0", [DH, HD], F32, kind="ExternalInput")
    w1_d = p.dram("w1", [HD, DH], F32, kind="ExternalInput")
    w2_d = p.dram("w2", [DH, HD], F32, kind="ExternalInput")
    gcol_d = p.dram("gcolh", [128, 4], F32, kind="ExternalInput")
    idt_d = p.dram("idt", [128, 128], BF16, kind="ExternalInput")
    diag_d = p.dram("diag", [128, 5 * 128], BF16, kind="ExternalInput")
    ttt_o = p.dram("ttt", [HD, SEQ], BF16, kind="ExternalOutput")

    C = CHUNK
    NT = C // 128

    qn = p.sb("qn", [128, 4 * C], BF16)
    kn = p.sb("kn", [128, 4 * C], BF16)
    vv = p.sb("vv", [128, 4 * C], BF16)
    kTM = p.sb("kTM", [128, NT * 512], BF16)
    vTM = p.sb("vTM", [128, NT * 512], BF16)
    momr = p.sb("momr", [1, C], BF16)
    sclr = p.sb("sclr", [1, C], F32)
    lrcf = [p.sb(f"lrcf{i}", [128, NT * 4], F32) for i in range(2)]
    mmc = p.sb("mmc", [128, 1], F32)
    rnc = p.sb("rnc", [128, 1], F32)
    W = [p.sb(f"W{m}", [128, 4 * 512], F32) for m in range(3)]
    Mb = [p.sb(f"Mb{m}", [128, 4 * 512], BF16) for m in range(3)]
    Wb = [p.sb(f"Wb{m}", [128, 4 * 512], BF16) for m in range(3)]
    WTb = [p.sb(f"WTb{m}", [128, 4 * 512], BF16) for m in range(3)]
    ncol = p.sb("ncol", [128, 12], F32)
    rcol = p.sb("rcol", [128, 8], F32)
    dwf = p.sb("dwf", [128, 4 * 512], F32)
    X5f = p.sb("X5f", [128, 4 * 512], F32)
    Xb = [p.sb(f"Xb{i}", [128, 4 * 512], BF16) for i in range(2)]
    XTb = [p.sb(f"XTb{i}", [128, 4 * 512], BF16) for i in range(2)]
    Ab = p.sb("Ab", [128, 4 * 512], BF16)
    Acb = p.sb("Acb", [128, 4 * 512], BF16)
    hidT = p.sb("hidT", [128, 4 * C], BF16)
    gateT = p.sb("gateT", [128, 4 * C], BF16)
    outb = p.sb("outb", [128, 4 * C], BF16)
    onesrow_b = p.sb("onesrow_b", [1, 128], BF16)
    dgb = gateT
    dhb = hidT
    hl1 = outb
    klb = [p.sb(f"klb{i}", [128, 512], BF16) for i in range(2)]
    gsc = [p.sb(f"gsc{i}", [128, 1024], F32) for i in range(5)]
    rowA = p.sb("rowA", [1, C], F32)
    rowB = p.sb("rowB", [1, C], F32)
    rqb = p.sb("rqb", [128, C], BF16)
    rsgb = p.sb("rsgb", [128, C], BF16)
    gcol = p.sb("gcol", [128, 4], F32)
    onesb = p.sb("onesb", [128, 1], BF16)
    onesf = p.sb("onesf", [128, 1], F32)
    onesrow_f = p.sb("onesrow_f", [1, 128], F32)
    idt = p.sb("idt_sb", [128, 128], BF16)
    diag = p.sb("diag_sb", [128, 5 * 128], BF16)
    cst = p.sb("cst", [128, 4], F32)
    srow = p.sb("srow_sb", [1, 8], F32)

    PSA = p.ps("PSA", [128, 2048], F32)
    PS = [PSA[:, j * 512:(j + 1) * 512] for j in range(4)]
    PS4 = p.ps("PS4", [128, 512], F32)
    PS5 = p.ps("PS5", [128, 512], F32)
    PT = [p.ps(f"PTb{i}", [128, 1024], BF16) for i in range(2)]

    s_in = [p.sem(f"s_in{i}") for i in range(2)]
    s_w = p.sem("s_wl")
    s_pe = p.sem("s_pel")
    s_ev = p.sem("s_evl")
    s_ac = p.sem("s_acl")
    s_out = p.sem("s_outl")

    V, SC, T = nc.vector, nc.scalar, nc.tensor

    ps_free = [None] * 4
    ps4_free = None
    pt_free = [[None] * 8, [None] * 8]
    pt_ctr = [0]

    def bcast_col(row_ap, out_col):
        nonlocal ps4_free
        p.wait(nc.tensor, ps4_free)
        t = p.inc(T.matmul(PS4[:, 0:1], onesrow_f[:], row_ap, start=True, stop=True),
                  s_pe)
        p.wait(nc.scalar, t)
        tok = p.inc(SC.copy(out_col, PS4[:, 0:1]), s_ac)
        ps4_free = tok
        return tok

    def bcast_row(row_f32, out_tile, n, rdy=None):
        nonlocal ps_free
        toks = []
        p.wait(nc.tensor, rdy)
        for h in range(cdiv(n, 512)):
            n0, n1 = h * 512, min(n, (h + 1) * 512)
            p.wait(nc.tensor, ps_free[h])
            t = p.inc(T.matmul(PS[h][0:128, 0:n1 - n0], onesrow_f[:],
                               row_f32[0:1, n0:n1], start=True, stop=True), s_pe)
            p.wait(nc.vector, t)
            tok = p.inc(V.tensor_copy(out_tile[:, n0:n1], PS[h][0:128, 0:n1 - n0]), s_ev)
            ps_free[h] = tok
            toks.append(tok)
        return toks[-1]

    pt_bank_free = [None, None]

    def transpose_batch(pairs):
        tok = None
        bank = pt_ctr[0] % 2
        i = 0
        while i < len(pairs):
            grp = pairs[i:i + 8]
            i += 8
            pt_ctr[0] += 1
            bank = 1 - bank
            p.wait(nc.tensor, pt_bank_free[bank])
            tt = None
            for r, (s_ap, d_ap, npart) in enumerate(grp):
                tt = p.inc(T.transpose(
                    PT[bank][:, r * 128:r * 128 + s_ap.shape[0]], s_ap,
                    idt[0:npart, 0:npart] if npart < 128 else idt[:]), s_pe)
            p.wait(nc.vector, tt)
            for r, (s_ap, d_ap, npart) in enumerate(grp):
                tok = p.inc(V.tensor_copy(
                    d_ap, PT[bank][:, r * 128:r * 128 + s_ap.shape[0]]), s_ev)
            pt_bank_free[bank] = tok
        return tok

    def transpose512(src_bf, dst_bf, rdy_tok):
        p.wait(nc.tensor, rdy_tok)
        pairs = []
        for j in range(4):
            for i in range(4):
                pairs.append((src_bf[:, j * 512 + i * 128:j * 512 + (i + 1) * 128],
                              dst_bf[:, i * 512 + j * 128:i * 512 + (j + 1) * 128],
                              128))
        return transpose_batch(pairs)

    nc.vector.memset(cst[:, 0:1], 1e-5)
    nc.vector.memset(cst[:, 1:2], -0.5)
    nc.vector.memset(cst[:, 2:3], 1e-7)
    nc.vector.memset(onesb[:], 1.0)
    nc.vector.memset(onesf[:], 1.0)
    nc.vector.memset(onesrow_f[:], 1.0)
    nc.vector.memset(onesrow_b[:], 1.0)
    for m in range(3):
        nc.vector.memset(Mb[m][:], 0.0)
    ini = p.inc(nc.vector.memset(outb[:], 0.0), s_ev)

    wl = None
    for m, wd in enumerate([w0_d, w1_d, w2_d]):
        for j in range(4):
            wl = p.inc(nc.gpsimd.dma_start(
                out=W[m][:, j * 512:(j + 1) * 512],
                in_=wd[j * 128:(j + 1) * 128, :]), s_w, 16)
    wl = p.inc(nc.gpsimd.dma_start(out=idt[:], in_=idt_d), s_w, 16)
    wl = p.inc(nc.gpsimd.dma_start(out=diag[:], in_=diag_d), s_w, 16)
    wl = p.inc(nc.gpsimd.dma_start(out=gcol[:], in_=gcol_d), s_w, 16)

    p.wait(nc.vector, wl)
    p.wait(nc.scalar, wl)
    cast0 = None
    for m in range(3):
        cast0 = p.inc(V.tensor_copy(Wb[m][:], W[m][:]), s_ev)
    nrm = None
    for m in range(3):
        for j in range(4):
            nrm = p.fence(nc.scalar, SC.activation(
                gsc[0][:, 0:512], W[m][:, j * 512:(j + 1) * 512], AF.Square,
                accum_out=ncol[:, 4 * m + j:4 * m + j + 1]), s_ac)
    nrm = p.inc(SC.sqrt(ncol[:, 0:12], ncol[:, 0:12]), s_ac)

    gc = wl

    for m in range(3):
        wt0 = transpose512(Wb[m], WTb[m], cast0)
    prev_wt = wt0

    prev_grad_done = None
    prev_chunk_end = None
    ldq_next = None
    for c in range(nchunks):
        par = c % 2
        sin = s_in[par]
        if c == 0:
            p.wait(nc.gpsimd, None)
            ldq = None
            for oc in range(4):
                for (dst, srcd) in ((qn, qh_d), (kn, kh_d), (vv, vh_d)):
                    ldq = p.inc(nc.gpsimd.dma_start(
                        out=dst[:, oc * C:(oc + 1) * C],
                        in_=srcd[oc * 128:(oc + 1) * 128, 0:C]), sin, 16)
            ldq = p.inc(nc.gpsimd.dma_start(out=lrcf[0][:], in_=lrtm_d[:, 0:32]), sin, 16)
            ldq = p.inc(nc.gpsimd.dma_start(out=momr[:], in_=mom_d[:, 0:C]), sin, 16)
            ldq = p.inc(nc.gpsimd.dma_start(out=sclr[:], in_=scl_d[:, 0:C]), sin, 16)
        else:
            ldq = ldq_next

        for tgt in ((qn, kn) if phases >= 1 else ()):
            p.wait(nc.vector, ldq)
            sq = None
            for oc in range(4):
                sq = V.tensor_mul(hidT[:, oc * C:(oc + 1) * C],
                                  tgt[:, oc * C:(oc + 1) * C],
                                  tgt[:, oc * C:(oc + 1) * C])
            sq = p.inc(sq, s_ev)
            p.wait(nc.tensor, sq)
            p.wait(nc.tensor, ini)
            last = None
            for h in range(2):
                p.wait(nc.tensor, ps_free[h])
                for oc in range(4):
                    last = T.matmul(PS[h][0:1, :], onesb[:, 0:1],
                                    hidT[:, oc * C + h * 512:oc * C + (h + 1) * 512],
                                    start=(oc == 0), stop=(oc == 3))
            tsq = p.inc(last, s_pe)
            p.wait(nc.scalar, tsq)
            a = None
            for h in range(2):
                a = p.fence(nc.scalar, SC.sqrt(rowA[0:1, h * 512:(h + 1) * 512],
                                               PS[h][0:1, :]), s_ac)
                ps_free[h] = a
            p.wait(nc.vector, a)
            rb = p.fence(nc.vector, V.tensor_scalar_add(rowA[0:1, :], rowA[0:1, :],
                                                        1e-5), s_ev)
            rb = p.fence(nc.vector, V.reciprocal(rowB[0:1, :], rowA[0:1, :]), s_ev)
            bt_ = bcast_row(rowB, rqb, C, rb)
            p.wait(nc.vector, bt_)
            rq = None
            for oc in range(4):
                rq = V.tensor_mul(tgt[:, oc * C:(oc + 1) * C],
                                  tgt[:, oc * C:(oc + 1) * C], rqb[:])
            nrm_done = p.inc(rq, s_ev)

        tok = None
        for (src, dst) in (((kn, kTM), (vv, vTM)) if phases >= 2 else ()):
            p.wait(nc.tensor, nrm_done if src is kn else ldq)
            prs = []
            for tc in range(NT):
                for oc in range(4):
                    prs.append((src[:, oc * C + tc * 128:oc * C + (tc + 1) * 128],
                                dst[:, tc * 512 + oc * 128:tc * 512 + (oc + 1) * 128],
                                128))
            tok = transpose_batch(prs)
        ktm_done = tok

        lrc = ldq
        lrcf_t = ldq
        lrcfc = lrcf[par]
        momrc = momr
        sclrc = sclr
        if phases < 3:
            for oc in range(4):
                nc.sync.wait_ge(*(ldq if phases >= 1 else ldq))
                od = p.inc(nc.sync.dma_start(
                    out=ttt_o[oc * 128:(oc + 1) * 128, c * C:(c + 1) * C],
                    in_=qn[:, oc * C:(oc + 1) * C]), s_out, 16)
            continue
        p.wait(nc.vector, ldq)
        mm1 = p.fence(nc.vector, V.reduce_sum(rowA[0:1, 0:1], momrc[:], axis=AX.X), s_ev)
        p.wait(nc.scalar, mm1)
        mm2 = p.inc(SC.mul(srow[0:1, 0:1], rowA[0:1, 0:1], 1.0 / C), s_ac)
        p.wait(nc.tensor, mm2)
        mmr = bcast_col(srow[0:1, 0:1], mmc[:, 0:1])

        if phases < 4:
            for oc in range(4):
                nc.sync.wait_ge(*lrc)
                od = p.inc(nc.sync.dma_start(
                    out=ttt_o[oc * 128:(oc + 1) * 128, c * C:(c + 1) * C],
                    in_=qn[:, oc * C:(oc + 1) * C]), s_out, 16)
            continue
        p.wait(nc.tensor, prev_wt)
        hv = gv = None
        for oc in range(4):
            for half in range(2):
                u = oc * 2 + half
                bh = u % 2
                bg = 2 + u % 2
                p.wait(nc.tensor, ps_free[bh])
                last = None
                for dch in range(4):
                    last = T.matmul(PS[bh][:],
                                    WTb[2][:, dch * 512 + oc * 128:dch * 512 + (oc + 1) * 128],
                                    qn[:, dch * C + half * 512:dch * C + (half + 1) * 512],
                                    start=(dch == 0), stop=(dch == 3))
                th = p.inc(last, s_pe)
                p.wait(nc.tensor, ps_free[bg])
                last = None
                for dch in range(4):
                    last = T.matmul(PS[bg][:],
                                    WTb[0][:, dch * 512 + oc * 128:dch * 512 + (oc + 1) * 128],
                                    qn[:, dch * C + half * 512:dch * C + (half + 1) * 512],
                                    start=(dch == 0), stop=(dch == 3))
                tg2 = p.inc(last, s_pe)
                p.wait(nc.vector, th)
                hv = p.inc(V.tensor_copy(
                    hidT[:, oc * C + half * 512:oc * C + (half + 1) * 512],
                    PS[bh][:]), s_ev)
                ps_free[bh] = hv
                p.wait(nc.scalar, tg2)
                gv = p.inc(SC.activation(
                    gateT[:, oc * C + half * 512:oc * C + (half + 1) * 512],
                    PS[bg][:], AF.Silu), s_ac)
                ps_free[bg] = gv
        p.wait(nc.vector, gv)
        p.wait(nc.vector, hv)
        gh = None
        for oc in range(4):
            gh = V.tensor_mul(gateT[:, oc * C:(oc + 1) * C],
                              gateT[:, oc * C:(oc + 1) * C],
                              hidT[:, oc * C:(oc + 1) * C])
        gh = p.inc(gh, s_ev)
        p.wait(nc.tensor, gh)
        ov = None
        for oc in range(4):
            for half in range(2):
                bank = (oc * 2 + half) % 4
                p.wait(nc.tensor, ps_free[bank])
                last = None
                for hch in range(4):
                    last = T.matmul(PS[bank][:],
                                    WTb[1][:, hch * 512 + oc * 128:hch * 512 + (oc + 1) * 128],
                                    gateT[:, hch * C + half * 512:hch * C + (half + 1) * 512],
                                    start=(hch == 0), stop=(hch == 3))
                to = p.inc(last, s_pe)
                last_out_mm = to
                p.wait(nc.vector, to)
                ov = p.inc(V.tensor_copy(
                    outb[:, oc * C + half * 512:oc * C + (half + 1) * 512],
                    PS[bank][:]), s_ev)
                ps_free[bank] = ov

        if phases < 5:
            for oc in range(4):
                nc.sync.wait_ge(*ov)
                od = p.inc(nc.sync.dma_start(
                    out=ttt_o[oc * 128:(oc + 1) * 128, c * C:(c + 1) * C],
                    in_=outb[:, oc * C:(oc + 1) * C]), s_out, 16)
            continue
        p.wait(nc.vector, ov)
        sq = None
        for oc in range(4):
            sq = V.tensor_mul(hidT[:, oc * C:(oc + 1) * C],
                              outb[:, oc * C:(oc + 1) * C],
                              outb[:, oc * C:(oc + 1) * C])
        sq = p.inc(sq, s_ev)
        p.wait(nc.tensor, sq)
        last = None
        for h in range(2):
            p.wait(nc.tensor, ps_free[h])
            for oc in range(4):
                last = T.matmul(PS[h][0:1, :], onesb[:, 0:1],
                                hidT[:, oc * C + h * 512:oc * C + (h + 1) * 512],
                                start=(oc == 0), stop=(oc == 3))
        tms = p.inc(last, s_pe)
        p.wait(nc.scalar, tms)
        a = None
        for h in range(2):
            a = p.fence(nc.scalar, SC.activation(rowA[0:1, h * 512:(h + 1) * 512],
                                                 PS[h][0:1, :], AF.Ln,
                                                 bias=cst[0:1, 0:1], scale=1.0 / HD), s_ac)
            ps_free[h] = a
        a = p.fence(nc.scalar, SC.activation(rowA[0:1, :], rowA[0:1, :], AF.Exp,
                                             scale=cst[0:1, 1:2]), s_ac)
        p.wait(nc.vector, a)
        rs = p.fence(nc.vector, V.tensor_mul(rowB[0:1, :], rowA[0:1, :], sclrc[:]), s_ev)
        bt2 = bcast_row(rowB, rsgb, C, rs)
        p.wait(nc.vector, bt2)
        p.wait(nc.vector, gc)
        fin = None
        for oc in range(4):
            f1 = p.fence(nc.vector, V.tensor_scalar(
                outb[:, oc * C:(oc + 1) * C], outb[:, oc * C:(oc + 1) * C],
                gcol[:, oc:oc + 1], None, ALU.mult), s_ev)
            fin = p.fence(nc.vector, V.tensor_mul(
                outb[:, oc * C:(oc + 1) * C], outb[:, oc * C:(oc + 1) * C],
                rsgb[:]), s_ev)
        for oc in range(4):
            nc.sync.wait_ge(*fin)
            od = p.inc(nc.sync.dma_start(
                out=ttt_o[oc * 128:(oc + 1) * 128, c * C:(c + 1) * C],
                in_=outb[:, oc * C:(oc + 1) * C]), s_out, 16)

        if phases < 6:
            continue
        ge = None
        for tp in range(NT // 2):
            for half in range(2):
                tc = tp * 2 + half
                for bi, wt in ((0, WTb[0]), (1, WTb[2])):
                    p.wait(nc.tensor, ps_free[half * 2 + bi] if half * 2 + bi < 4 else None)
                    last = None
                    for dch in range(4):
                        last = T.matmul(PS[half * 2 + bi][:],
                                        kn[:, dch * C + tc * 128:dch * C + (tc + 1) * 128],
                                        wt[:, dch * 512:(dch + 1) * 512],
                                        start=(dch == 0), stop=(dch == 3))
                    p.inc(last, s_pe)
            p.wait(nc.tensor, ps4_free)
            last2 = None
            for half in range(2):
                tc = tp * 2 + half
                bankap = PS4[:] if half == 0 else PS5[:]
                for och in range(4):
                    last2 = T.matmul(bankap,
                                     vv[:, och * C + tc * 128:och * C + (tc + 1) * 128],
                                     Wb[1][:, och * 512:(och + 1) * 512],
                                     start=(och == 0), stop=(och == 3))
            tgrp = p.inc(last2, s_pe)

            HBM, DH_, S_, SG, HID = 0, 1, 2, 3, 4
            t0 = tp * 2
            p.wait(nc.scalar, tgrp)
            p.wait(nc.scalar, ge)
            sA = p.fence(nc.scalar, SC.activation(gsc[S_][:, 0:512], PS[0][:],
                                                  AF.Sigmoid), s_ac)
            sA = p.fence(nc.scalar, SC.activation(gsc[S_][:, 512:1024], PS[2][:],
                                                  AF.Sigmoid), s_ac)
            p.wait(nc.vector, tgrp)
            p.wait(nc.vector, ge)
            p.wait(nc.vector, sA)
            sG = p.fence(nc.vector, V.tensor_mul(gsc[SG][:, 0:512],
                                                 gsc[S_][:, 0:512], PS[0][:]), s_ev)
            sG = p.fence(nc.vector, V.tensor_mul(gsc[SG][:, 512:1024],
                                                 gsc[S_][:, 512:1024], PS[2][:]), s_ev)
            e2 = p.fence(nc.vector, V.tensor_copy(gsc[HBM][:, 0:512], PS[1][:]), s_ev)
            e2 = p.fence(nc.vector, V.tensor_copy(gsc[HBM][:, 512:1024], PS[3][:]), s_ev)
            e3 = p.fence(nc.vector, V.tensor_copy(gsc[DH_][:, 0:512], PS4[:]), s_ev)
            e3 = p.fence(nc.vector, V.tensor_copy(gsc[DH_][:, 512:1024], PS5[:]), s_ev)
            for j in range(4):
                ps_free[j] = e2
            ps4_free = e3
            if tp == 0:
                p.wait(nc.vector, od)
                p.wait(nc.vector, tms)
                p.wait(nc.vector, last_out_mm)
            hid_ = p.fence(nc.vector, V.tensor_mul(gsc[HID][:], gsc[SG][:],
                                                   gsc[HBM][:]), s_ev)
            p.wait(nc.vector, lrcf_t)
            h1 = p.fence(nc.vector, V.tensor_scalar(
                hl1[:, t0 * 512:(t0 + 1) * 512], gsc[HID][:, 0:512],
                lrcfc[:, t0 * 4 + 1:t0 * 4 + 2], None, ALU.mult), s_ev)
            h1 = p.fence(nc.vector, V.tensor_scalar(
                hl1[:, (t0 + 1) * 512:(t0 + 2) * 512], gsc[HID][:, 512:1024],
                lrcfc[:, (t0 + 1) * 4 + 1:(t0 + 1) * 4 + 2], None, ALU.mult), s_ev)
            db = p.fence(nc.vector, V.tensor_mul(
                dhb[:, t0 * 512:(t0 + 2) * 512], gsc[DH_][:], gsc[SG][:]), s_ev)
            a_ = p.fence(nc.vector, V.tensor_mul(gsc[HID][:], gsc[SG][:],
                                                 gsc[S_][:]), s_ev)
            a_ = p.fence(nc.vector, V.tensor_sub(gsc[HID][:], gsc[SG][:],
                                                 gsc[HID][:]), s_ev)
            a_ = p.fence(nc.vector, V.tensor_add(gsc[HID][:], gsc[S_][:],
                                                 gsc[HID][:]), s_ev)
            t1 = p.fence(nc.vector, V.tensor_mul(gsc[SG][:], gsc[DH_][:],
                                                 gsc[HBM][:]), s_ev)
            ge = p.fence(nc.vector, V.tensor_mul(
                dgb[:, t0 * 512:(t0 + 2) * 512], gsc[SG][:], gsc[HID][:]), s_ev)
        grad_done = ge
        if c + 1 < nchunks:
            npar = (c + 1) % 2
            nin = s_in[npar]
            p.wait(nc.gpsimd, grad_done)
            ldq_next = None
            for oc in range(4):
                for (dst, srcd) in ((qn, qh_d), (kn, kh_d), (vv, vh_d)):
                    ldq_next = p.inc(nc.gpsimd.dma_start(
                        out=dst[:, oc * C:(oc + 1) * C],
                        in_=srcd[oc * 128:(oc + 1) * 128, (c + 1) * C:(c + 2) * C]),
                        nin, 16)
            ldq_next = p.inc(nc.gpsimd.dma_start(
                out=lrcf[npar][:], in_=lrtm_d[:, (c + 1) * 32:(c + 2) * 32]), nin, 16)
            ldq_next = p.inc(nc.gpsimd.dma_start(
                out=momr[:], in_=mom_d[:, (c + 1) * C:(c + 2) * C]), nin, 16)
            ldq_next = p.inc(nc.gpsimd.dma_start(
                out=sclr[:], in_=scl_d[:, (c + 1) * C:(c + 2) * C]), nin, 16)

        for m in (range(3) if phases >= 7 else ()):
            lhs_slab = [dgb, vTM, dhb][m]
            p.wait(nc.tensor, grad_done)
            p.wait(nc.tensor, ktm_done)
            for b in range(4):
                p.wait(nc.tensor, ps_free[b])
            tdw = None
            tdw_prev = [None, None]
            kls = [None, None]
            if m != 1:
                li = 0 if m == 0 else 2
                kls[0] = p.fence(nc.vector, V.tensor_scalar(
                    klb[0][:], kTM[:, 0:512],
                    lrcfc[:, li:li + 1], None, ALU.mult), s_ev)
            for tc in range(NT):
                kp = tc % 2
                if m != 1:
                    li = 0 if m == 0 else 2
                    if tc + 1 < NT:
                        p.wait(nc.vector, tdw_prev[1 - kp])
                        kls[1 - kp] = p.fence(nc.vector, V.tensor_scalar(
                            klb[1 - kp][:], kTM[:, (tc + 1) * 512:(tc + 2) * 512],
                            lrcfc[:, (tc + 1) * 4 + li:(tc + 1) * 4 + li + 1], None,
                            ALU.mult), s_ev)
                    p.wait(nc.tensor, kls[kp])
                    rhs_ap = klb[kp][:]
                else:
                    rhs_ap = hl1[:, tc * 512:(tc + 1) * 512]
                last = None
                for mc in range(4):
                    last = T.matmul(PS[mc][:],
                                    lhs_slab[:, tc * 512 + mc * 128:tc * 512 + (mc + 1) * 128],
                                    rhs_ap,
                                    start=(tc == 0), stop=(tc == NT - 1))
                tdw = p.inc(last, s_pe)
                tdw_prev[kp] = tdw

            p.wait(nc.vector, tdw)
            p.wait(nc.scalar, mmr)
            ev = p.fence(nc.vector, V.scalar_tensor_tensor(
                dwf[:], Mb[m][:], mmc[:, 0:1], PSA[:], ALU.mult, ALU.add), s_ev)
            for j in range(4):
                ps_free[j] = ev
            mnew = p.fence(nc.vector, V.tensor_copy(Mb[m][:], dwf[:]), s_ev)

            p.wait(nc.scalar, mnew)
            aq2 = None
            for j in range(4):
                aq2 = p.fence(nc.scalar, SC.activation(
                    gsc[0][:, 0:512], dwf[:, j * 512:(j + 1) * 512], AF.Square,
                    accum_out=rcol[:, j:j + 1]), s_ac)
            p.wait(nc.tensor, aq2)
            p.wait(nc.tensor, ps4_free)
            tnf = p.inc(T.matmul(PS4[0:1, 0:4], onesf[:, 0:1], rcol[:, 0:4],
                                 start=True, stop=True), s_pe)
            p.wait(nc.vector, tnf)
            nf = p.fence(nc.vector, V.reduce_sum(srow[0:1, 0:1], PS4[0:1, 0:4],
                                                 axis=AX.X), s_ev)
            ps4_free = nf
            p.wait(nc.scalar, nf)
            nf2 = p.fence(nc.scalar, SC.sqrt(srow[0:1, 1:2], srow[0:1, 0:1]), s_ac)
            p.wait(nc.vector, nf2)
            nf3 = p.fence(nc.vector, V.tensor_scalar_add(srow[0:1, 2:3], srow[0:1, 1:2],
                                                         1e-7), s_ev)
            nf4 = p.fence(nc.vector, V.reciprocal(srow[0:1, 3:4], srow[0:1, 2:3]), s_ev)
            p.wait(nc.tensor, nf4)
            rno = bcast_col(srow[0:1, 3:4], rnc[:, 0:1])

            p.wait(nc.vector, rno)
            x0 = None
            for j in range(4):
                x0 = p.fence(nc.vector, V.tensor_scalar(
                    Xb[0][:, j * 512:(j + 1) * 512], dwf[:, j * 512:(j + 1) * 512],
                    rnc[:, 0:1], None, ALU.mult), s_ev)
            x0t = transpose512(Xb[0], XTb[0], x0)

            cx = 0
            prev_ev = x0t
            x5 = None
            for it, (ca, cb_, cc_) in enumerate(NS_COEFFS):
                XB, XT_ = Xb[cx], XTb[cx]
                XB2, XT2 = Xb[1 - cx], XTb[1 - cx]
                p.wait(nc.tensor, prev_ev)
                for b in range(4):
                    p.wait(nc.tensor, ps_free[b])
                last = None
                for ic in range(4):
                    for dch in range(4):
                        last = T.matmul(PS[ic][:],
                                        XT_[:, dch * 512 + ic * 128:dch * 512 + (ic + 1) * 128],
                                        XT_[:, dch * 512:(dch + 1) * 512],
                                        start=(dch == 0), stop=(dch == 3))
                tA = p.inc(last, s_pe)
                p.wait(nc.scalar, tA)
                p.wait(nc.vector, tA)
                SC.copy(Ab[:, 0:1024], PSA[:, 0:1024])
                ab = p.inc(SC.activation(Acb[:, 0:1024], PSA[:, 0:1024], AF.Copy,
                                         scale=float(cc_)), s_ac)
                V.tensor_copy(Ab[:, 1024:2048], PSA[:, 1024:2048])
                acb = p.inc(V.tensor_scalar(Acb[:, 1024:2048], PSA[:, 1024:2048],
                                            float(cc_), None, ALU.mult), s_ev)
                p.wait(nc.tensor, ab)
                p.wait(nc.tensor, acb)
                last = None
                for ic in range(4):
                    for jch in range(4):
                        last = T.matmul(PS[ic][:],
                                        Ab[:, jch * 512 + ic * 128:jch * 512 + (ic + 1) * 128],
                                        Acb[:, jch * 512:(jch + 1) * 512],
                                        start=(jch == 0), stop=(jch == 3))
                tA2 = p.inc(last, s_pe)
                p.wait(nc.vector, tA2)
                bb_ = p.fence(nc.vector, V.scalar_tensor_tensor(
                    Acb[:], Ab[:], float(cb_), PSA[:], ALU.mult, ALU.add), s_ev)
                p.wait(nc.tensor, bb_)
                last = None
                for ic in range(4):
                    for jch in range(4):
                        T.matmul(PS[ic][:],
                                 Acb[:, jch * 512 + ic * 128:jch * 512 + (ic + 1) * 128],
                                 XB[:, jch * 512:(jch + 1) * 512],
                                 start=(jch == 0), stop=False)
                    last = T.matmul(PS[ic][:], diag[:, it * 128:(it + 1) * 128],
                                    XB[:, ic * 512:(ic + 1) * 512],
                                    start=False, stop=True)
                tBX = p.inc(last, s_pe)
                if it == 4:
                    p.wait(nc.scalar, tBX)
                    p.wait(nc.vector, tBX)
                    x5a = p.inc(SC.copy(X5f[:, 0:1024], PSA[:, 0:1024]), s_ac)
                    x5 = p.inc(V.tensor_copy(X5f[:, 1024:2048], PSA[:, 1024:2048]), s_ev)
                    x5 = [x5a, x5]
                    for j in range(4):
                        ps_free[j] = x5
                else:
                    p.wait(nc.scalar, tBX)
                    p.wait(nc.vector, tBX)
                    xba = p.inc(SC.copy(XB2[:, 0:1024], PSA[:, 0:1024]), s_ac)
                    xbb = p.inc(V.tensor_copy(XB2[:, 1024:2048], PSA[:, 1024:2048]), s_ev)
                    xb2 = [xba, xbb]
                    p.wait(nc.tensor, xb2)
                    last = None
                    for dc in range(4):
                        for jch in range(4):
                            T.matmul(PS[dc][:],
                                     XB[:, jch * 512 + dc * 128:jch * 512 + (dc + 1) * 128],
                                     Acb[:, jch * 512:(jch + 1) * 512],
                                     start=(jch == 0), stop=False)
                        last = T.matmul(PS[dc][:], diag[:, it * 128:(it + 1) * 128],
                                        XT_[:, dc * 512:(dc + 1) * 512],
                                        start=False, stop=True)
                    tXT = p.inc(last, s_pe)
                    p.wait(nc.vector, tXT)
                    p.wait(nc.scalar, tXT)
                    xta = p.inc(SC.copy(XT2[:, 0:1024], PSA[:, 0:1024]), s_ac)
                    xtb = p.inc(V.tensor_copy(XT2[:, 1024:2048], PSA[:, 1024:2048]), s_ev)
                    xt2 = [xta, xtb]
                    for j in range(4):
                        ps_free[j] = xt2
                    prev_ev = xt2
                    cx = 1 - cx

            p.wait(nc.vector, x5)
            tw = p.fence(nc.vector, V.tensor_add(dwf[:], W[m][:], X5f[:]), s_ev)
            p.wait(nc.scalar, tw)
            p.wait(nc.scalar, nrm)
            rn = None
            for j in range(4):
                rn = p.fence(nc.scalar, SC.activation(
                    gsc[0][:, 0:512], dwf[:, j * 512:(j + 1) * 512], AF.Square,
                    accum_out=rcol[:, j:j + 1]), s_ac)
            rn = p.fence(nc.scalar, SC.sqrt(rcol[:, 0:4], rcol[:, 0:4]), s_ac)
            p.wait(nc.vector, rn)
            sc1 = p.fence(nc.vector, V.tensor_scalar_add(rcol[:, 0:4], rcol[:, 0:4],
                                                         1e-5), s_ev)
            sc2 = p.fence(nc.vector, V.reciprocal(rcol[:, 4:8], rcol[:, 0:4]), s_ev)
            sc3 = p.fence(nc.vector, V.tensor_mul(rcol[:, 4:8], rcol[:, 4:8],
                                                  ncol[:, 4 * m:4 * m + 4]), s_ev)
            wn = None
            for j in range(4):
                wn = p.fence(nc.vector, V.tensor_scalar(
                    W[m][:, j * 512:(j + 1) * 512], dwf[:, j * 512:(j + 1) * 512],
                    rcol[:, 4 + j:5 + j], None, ALU.mult), s_ev)
            wb2 = p.fence(nc.vector, V.tensor_copy(Wb[m][:], W[m][:]), s_ev)
            chunk_last = transpose512(Wb[m], WTb[m], wb2)
        if phases >= 7:
            prev_wt = chunk_last

    nc.sync.wait_ge(s_out, p.val(s_out))
    p.close()
    return p.nc


def _lr_tm(lr3):
    out = np.zeros((128, NCHUNK * 32), np.float32)
    a = np.asarray(lr3, np.float32).reshape(3, NCHUNK, 8, 128)
    for c in range(NCHUNK):
        for tc in range(8):
            for i in range(3):
                out[:, c * 32 + tc * 4 + i] = a[i, c, tc]
    return out


def _chain_inputs(st1, inp, head):
    return {
        "qh": np.ascontiguousarray(st1["qkv"][head * HD:(head + 1) * HD]),
        "kh": np.ascontiguousarray(st1["qkv"][2048 + head * HD:2048 + (head + 1) * HD]),
        "vh": np.ascontiguousarray(st1["qkv"][4096 + head * HD:4096 + (head + 1) * HD]),
        "lrtm": _lr_tm(st1["aux"][3 * head:3 * head + 3]),
        "momh": np.ascontiguousarray(st1["aux"][32 + head:33 + head]).astype(np.float32),
        "sclh": np.ascontiguousarray(st1["aux"][64 + head:65 + head]).astype(np.float32),
        "w0": np.ascontiguousarray(inp["L0_attn_w0"][head]).astype(np.float32),
        "w1": np.ascontiguousarray(inp["L0_attn_w1"][head]).astype(np.float32),
        "w2": np.ascontiguousarray(inp["L0_attn_w2"][head]).astype(np.float32),
        "gcolh": np.ascontiguousarray(
            inp["L0_attn_o_norm_g"].reshape(4, 128).T).astype(np.float32),
        "idt": np.eye(128).astype(BF),
        "diag": np.concatenate([np.eye(128) * a for (a, b, c) in NS_COEFFS],
                               axis=1).astype(BF),
    }


def run_chain_stage(st1, inp):
    nc = build_chain()
    in_maps = [_chain_inputs(st1, inp, c % 4) for c in range(8)]
    res = run_bass_kernel_spmd(nc, in_maps, core_ids=list(range(8)))
    return np.concatenate([res.results[c]["ttt"] for c in range(4)], axis=0)



def build_attn(nheads=NAH):
    p = P("attn")
    nc = p.nc
    aq_d = p.dram("aqh", [DIM, CHUNK], BF16, kind="ExternalInput")
    ak_d = p.dram("akh", [DIM, 2 * CHUNK], BF16, kind="ExternalInput")
    av_d = p.dram("avh", [DIM, 2 * CHUNK], BF16, kind="ExternalInput")
    ttt_d = p.dram("tttc", [DIM, CHUNK], BF16, kind="ExternalInput")
    mup_d = p.dram("mup", [128, 128], F32, kind="ExternalInput")
    mlo_d = p.dram("mlo", [128, 128], F32, kind="ExternalInput")
    negc_d = p.dram("negc", [128, 1], F32, kind="ExternalInput")
    idt_d = p.dram("idt", [128, 128], BF16, kind="ExternalInput")
    mix_o = p.dram("mix", [DIM, CHUNK], BF16, kind="ExternalOutput")

    NQT = CHUNK // 128
    WINC = 12

    aqb = [p.sb(f"aqb{i}", [128, CHUNK], BF16) for i in range(2)]
    akb = [p.sb(f"akb{i}", [128, 2 * CHUNK], BF16) for i in range(2)]
    avb = [p.sb(f"avb{i}", [128, 2 * CHUNK], BF16) for i in range(2)]
    ttb = [p.sb(f"ttb{i}", [128, CHUNK], BF16) for i in range(2)]
    avTM = p.sb("avTM", [128, 16 * 128], BF16)
    probs_f = [p.sb(f"probs_f{i}", [128, 1536], F32) for i in range(2)]
    probsb = [p.sb(f"probsb{i}", [128, 1536], BF16) for i in range(2)]
    mcol2 = [p.sb(f"mcol2_{i}", [128, 8], F32) for i in range(2)]
    zcol2 = [p.sb(f"zcol2_{i}", [128, 8], F32) for i in range(2)]
    strip = p.sb("strip", [128, WINC * 512], BF16)
    mix = p.sb("mix_sb", [128, 16 * CHUNK], BF16)
    mup = p.sb("mup_sb", [128, 128], F32)
    mlo = p.sb("mlo_sb", [128, 128], F32)
    negc = p.sb("negc_sb", [128, 1], F32)
    idt = p.sb("idt_sb", [128, 128], BF16)
    mcol = p.sb("mcol", [128, 8], F32)
    zcol = p.sb("zcol", [128, 8], F32)

    PSQ = [p.ps(f"PSQ{i}", [128, 1536], F32) for i in range(2)]
    PV = p.ps("PVb", [128, 512], F32)
    PT = [p.ps("PTb0", [128, 1024], BF16)]

    s_in = [p.sem(f"s_ina{i}") for i in range(2)]
    s_w = p.sem("s_wa")
    s_pe = p.sem("s_pea")
    s_ev = p.sem("s_eva")
    s_ac = p.sem("s_aca")
    s_out = p.sem("s_outa")

    V, SC, T = nc.vector, nc.scalar, nc.tensor

    psq_free = [None, None]
    pv_free = None
    pt_bank_free = [None]
    pt_ctr = [0]

    def transpose_batch(pairs):
        tok = None
        i = 0
        while i < len(pairs):
            grp = pairs[i:i + 8]
            i += 8
            p.wait(nc.tensor, pt_bank_free[0])
            tt = None
            for r, (s_ap, d_ap) in enumerate(grp):
                tt = p.inc(T.transpose(PT[0][:, r * 128:(r + 1) * 128], s_ap,
                                       idt[:]), s_pe)
            p.wait(nc.vector, tt)
            tok = None
            for r, (s_ap, d_ap) in enumerate(grp):
                tok = p.inc(V.tensor_copy(d_ap, PT[0][:, r * 128:(r + 1) * 128]),
                            s_ev)
            pt_bank_free[0] = tok
        return tok

    wl = p.inc(nc.gpsimd.dma_start(out=mup[:], in_=mup_d), s_w, 16)
    wl = p.inc(nc.gpsimd.dma_start(out=mlo[:], in_=mlo_d), s_w, 16)
    wl = p.inc(nc.gpsimd.dma_start(out=negc[:], in_=negc_d), s_w, 16)
    wl = p.inc(nc.gpsimd.dma_start(out=idt[:], in_=idt_d), s_w, 16)

    head_done = [None, None]
    ld = [None, None]
    for a in range(nheads + 1):
        par = a % 2
        if a < nheads:
            p.wait(nc.gpsimd, head_done[par])
            t = None
            t = p.inc(nc.gpsimd.dma_start(
                out=aqb[par][:], in_=aq_d[a * 128:(a + 1) * 128, :]), s_in[par], 16)
            t = p.inc(nc.gpsimd.dma_start(
                out=akb[par][:], in_=ak_d[a * 128:(a + 1) * 128, :]), s_in[par], 16)
            t = p.inc(nc.gpsimd.dma_start(
                out=avb[par][:], in_=av_d[a * 128:(a + 1) * 128, :]), s_in[par], 16)
            t = p.inc(nc.gpsimd.dma_start(
                out=ttb[par][:], in_=ttt_d[a * 128:(a + 1) * 128, :]), s_in[par], 16)
            ld[par] = t
        if a == 0:
            continue
        h = a - 1
        bpar = 1 - par
        aq, ak, av, tb = aqb[bpar], akb[bpar], avb[bpar], ttb[bpar]
        p.wait(nc.vector, ld[bpar])
        aqs = p.fence(nc.vector, V.tensor_scalar(
            aq[:], aq[:], float(1.0 / np.sqrt(AHD)), None, ALU.mult), s_ev)
        p.wait(nc.tensor, ld[bpar])
        p.wait(nc.tensor, aqs)
        p.wait(nc.tensor, wl)
        avt = transpose_batch([(av[:, t_ * 128:(t_ + 1) * 128],
                                avTM[:, t_ * 128:(t_ + 1) * 128]) for t_ in range(16)])
        mix_ev = None
        for grp in range(2):
            p.wait(nc.vector, None)
            zs = p.fence(nc.vector, V.memset(strip[:], 0.0), s_ev)
            trs = None

            def phaseA(qt):
                qg = grp * 4 + qt
                par2 = qt % 2
                sbank = PSQ[par2]
                base = 128 * qg
                widths = (512, 512, 128)
                p.wait(nc.tensor, psq_free[par2])
                tS = None
                for i3 in range(3):
                    off = base + (512 * i3)
                    tS = p.inc(T.matmul(
                        sbank[:, i3 * 512:i3 * 512 + widths[i3]],
                        aq[:, qg * 128:(qg + 1) * 128],
                        ak[:, off:off + widths[i3]], start=True, stop=True), s_pe)
                p.wait(nc.vector, tS)
                mk = p.fence(nc.vector, V.tensor_add(sbank[:, 0:128], sbank[:, 0:128],
                                                     mup[:]), s_ev)
                mk = p.fence(nc.vector, V.tensor_add(sbank[:, 1024:1152],
                                                     sbank[:, 1024:1152], mlo[:]), s_ev)
                pref = 1024 - 128 * qg
                if pref > 0:
                    w0 = min(pref, 512)
                    mk = p.fence(nc.vector, V.tensor_scalar(
                        sbank[:, 0:w0], sbank[:, 0:w0], negc[:, 0:1], None,
                        ALU.add), s_ev)
                    if pref > 512:
                        mk = p.fence(nc.vector, V.tensor_scalar(
                            sbank[:, 512:512 + pref - 512],
                            sbank[:, 512:512 + pref - 512],
                            negc[:, 0:1], None, ALU.add), s_ev)
                mc = mcol2[par2]
                mk = p.fence(nc.vector, V.reduce_max(mc[:, 3:4], sbank[:, 0:1152],
                                                     axis=AX.X, negate=True), s_ev)
                return mk

            def phaseB(qt, mk):
                nonlocal trs
                par2 = qt % 2
                sbank = PSQ[par2]
                widths = (512, 512, 128)
                mc, zc = mcol2[par2], zcol2[par2]
                pf, pb2 = probs_f[par2], probsb[par2]
                p.wait(nc.scalar, mk)
                az = p.fence(nc.scalar, SC.activation(
                    pf[:, 0:1152], sbank[:, 0:1152], AF.Exp,
                    bias=mc[:, 3:4], accum_out=zc[:, 3:4]), s_ac)
                p.wait(nc.vector, az)
                zz = p.fence(nc.vector, V.reciprocal(zc[:, 4:5], zc[:, 3:4]), s_ev)
                p.wait(nc.scalar, zz)
                pb = p.inc(SC.activation(pb2[:, 0:1152], pf[:, 0:1152],
                                         AF.Copy, scale=zc[:, 4:5]), s_ac)
                psq_free[par2] = pb
                p.wait(nc.tensor, pb)
                p.wait(nc.tensor, zs)
                trs = transpose_batch(
                    [(pb2[:, j * 128:(j + 1) * 128],
                      strip[:, (qt + j) * 512 + qt * 128:(qt + j) * 512 + (qt + 1) * 128])
                     for j in range(9)])

            mks = {}
            mks[0] = phaseA(0)
            mks[1] = phaseA(1)
            phaseB(0, mks[0])
            mks[2] = phaseA(2)
            phaseB(1, mks[1])
            mks[3] = phaseA(3)
            phaseB(2, mks[2])
            phaseB(3, mks[3])

            p.wait(nc.tensor, trs)
            p.wait(nc.tensor, avt)
            p.wait(nc.tensor, pv_free)
            last = None
            for kc in range(WINC):
                ktile = grp * 4 + kc
                last = T.matmul(PV[:], avTM[:, ktile * 128:(ktile + 1) * 128],
                                strip[:, kc * 512:(kc + 1) * 512],
                                start=(kc == 0), stop=(kc == WINC - 1))
            tpv = p.inc(last, s_pe)
            p.wait(nc.vector, tpv)
            p.wait(nc.vector, ld[bpar])
            mix_ev = p.inc(V.tensor_add(
                mix[:, h * CHUNK + grp * 512:h * CHUNK + (grp + 1) * 512],
                tb[:, grp * 512:(grp + 1) * 512], PV[:]), s_ev)
            pv_free = mix_ev
        nc.sync.wait_ge(*mix_ev)
        od = p.inc(nc.sync.dma_start(
            out=mix_o[h * 128:(h + 1) * 128, :],
            in_=mix[:, h * CHUNK:(h + 1) * CHUNK]), s_out, 16)
        head_done[bpar] = [od, avt]

    nc.sync.wait_ge(s_out, p.val(s_out))
    p.close()
    return p.nc


def _attn_inputs(st1, st2, chunk):
    c0 = chunk * CHUNK
    ak = np.zeros((DIM, 2 * CHUNK), dtype=BF)
    av = np.zeros((DIM, 2 * CHUNK), dtype=BF)
    lo = max(0, c0 - CHUNK)
    n = c0 - lo
    if n > 0:
        ak[:, CHUNK - n:CHUNK] = st1["aqk"][DIM:, lo:c0]
        av[:, CHUNK - n:CHUNK] = st1["av"][:, lo:c0]
    ak[:, CHUNK:] = st1["aqk"][DIM:, c0:c0 + CHUNK]
    av[:, CHUNK:] = st1["av"][:, c0:c0 + CHUNK]
    qi = np.arange(128)[:, None]
    kj = np.arange(128)[None, :]
    mup = np.where(kj >= qi, 0.0, NEG_BIG).astype(np.float32)
    mlo = np.where(kj <= qi, 0.0, NEG_BIG).astype(np.float32)
    return {
        "aqh": np.ascontiguousarray(st1["aqk"][:DIM, c0:c0 + CHUNK]),
        "akh": ak, "avh": av,
        "tttc": np.ascontiguousarray(st2[:, c0:c0 + CHUNK]),
        "mup": mup, "mlo": mlo,
        "negc": np.full((128, 1), NEG_BIG if chunk == 0 else 0.0, np.float32),
        "idt": np.eye(128).astype(BF),
    }


def run_attn_stage(st1, st2):
    nc = build_attn()
    in_maps = [_attn_inputs(st1, st2, c) for c in range(8)]
    res = run_bass_kernel_spmd(nc, in_maps, core_ids=list(range(8)))
    return np.concatenate([res.results[c]["mix"] for c in range(8)], axis=1)



def build_tail(ttok=1024):
    p = P("tail")
    nc = p.nc
    mix_d = p.dram("mixc", [DIM, ttok], BF16, kind="ExternalInput")
    x0_d = p.dram("x0t", [DIM, ttok], F32, kind="ExternalInput")
    wo_d = p.dram("wo", [DIM, DIM], F32, kind="ExternalInput")
    w1_d = p.dram("w1t", [DIM, FFN_H], F32, kind="ExternalInput")
    w3_d = p.dram("w3t", [DIM, FFN_H], F32, kind="ExternalInput")
    w2_d = p.dram("w2t", [FFN_H, DIM], F32, kind="ExternalInput")
    hblk_d = p.dram("hblk", [128, 128], BF16, kind="ExternalInput")
    h128_d = p.dram("h128", [128, 128], BF16, kind="ExternalInput")
    out_d = p.dram("outt", [DIM, ttok], F32, kind="ExternalOutput")
    w2pp_d = p.dram("w2pp", [FFN_H, DIM], BF16, kind="Internal")
    y1_d = p.dram("y1d", [FFN_H, ttok], BF16, kind="Internal")

    NTB = ttok // 512
    NSUB = ttok // 512

    mix = p.sb("mixs", [128, 16 * ttok], BF16)
    ab = p.sb("abs_", [128, 16 * ttok], BF16)
    x1b = p.sb("x1b", [128, 16 * ttok], BF16)
    wbuf = p.sb("wbuf", [128, 32768], BF16)
    wsl = [p.sb(f"wsl{i}", [128, 2048], BF16) for i in range(2)]
    fb = p.sb("fb", [128, 16 * 512], BF16)
    outf = p.sb("outf", [128, 2 * 512], F32)
    hblk = p.sb("hblk_sb", [128, 128], BF16)
    h128 = p.sb("h128_sb", [128, 128], BF16)
    hsb = p.sb("hsb", [128, 512], BF16)
    sfb = p.sb("sfb", [128, 512], BF16)
    rows = p.sb("rows", [1, 3 * 1024], F32)
    bcA = p.sb("bcA", [128, 1024], BF16)
    bcB = p.sb("bcB", [128, 1024], BF16)
    onesb = p.sb("onesb", [128, 1], BF16)
    onesrow_b = p.sb("onesrow_b", [1, 128], BF16)
    cst = p.sb("cst", [128, 2], F32)
    y1tmp = p.sb("y1tmp", [128, 512], BF16)

    PS = [p.ps(f"PS{i}", [128, 512], F32) for i in range(8)]

    s_w = [p.sem(f"s_wt{i}") for i in range(2)]
    s_x = p.sem("s_xt")
    s_mixl = p.sem("s_mixl")
    s_wol = p.sem("s_wol")
    s_x0l = p.sem("s_x0l")
    s_prm = p.sem("s_prmt")
    s_pe = p.sem("s_pet")
    s_ev = p.sem("s_evt")
    s_ac = p.sem("s_act")
    s_out = [p.sem(f"s_outt{i}") for i in range(2)]
    s_y1 = p.sem("s_y1")
    s_pp = p.sem("s_pp")

    V, SC, T = nc.vector, nc.scalar, nc.tensor
    ps_free = [None] * 8

    nc.vector.memset(cst[:, 0:1], 1e-5)
    nc.vector.memset(onesb[:], 1.0)
    ini = p.inc(nc.vector.memset(onesrow_b[:], 1.0), s_ev)

    wl = p.inc(nc.gpsimd.dma_start(out=hblk[:], in_=hblk_d), s_prm, 16)
    wl = p.inc(nc.gpsimd.dma_start(out=h128[:], in_=h128_d), s_prm, 16)
    mld = None
    for dch in range(16):
        mld = p.inc(nc.gpsimd.dma_start(
            out=mix[:, dch * ttok:(dch + 1) * ttok],
            in_=mix_d[dch * 128:(dch + 1) * 128, :]), s_mixl, 16)

    w2r = w2_d.rearrange("(ih il) d -> il ih d", il=64)
    wld = [None, None]
    buf_cons = [None, None]
    pp_done = None
    pp_dmas = []
    for il in range(64):
        par = il % 2
        p.wait(nc.gpsimd, buf_cons[par])
        wld[par] = p.inc(nc.gpsimd.dma_start(out=wsl[par][:], in_=w2r[il]),
                         s_w[par], 16)
        p.wait(nc.tensor, wld[par])
        p.wait(nc.tensor, wl)
        for q in range(4):
            bank = q
            p.wait(nc.tensor, ps_free[bank])
            tq = p.inc(T.matmul(PS[bank][:], h128[:],
                                wsl[par][:, q * 512:(q + 1) * 512],
                                start=True, stop=True), s_pe)
            buf_cons[par] = tq
            p.wait(nc.vector, tq)
            p.wait(nc.vector, pp_done)
            ev = p.inc(V.tensor_copy(fb[:, q * 1024:q * 1024 + 512], PS[bank][:]),
                       s_ev)
            ps_free[bank] = ev
            nc.sync.wait_ge(*ev)
            pp_done = p.inc(nc.sync.dma_start(
                out=w2pp_d.rearrange("(jh il) d -> il jh d", il=64)[il][:, q * 512:(q + 1) * 512],
                in_=fb[:, q * 1024:q * 1024 + 512]), s_pp, 16)

    wol = None
    for ic in range(16):
        wol = p.inc(nc.gpsimd.dma_start(
            out=wbuf[:, ic * 2048:(ic + 1) * 2048],
            in_=wo_d[ic * 128:(ic + 1) * 128, :]), s_wol, 16)
    p.wait(nc.tensor, wol)
    p.wait(nc.tensor, mld)
    ab_ev = None
    for tb in range(NTB):
        for oc in range(16):
            bank = oc % 8
            p.wait(nc.tensor, ps_free[bank])
            last = None
            for ic in range(16):
                last = T.matmul(PS[bank][:],
                                wbuf[:, ic * 2048 + oc * 128:ic * 2048 + (oc + 1) * 128],
                                mix[:, ic * ttok + tb * 512:ic * ttok + (tb + 1) * 512],
                                start=(ic == 0), stop=(ic == 15))
            t_mm = p.inc(last, s_pe)
            p.wait(nc.vector, t_mm)
            ab_ev = p.inc(V.tensor_copy(
                ab[:, oc * ttok + tb * 512:oc * ttok + (tb + 1) * 512], PS[bank][:]),
                s_ev)
            ps_free[bank] = ab_ev

    def ln_stats(slab, sqslab, sq_rdy, row_off):
        sq = None
        p.wait(nc.vector, sq_rdy)
        for dch in range(16):
            sq = V.tensor_mul(sqslab[:, dch * ttok:(dch + 1) * ttok],
                              slab[:, dch * ttok:(dch + 1) * ttok],
                              slab[:, dch * ttok:(dch + 1) * ttok])
        sq = p.inc(sq, s_ev)
        p.wait(nc.tensor, sq)
        p.wait(nc.tensor, ini)
        toks = []
        for h in range(ttok // 512):
            bank = h % 4
            p.wait(nc.tensor, ps_free[bank])
            p.wait(nc.tensor, ps_free[4 + bank])
            last = None
            for dch in range(16):
                last = T.matmul(PS[bank][0:1, :], onesb[:, 0:1],
                                slab[:, dch * ttok + h * 512:dch * ttok + (h + 1) * 512],
                                start=(dch == 0), stop=(dch == 15))
            for dch in range(16):
                last = T.matmul(PS[4 + bank][0:1, :], onesb[:, 0:1],
                                sqslab[:, dch * ttok + h * 512:dch * ttok + (h + 1) * 512],
                                start=(dch == 0), stop=(dch == 15))
            tt = p.inc(last, s_pe)
            p.wait(nc.scalar, tt)
            a1 = p.fence(nc.scalar, SC.mul(rows[0:1, row_off + h * 512:row_off + (h + 1) * 512],
                                           PS[bank][0:1, :], -1.0 / DIM), s_ac)
            a1 = p.fence(nc.scalar, SC.mul(rows[0:1, row_off + ttok + h * 512:row_off + ttok + (h + 1) * 512],
                                           PS[4 + bank][0:1, :], 1.0 / DIM), s_ac)
            ps_free[bank] = a1
            ps_free[4 + bank] = a1
            toks.append(a1)
        nm = rows[0:1, row_off:row_off + ttok]
        msq = rows[0:1, row_off + ttok:row_off + 2 * ttok]
        p.wait(nc.vector, toks[-1])
        v1 = p.fence(nc.vector, V.tensor_mul(rows[0:1, row_off + 2 * ttok:row_off + 3 * ttok],
                                             nm, nm), s_ev)
        v1 = p.fence(nc.vector, V.tensor_sub(msq, msq,
                                             rows[0:1, row_off + 2 * ttok:row_off + 3 * ttok]),
                     s_ev)
        p.wait(nc.scalar, v1)
        a2 = p.fence(nc.scalar, SC.activation(msq, msq, AF.Ln, bias=cst[0:1, 0:1]), s_ac)
        a2 = p.fence(nc.scalar, SC.activation(msq, msq, AF.Exp, scale=-0.5), s_ac)
        return a2

    def ln_bcast(row_off, rdy):
        p.wait(nc.vector, rdy)
        p.wait(nc.scalar, rdy)
        c1 = p.fence(nc.vector, V.tensor_copy(fb[0:1, 0:ttok],
                                              rows[0:1, row_off:row_off + ttok]), s_ev)
        c2 = p.fence(nc.scalar, SC.copy(bcB[0:1, 0:ttok],
                                        rows[0:1, row_off + ttok:row_off + 2 * ttok]), s_ac)
        toks = []
        for h in range(ttok // 512):
            bank = h % 4
            p.wait(nc.tensor, [ps_free[bank], c1, c2])
            t1 = p.inc(T.matmul(PS[bank][:], onesrow_b[:],
                                fb[0:1, h * 512:(h + 1) * 512], start=True, stop=True),
                       s_pe)
            p.wait(nc.vector, t1)
            e1 = p.inc(V.tensor_copy(bcA[:, h * 512:(h + 1) * 512], PS[bank][:]), s_ev)
            ps_free[bank] = e1
            p.wait(nc.tensor, e1)
            t2 = p.inc(T.matmul(PS[bank][:], onesrow_b[:],
                                bcB[0:1, h * 512:(h + 1) * 512], start=True, stop=True),
                       s_pe)
            p.wait(nc.vector, t2)
            e2 = p.inc(V.tensor_copy(bcB[:, h * 512:(h + 1) * 512], PS[bank][:]), s_ev)
            ps_free[bank] = e2
            toks.append(e2)
        return toks[-1]

    st2tok = ln_stats(ab, mix, ab_ev, 0)
    bc2 = ln_bcast(0, st2tok)
    x0ld = None
    p.wait(nc.gpsimd, bc2)
    for dch in range(16):
        x0ld = p.inc(nc.gpsimd.dma_start(
            out=mix[:, dch * ttok:(dch + 1) * ttok],
            in_=x0_d[dch * 128:(dch + 1) * 128, :]), s_x0l, 16)
    x1tok = None
    p.wait(nc.vector, bc2)
    p.wait(nc.vector, x0ld)
    for dch in range(16):
        x1tok = p.fence(nc.vector, V.tensor_add(
            x1b[:, dch * ttok:(dch + 1) * ttok], ab[:, dch * ttok:(dch + 1) * ttok],
            bcA[:, 0:ttok]), s_ev)
        x1tok = p.fence(nc.vector, V.tensor_mul(
            x1b[:, dch * ttok:(dch + 1) * ttok], x1b[:, dch * ttok:(dch + 1) * ttok],
            bcB[:, 0:ttok]), s_ev)
        x1tok = p.fence(nc.vector, V.tensor_add(
            x1b[:, dch * ttok:(dch + 1) * ttok], x1b[:, dch * ttok:(dch + 1) * ttok],
            mix[:, dch * ttok:(dch + 1) * ttok]), s_ev)
    st3tok = ln_stats(x1b, ab, x1tok, 0)
    bc3 = ln_bcast(0, st3tok)
    xn_tok = None
    p.wait(nc.vector, bc3)
    for dch in range(16):
        xn_tok = p.fence(nc.vector, V.tensor_add(
            ab[:, dch * ttok:(dch + 1) * ttok], x1b[:, dch * ttok:(dch + 1) * ttok],
            bcA[:, 0:ttok]), s_ev)
        xn_tok = p.fence(nc.vector, V.tensor_mul(
            ab[:, dch * ttok:(dch + 1) * ttok], ab[:, dch * ttok:(dch + 1) * ttok],
            bcB[:, 0:ttok]), s_ev)
    x1n = ab

    p.wait(nc.tensor, xn_tok)
    wload = [None, None]
    y1_done = None
    y1tmp_free = None
    hsb_free = None
    sfb_free = None
    for hb in range(16):
        par = hb % 2
        p.wait(nc.gpsimd, wload[par])
        wt = None
        for dch in range(16):
            wt = p.inc(nc.gpsimd.dma_start(
                out=wbuf[:, par * 16384 + dch * 512:par * 16384 + (dch + 1) * 512],
                in_=w1_d[dch * 128:(dch + 1) * 128, hb * 512:(hb + 1) * 512]),
                s_w[par], 16)
            wt = p.inc(nc.gpsimd.dma_start(
                out=wbuf[:, par * 16384 + 8192 + dch * 512:par * 16384 + 8192 + (dch + 1) * 512],
                in_=w3_d[dch * 128:(dch + 1) * 128, hb * 512:(hb + 1) * 512]),
                s_w[par], 16)
        p.wait(nc.tensor, wt)
        lastcons = None

        def upA(hh, tb):
            b_up, b_li = (hh * 2 + tb) % 3, 3 + (hh * 2 + tb) % 3
            p.wait(nc.tensor, ps_free[b_up])
            p.wait(nc.tensor, ps_free[b_li])
            for dch in range(16):
                T.matmul(PS[b_up][:],
                         wbuf[:, par * 16384 + dch * 512 + hh * 128:par * 16384 + dch * 512 + (hh + 1) * 128],
                         x1n[:, dch * ttok + tb * 512:dch * ttok + (tb + 1) * 512],
                         start=(dch == 0), stop=(dch == 15))
            last = None
            for dch in range(16):
                last = T.matmul(PS[b_li][:],
                                wbuf[:, par * 16384 + 8192 + dch * 512 + hh * 128:par * 16384 + 8192 + dch * 512 + (hh + 1) * 128],
                                x1n[:, dch * ttok + tb * 512:dch * ttok + (tb + 1) * 512],
                                start=(dch == 0), stop=(dch == 15))
            return p.inc(last, s_pe)

        def upB(hh, tb, tmm):
            nonlocal lastcons, y1_done, y1tmp_free, hsb_free, sfb_free
            b_up, b_li = (hh * 2 + tb) % 3, 3 + (hh * 2 + tb) % 3
            lastcons = tmm
            p.wait(nc.scalar, tmm)
            p.wait(nc.scalar, sfb_free)
            sfe = p.inc(SC.activation(sfb[:], PS[b_up][:], AF.Silu), s_ac)
            p.wait(nc.vector, sfe)
            p.wait(nc.vector, hsb_free)
            hbe = p.fence(nc.vector, V.tensor_mul(hsb[:], sfb[:], PS[b_li][:]), s_ev)
            sfb_free = hbe
            ps_free[b_up] = [sfe, hbe]
            ps_free[b_li] = hbe
            p.wait(nc.tensor, hbe)
            bank1 = 6 + (hh + tb) % 2
            p.wait(nc.tensor, ps_free[bank1])
            ts1 = p.inc(T.matmul(PS[bank1][:], hblk[:], hsb[:],
                                 start=True, stop=True), s_pe)
            hsb_free = ts1
            p.wait(nc.vector, ts1)
            p.wait(nc.vector, y1tmp_free)
            y1e = p.inc(V.tensor_copy(y1tmp[:], PS[bank1][:]), s_ev)
            ps_free[bank1] = y1e
            nc.sync.wait_ge(*y1e)
            y1_done = p.inc(nc.sync.dma_start(
                out=y1_d[hb * 512 + hh * 128:hb * 512 + (hh + 1) * 128,
                         tb * 512:(tb + 1) * 512],
                in_=y1tmp[:]), s_y1, 16)
            y1tmp_free = y1_done

        units = [(hh, tb) for hh in range(4) for tb in range(NTB)]
        pend = None
        for i, (hh, tb) in enumerate(units):
            t = upA(hh, tb)
            if pend is not None:
                upB(*pend)
            pend = (hh, tb, t)
        upB(*pend)
        wload[par] = lastcons

    pp_par_cons = [None, None]
    prev_b_done = None
    out_free = [None, None]
    for tsub in range(NSUB):
        yld = None
        p.wait(nc.gpsimd, wload[0])
        p.wait(nc.gpsimd, wload[1])
        p.wait(nc.gpsimd, y1_done)
        p.wait(nc.gpsimd, pp_par_cons[0])
        p.wait(nc.gpsimd, pp_par_cons[1])
        for hc in range(64):
            yld = p.inc(nc.gpsimd.dma_start(
                out=wbuf[:, hc * 512:(hc + 1) * 512],
                in_=y1_d[hc * 128:(hc + 1) * 128, tsub * 512:(tsub + 1) * 512]),
                s_x, 16)
        p.wait(nc.tensor, yld)
        fev = None
        for opass in range(2):
            for b8 in range(8):
                p.wait(nc.tensor, ps_free[b8])
            last = None
            for hc in range(64):
                par = hc % 2
                p.wait(nc.gpsimd, pp_par_cons[par])
                if tsub == 0 and opass == 0:
                    p.wait(nc.gpsimd, pp_done)
                wld2 = p.inc(nc.gpsimd.dma_start(
                    out=wsl[par][:],
                    in_=w2pp_d[hc * 128:(hc + 1) * 128, :]), s_w[par], 16)
                p.wait(nc.tensor, wld2)
                for b8 in range(8):
                    oc = opass * 8 + b8
                    last = T.matmul(PS[b8][:],
                                    wsl[par][:, oc * 128:(oc + 1) * 128],
                                    wbuf[:, hc * 512:(hc + 1) * 512],
                                    start=(hc == 0), stop=(hc == 63))
                tfm = p.inc(last, s_pe)
                pp_par_cons[par] = tfm
            p.wait(nc.vector, tfm)
            for b8 in range(8):
                oc = opass * 8 + b8
                fev = p.inc(V.tensor_copy(fb[:, oc * 512:(oc + 1) * 512], PS[b8][:]),
                            s_ev)
                ps_free[b8] = fev
        sq = None
        p.wait(nc.vector, fev)
        for dch in range(16):
            sq = V.tensor_mul(mix[:, dch * 512:(dch + 1) * 512],
                              fb[:, dch * 512:(dch + 1) * 512],
                              fb[:, dch * 512:(dch + 1) * 512])
        sq = p.inc(sq, s_ev)
        p.wait(nc.tensor, sq)
        p.wait(nc.tensor, ps_free[0])
        p.wait(nc.tensor, ps_free[4])
        last = None
        for dch in range(16):
            T.matmul(PS[0][0:1, :], onesb[:, 0:1],
                     fb[:, dch * 512:(dch + 1) * 512], start=(dch == 0), stop=(dch == 15))
        for dch in range(16):
            last = T.matmul(PS[4][0:1, :], onesb[:, 0:1],
                            mix[:, dch * 512:(dch + 1) * 512],
                            start=(dch == 0), stop=(dch == 15))
        tst = p.inc(last, s_pe)
        p.wait(nc.scalar, tst)
        a1 = p.fence(nc.scalar, SC.mul(rows[0:1, 0:512], PS[0][0:1, :], -1.0 / DIM),
                     s_ac)
        a1 = p.fence(nc.scalar, SC.mul(rows[0:1, 512:1024], PS[4][0:1, :], 1.0 / DIM),
                     s_ac)
        ps_free[0] = a1
        ps_free[4] = a1
        p.wait(nc.vector, a1)
        v1 = p.fence(nc.vector, V.tensor_mul(rows[0:1, 1024:1536], rows[0:1, 0:512],
                                             rows[0:1, 0:512]), s_ev)
        v1 = p.fence(nc.vector, V.tensor_sub(rows[0:1, 512:1024], rows[0:1, 512:1024],
                                             rows[0:1, 1024:1536]), s_ev)
        p.wait(nc.scalar, v1)
        a2 = p.fence(nc.scalar, SC.activation(rows[0:1, 512:1024], rows[0:1, 512:1024],
                                              AF.Ln, bias=cst[0:1, 0:1]), s_ac)
        a2 = p.fence(nc.scalar, SC.activation(rows[0:1, 512:1024], rows[0:1, 512:1024],
                                              AF.Exp, scale=-0.5), s_ac)
        p.wait(nc.vector, a2)
        c1 = p.fence(nc.vector, V.tensor_copy(bcA[0:1, 0:512], rows[0:1, 0:512]), s_ev)
        c1 = p.fence(nc.vector, V.tensor_copy(bcB[0:1, 0:512], rows[0:1, 512:1024]),
                     s_ev)
        p.wait(nc.tensor, [ps_free[0], ps_free[1], c1])
        t1 = p.inc(T.matmul(PS[0][:], onesrow_b[:], bcA[0:1, 0:512],
                            start=True, stop=True), s_pe)
        t1 = p.inc(T.matmul(PS[1][:], onesrow_b[:], bcB[0:1, 0:512],
                            start=True, stop=True), s_pe)
        p.wait(nc.vector, t1)
        e1 = p.inc(V.tensor_copy(bcA[:, 0:512], PS[0][:]), s_ev)
        e1 = p.inc(V.tensor_copy(bcB[:, 0:512], PS[1][:]), s_ev)
        ps_free[0] = e1
        ps_free[1] = e1
        ot = None
        fin = None
        p.wait(nc.vector, e1)
        for dch in range(16):
            opar = dch % 2
            ot = p.fence(nc.vector, V.tensor_add(
                fb[:, dch * 512:(dch + 1) * 512], fb[:, dch * 512:(dch + 1) * 512],
                bcA[:, 0:512]), s_ev)
            ot = p.fence(nc.vector, V.tensor_mul(
                fb[:, dch * 512:(dch + 1) * 512], fb[:, dch * 512:(dch + 1) * 512],
                bcB[:, 0:512]), s_ev)
            p.wait(nc.vector, out_free[opar])
            ot = p.fence(nc.vector, V.tensor_add(
                outf[:, opar * 512:(opar + 1) * 512], fb[:, dch * 512:(dch + 1) * 512],
                x1b[:, dch * ttok + tsub * 512:dch * ttok + (tsub + 1) * 512]), s_ev)
            nc.sync.wait_ge(*ot)
            fin = p.inc(nc.sync.dma_start(
                out=out_d[dch * 128:(dch + 1) * 128, tsub * 512:(tsub + 1) * 512],
                in_=outf[:, opar * 512:(opar + 1) * 512]), s_out[opar], 16)
            out_free[opar] = fin
        prev_b_done = fin

    for s in s_out:
        nc.sync.wait_ge(s, p.val(s))
    p.close()
    return p.nc


def _hadamard(n):
    h = np.array([[1.0]])
    while h.shape[0] < n:
        h = np.block([[h, h], [h, -h]])
    return h


def _tail_inputs(st3mix, inp, XT, chunk, ttok=1024):
    c0 = chunk * ttok
    hb = np.zeros((128, 128), np.float64)
    h64 = _hadamard(64)
    hb[0:64, 0:64] = h64
    hb[64:128, 64:128] = h64
    return {
        "mixc": np.ascontiguousarray(st3mix[:, c0:c0 + ttok]),
        "x0t": np.ascontiguousarray(XT[:, c0:c0 + ttok]).astype(np.float32),
        "wo": np.ascontiguousarray(inp["L0_attn_Wo"]).astype(np.float32),
        "w1t": np.ascontiguousarray(inp["L0_ffn_W1"].T).astype(np.float32),
        "w3t": np.ascontiguousarray(inp["L0_ffn_W3"].T).astype(np.float32),
        "w2t": np.ascontiguousarray(inp["L0_ffn_W2"].T).astype(np.float32),
        "hblk": (hb / np.sqrt(2.0)).astype(BF),
        "h128": (_hadamard(128) / 64.0).astype(BF),
    }


def run_tail_stage(st3mix, inp, XT):
    nc = build_tail()
    in_maps = [_tail_inputs(st3mix, inp, XT, c) for c in range(8)]
    res = run_bass_kernel_spmd(nc, in_maps, core_ids=list(range(8)))
    return np.concatenate([res.results[c]["outt"] for c in range(8)], axis=1)
